# version 17
# speedup vs baseline: 1.2431x; 1.0016x over previous
import numpy as np
import concourse.bass as bass
import concourse.mybir as mybir
from concourse.bass_utils import run_bass_kernel_spmd

F32 = mybir.dt.float32
BF16 = mybir.dt.bfloat16
ALU = mybir.AluOpType
ACT = mybir.ActivationFunctionType
AX = mybir.AxisListType

NEG_BIG = -1.0e30


def real_cfg():
    return dict(
        NCORES=8, D=4096, DSSM=4096, S=128, NH=64, HD=64, K=4, T=1024, Q=128,
        NSTR=4, CDIM=1024, SINK_IT=50, TAU=0.1,
        EPS_LN=1e-5, EPS_A=1e-5, EPS_F=1.1920929e-07,
        MM="fp32",
    )


class Build:
    def __init__(self, cfg):
        self.cfg = cfg
        for k_, v_ in cfg.items():
            setattr(self, k_, v_)
        assert self.S == 128 and self.HD == 64 and self.K == 4 and self.Q == 128
        self.KC = self.D // 128
        self.XC = self.DSSM // 128
        self.NCH = self.T // self.Q
        self.TC = self.T // 128
        self.CONV = self.DSSM + 2 * self.S
        self.PROJ = self.DSSM + self.CONV + self.NH
        self.TW = self.T + 4
        self.NL = self.NSTR * self.NSTR + 2 * self.NSTR
        self.CC = (self.CDIM + 127) // 128
        self.HP = 128 // self.HD
        self.NC_B = 2 * self.XC
        self.NC_C = 2 * self.XC + 1
        self.NC_DT = 2 * self.XC + 2
        self.NCHK = 2 * self.XC + 3
        self.ttiles = []
        o = 0
        while o < self.T:
            n = min(512, self.T - o)
            self.ttiles.append((o, n))
            o += n
        self.NTT = len(self.ttiles)
        assert self.NTT <= 2
        A = {}
        A["p0_x"] = 0; A["p0_xa"] = 4096
        A["stage"] = 0
        A["xsil"] = 1056
        A["acc"] = 2080
        A["btok"] = 3104
        A["piece"] = 7424; A["xw"] = 7680; A["sstage"] = 7936
        A["stageB"] = 12288
        A["wlgall"] = 3264
        A["dt_ax"] = 4608; A["dt_mx"] = 5632
        A["wraw"] = 6656
        A["hloc"] = 8192
        A["z_st"] = 0; A["sz_st"] = 1088; A["lg_st"] = 2176; A["wlg"] = 3264
        A["logT"] = 8192; A["P"] = 8448; A["rs"] = 8640; A["hp"] = 8704; A["ho"] = 8768
        A["fx"] = 0; A["fxn"] = 4096
        A["xp"] = 8832; A["xpt"] = 9856; A["wbr"] = 10880; A["ybr"] = 11456
        A["hg"] = 0; A["sread"] = 4096; A["yst3"] = 8192
        A["gst"] = 12288; A["nlt"] = 12416; A["lrow"] = 12480
        A["est"] = 12736; A["mst"] = 12992; A["ytr"] = 13248
        A["ldiag"] = 12608
        A["y4"] = 0; A["sz4"] = 4096; A["yg4"] = 8192; A["sq4"] = 9216
        A["row4"] = 10304; A["rrep"] = 11392
        A["w5"] = 0; A["ffnx5"] = 2048; A["ost5"] = 3072
        A["LT"] = 13440; A["dtT"] = 13952; A["acumT"] = 14464
        A["wfacT"] = 14976; A["dARep"] = 15488
        A["statc"] = 16000; A["lendrep"] = 16128
        self.A = A

    def host_inputs(self, inputs):
        x = np.ascontiguousarray(np.asarray(inputs["L0_attn_hidden_states"], np.float32).reshape(-1, self.D))
        L = x.shape[0]
        assert L == self.NCORES * self.T, (L, self.NCORES, self.T)
        wcat = np.ascontiguousarray(
            np.concatenate([np.asarray(inputs["L0_ffn_W_res"], np.float32),
                            np.asarray(inputs["L0_ffn_W_pre"], np.float32),
                            np.asarray(inputs["L0_ffn_W_post"], np.float32)], 0).T)
        wcat = np.ascontiguousarray(
            wcat.reshape(self.KC, 128, self.NL).transpose(1, 0, 2).reshape(128, self.KC * self.NL))
        basec = np.ascontiguousarray(
            np.concatenate([np.asarray(inputs["L0_ffn_H_res_base"], np.float32).reshape(-1),
                            np.asarray(inputs["L0_ffn_H_pre_base"], np.float32),
                            np.asarray(inputs["L0_ffn_H_post_base"], np.float32)])[None, :])
        sidx = np.arange(128)[:, None]; tidx = np.arange(128)[None, :]
        KC = self.KC
        w1 = np.asarray(inputs["L0_attn_in_proj_w"], np.float32)
        w2 = np.asarray(inputs["L0_attn_out_proj_w"], np.float32)
        wbr0 = np.asarray(inputs["L0_ffn_W_branch"], np.float32)

        def blocktiles(groups, padw):
            out = np.zeros((len(groups) * KC * 128, padw), np.float32)
            for gi, g in enumerate(groups):
                oc = 0
                for ncid in g:
                    c0, w = self.nc_cols(ncid)
                    blk = w1[:, c0:c0 + w].reshape(KC, 128, w)
                    out.reshape(len(groups), KC, 128, padw)[gi, :, :, oc:oc + w] = blk
                    oc += w
            return np.ascontiguousarray(out)

        w1p1 = blocktiles(self.wslab_groups(1), 256)
        w1p2 = blocktiles(self.wslab_groups(2), 384)
        nwid = min(512, self.D)
        NB = (self.D + nwid - 1) // nwid
        w2b = np.ascontiguousarray(
            w2.reshape(self.XC, 128, NB, nwid).transpose(2, 0, 1, 3).reshape(NB * self.XC * 128, nwid))
        nby = (self.CDIM + 511) // 512
        wbrb = np.ascontiguousarray(
            wbr0.reshape(self.CC, 128, nby, 512).transpose(2, 0, 1, 3).reshape(nby * self.CC * 128, 512))
        common = dict(
            w1p1=w1p1, w1p2=w1p2, w2b=w2b, wbrb=wbrb,
            wcat=wcat, basec=basec,
            eyer=np.eye(self.NSTR, dtype=np.float32).reshape(1, -1),
            ident=np.eye(128, dtype=np.float32),
            allones=np.ones((128, 128), np.float32),
            trineg=np.where(sidx <= tidx, 0.0, NEG_BIG).astype(np.float32),
            tril1=(sidx <= tidx).astype(np.float32),
            convw=np.ascontiguousarray(np.asarray(inputs["L0_attn_conv_w"], np.float32)),
            alog=np.asarray(inputs["L0_attn_A_log"], np.float32).reshape(1, -1),
            dtb=np.asarray(inputs["L0_attn_dt_bias"], np.float32).reshape(-1, 1),
            dh=np.asarray(inputs["L0_attn_D"], np.float32).reshape(1, -1),
            nrmw=np.asarray(inputs["L0_attn_norm_w"], np.float32).reshape(-1),
            lnw=np.asarray(inputs["L0_attn_ln_w"], np.float32).reshape(-1),
            mup=np.asarray(inputs["L0_attn_mup_vector"], np.float32).reshape(-1),
            gate=np.asarray(inputs["L0_ffn_gate"], np.float32).reshape(1, 1),
        )
        maps = []
        for i in range(self.NCORES):
            lo = i * self.T
            halo = x[lo - 3:lo] if i > 0 else np.zeros((3, self.D), np.float32)
            rm = np.zeros((8, 8), np.float32)
            zm = np.zeros((8, 1), np.float32)
            for j in range(8):
                if j < i:
                    zm[j, 0] = 1.0
                    for k in range(j + 1, i):
                        rm[k, j] = 1.0
            m = dict(common)
            m.update(xsh=np.ascontiguousarray(np.concatenate([halo, x[lo:lo + self.T]], 0)),
                     rankm=rm, zmask=zm)
            maps.append(m)
        return maps

    def build(self, dbg=False):
        nc = bass.Bass("TRN2", target_bir_lowering=False, debug=False)
        self.nc = nc
        D, DSSM, NH, T, NL, NSTR, CDIM = self.D, self.DSSM, self.NH, self.T, self.NL, self.NSTR, self.CDIM
        din = lambda n, s: nc.dram_tensor(n, s, F32, kind="ExternalInput")
        self.i_x = din("xsh", [T + 3, D])
        self.NG1 = len(self.wslab_groups(1))
        self.NG2 = len(self.wslab_groups(2))
        self.NB5 = (D + min(512, D) - 1) // min(512, D)
        self.NBY = (CDIM + 511) // 512
        self.i_w1p1 = din("w1p1", [self.NG1 * self.KC * 128, 256])
        self.i_w1p2 = din("w1p2", [self.NG2 * self.KC * 128, 384])
        self.i_w2b = din("w2b", [self.NB5 * self.XC * 128, min(512, D)])
        self.i_wbrb = din("wbrb", [self.NBY * self.CC * 128, 512])
        self.i_wcat = din("wcat", [128, self.KC * NL])
        self.i_conv = din("convw", [self.CONV, self.K])
        self.i_alog = din("alog", [1, NH])
        self.i_dtb = din("dtb", [NH, 1])
        self.i_dh = din("dh", [1, NH])
        self.i_nrmw = din("nrmw", [DSSM])
        self.i_lnw = din("lnw", [D])
        self.i_mup = din("mup", [self.PROJ])
        self.i_gate = din("gate", [1, 1])
        self.i_basec = din("basec", [1, NL])
        self.i_eyer = din("eyer", [1, NSTR * NSTR])
        self.i_ident = din("ident", [128, 128])
        self.i_ones = din("allones", [128, 128])
        self.i_trineg = din("trineg", [128, 128])
        self.i_tril1 = din("tril1", [128, 128])
        self.i_rankm = din("rankm", [8, 8])
        self.i_zmask = din("zmask", [8, 1])
        self.o_out = nc.dram_tensor("out", [T, D], F32, kind="ExternalOutput")
        self.d_xstok = nc.dram_tensor("xstok_d", [T, DSSM], F32)
        self.d_s = nc.dram_tensor("s_d", [self.NCH * 128, DSSM], F32)
        self.d_sz = nc.dram_tensor("sz_d", [DSSM, T], F32)
        self.d_yfm = nc.dram_tensor("yfm_d", [DSSM, T], F32)
        self.d_ffnx = nc.dram_tensor("ffnx_d", [T, D], F32)
        self.d_ccin = nc.dram_tensor("ccin", [129, DSSM], F32)
        self.d_ccout = nc.dram_tensor("ccout", [self.NCORES * 129, DSSM], F32)
        self.dbg = {}
        if dbg:
            for nm, sh in [("dbg_xa", [128, self.KC * self.TW]), ("dbg_lt", [128, self.NCH * NH]),
                           ("dbg_wf", [128, self.NCH * NH]),
                           ("dbg_bfm", [128, T]), ("dbg_hloc", [128, DSSM]),
                           ("dbg_stB", [128, 1028]), ("dbg_accB", [128, T]),
                           ("dbg_logt", [128, self.TC * NL]), ("dbg_p", [128, self.TC * NSTR * NSTR]),
                           ("dbg_y", [128, DSSM]), ("dbg_yn", [128, self.XC * self.TW]),
                           ("dbg_hg", [128, DSSM])]:
                self.dbg[nm] = nc.dram_tensor(nm, sh, F32, kind="ExternalOutput")

        import contextlib
        st = contextlib.ExitStack()
        sb = lambda n, s, d=F32: st.enter_context(nc.sbuf_tensor(n, s, d))
        if self.MM == "fp32":
            self.xa = sb("xa", [128, self.KC * self.TW])
        else:
            raise NotImplementedError
        self.AR = sb("AR", [128, 16192])
        self.bfm = sb("bfm", [128, T])
        self.cfm = sb("cfm", [128, T])
        self.ident = sb("ident_sb", [128, 128])
        self.ones = sb("ones_sb", [128, 128])
        self.trineg = sb("trineg_sb", [128, 128])
        self.tril1 = sb("tril1_sb", [128, 128])
        ncc = self.KC + self.XC + self.NCHK + (self.CONV // 128) * self.K + 8
        self.ccols = sb("ccols", [128, ncc])
        self.o_lnw = 0
        self.o_nrmw = self.KC
        self.o_mup = self.KC + self.XC
        self.o_convc = self.KC + self.XC + self.NCHK
        self.smalls = sb("smalls", [128, 136])
        self.rank_sb = sb("rank_sb", [8, 8])
        self.zmask_sb = sb("zmask_sb", [8, 1])
        self.rows = sb("rows_sb", [1, 192])
        self.dtb_sb = sb("dtb_sb", [NH, 1])
        self.base_rep = sb("base_rep", [128, NL])
        self.hbl = sb("hbl", [128, NSTR * NSTR])
        self.gcol = sb("gcol", [128, 1])
        self.drep = sb("drep", [128, NH])
        self.negarep = sb("negarep", [128, NH])
        self.rlnw = sb("rlnw", [128, self.KC])
        self.sink_sb = sb("sink_sb", [128, 448])
        self.SK = {"logT": 0, "P": 192, "rs": 320, "hp": 352, "ho": 384}
        self.ps = [st.enter_context(nc.psum_tensor(f"ps{i}", [128, 512], F32)) for i in range(8)]
        sem = lambda n: st.enter_context(nc.semaphore(n))
        self.s_si = sem("si"); self.c_si = 0
        self.s_so = sem("so"); self.c_so = 0
        self.s_sd = sem("sd"); self.c_sd = 0
        self.s_sv = sem("sv"); self.c_sv = 0
        self.s_sa = sem("sa"); self.c_sa = 0
        self.s_sp = sem("sp"); self.c_sp = 0
        self.s_cc = sem("scc"); self.c_cc = 0
        self.s_w = [sem("sw0"), sem("sw1")]; self.c_w = [0, 0]
        blk = st.enter_context(nc.Block())

        @blk.gpsimd
        def _(g):
            self.emit_all()
        st.close()
        return nc

    def vi(self, instr):
        instr.then_inc(self.s_sv, 1); self.c_sv += 1; return self.c_sv

    def ai(self, instr):
        instr.then_inc(self.s_sa, 1); self.c_sa += 1; return self.c_sa

    def pi(self, instr):
        instr.then_inc(self.s_sp, 1); self.c_sp += 1; return self.c_sp

    def dsi(self, instr):
        instr.then_inc(self.s_si, 16); self.c_si += 16; return self.c_si

    def dso(self, instr):
        instr.then_inc(self.s_so, 16); self.c_so += 16; return self.c_so

    def dsd(self, instr):
        instr.then_inc(self.s_sd, 16); self.c_sd += 16; return self.c_sd

    def dwi(self, par, instr):
        instr.then_inc(self.s_w[par], 16); self.c_w[par] += 16; return self.c_w[par]

    def wwi(self, eng, par, n=None):
        n = self.c_w[par] if n is None else n
        if n > 0: eng.wait_ge(self.s_w[par], n)

    def wv(self, eng, n=None):
        n = self.c_sv if n is None else n
        if n > 0: eng.wait_ge(self.s_sv, n)

    def wa(self, eng, n=None):
        n = self.c_sa if n is None else n
        if n > 0: eng.wait_ge(self.s_sa, n)

    def wp(self, eng, n=None):
        n = self.c_sp if n is None else n
        if n > 0: eng.wait_ge(self.s_sp, n)

    def wsi(self, eng, n=None):
        n = self.c_si if n is None else n
        if n > 0: eng.wait_ge(self.s_si, n)

    def wso(self, eng, n=None):
        n = self.c_so if n is None else n
        if n > 0: eng.wait_ge(self.s_so, n)

    def wsd(self, eng, n=None):
        n = self.c_sd if n is None else n
        if n > 0: eng.wait_ge(self.s_sd, n)

    def s_(self, name, c0=0, cn=None, p0=0, pn=128):
        o = self.SK[name] + c0
        return self.sink_sb[p0:p0 + pn, o:o + cn]

    def a_(self, name, c0=0, cn=None, p0=0, pn=128):
        base = self.A[name]
        cn = 1 if cn is None else cn
        return self.AR[p0:p0 + pn, base + c0:base + c0 + cn]

    def bcastv(self, ap2d, ngrp, rep):
        return bass.AP(ap2d.tensor, ap2d.offset,
                       [list(ap2d.ap[0]), [ap2d.ap[1][0], ngrp], [0, rep]])

    def tw_off(self, k, col):
        return k * self.TW + col

    def emit_all(self):
        self.load_consts()
        self.phase0()
        self.phase1()
        self.collective()
        self.phase2()
        self.phase3()
        self.ffn_mix()
        self.phase4()
        self.phase5()
        nc = self.nc
        nc.sync.wait_ge(self.s_si, self.c_si)
        nc.scalar.wait_ge(self.s_sd, self.c_sd)
        nc.gpsimd.wait_ge(self.s_so, self.c_so)

    def load_consts(self):
        nc = self.nc
        v, s, y, t = nc.vector, nc.scalar, nc.sync, nc.tensor
        NH, KC, XC, NL, NSTR = self.NH, self.KC, self.XC, self.NL, self.NSTR
        self.dsi(y.dma_start(out=self.ident[:], in_=self.i_ident.ap()))
        self.dsi(y.dma_start(out=self.ones[:], in_=self.i_ones.ap()))
        self.dsi(y.dma_start(out=self.trineg[:], in_=self.i_trineg.ap()))
        self.dsi(y.dma_start(out=self.tril1[:], in_=self.i_tril1.ap()))
        self.dsi(y.dma_start(out=self.rank_sb[:], in_=self.i_rankm.ap()))
        self.dsi(y.dma_start(out=self.zmask_sb[:], in_=self.i_zmask.ap()))
        self.dsi(y.dma_start(out=self.dtb_sb[:], in_=self.i_dtb.ap()))
        self.o_rbase, self.o_reye, self.o_rgate, self.o_ralog, self.o_rdh = 0, NL, NL + 16, NL + 24, NL + 24 + NH
        self.dsi(y.dma_start(out=self.rows[0:1, 0:NL], in_=self.i_basec.ap()))
        self.dsi(y.dma_start(out=self.rows[0:1, self.o_reye:self.o_reye + NSTR * NSTR], in_=self.i_eyer.ap()))
        self.dsi(y.dma_start(out=self.rows[0:1, self.o_rgate:self.o_rgate + 1], in_=self.i_gate.ap()))
        self.dsi(y.dma_start(out=self.rows[0:1, self.o_ralog:self.o_ralog + NH], in_=self.i_alog.ap()))
        self.dsi(y.dma_start(out=self.rows[0:1, self.o_rdh:self.o_rdh + NH], in_=self.i_dh.ap()))
        self.dsi(y.dma_start(out=self.ccols[:, self.o_lnw:self.o_lnw + KC],
                             in_=self.i_lnw.ap().rearrange("(c p) -> p c", p=128), allow_slow_non_contiguous=True))
        self.dsi(y.dma_start(out=self.ccols[:, self.o_nrmw:self.o_nrmw + XC],
                             in_=self.i_nrmw.ap().rearrange("(c p) -> p c", p=128), allow_slow_non_contiguous=True))
        nfull = self.PROJ // 128
        self.dsi(y.dma_start(out=self.ccols[:, self.o_mup:self.o_mup + nfull],
                             in_=self.i_mup[0:nfull * 128].rearrange("(c p) -> p c", p=128), allow_slow_non_contiguous=True))
        if self.PROJ > nfull * 128:
            self.dsi(y.dma_start(out=self.ccols[0:NH, self.o_mup + nfull:self.o_mup + nfull + 1],
                                 in_=self.i_mup[nfull * 128:nfull * 128 + NH].rearrange("(p o) -> p o", o=1), allow_slow_non_contiguous=True))
        ncv = self.CONV // 128
        self.dsi(y.dma_start(out=self.ccols[:, self.o_convc:self.o_convc + ncv * self.K].rearrange(
                                 "p (c k) -> p c k", k=self.K),
                             in_=self.i_conv.ap().rearrange("(c p) k -> p c k", p=128),
                             allow_slow_non_contiguous=True))
        self.wsi(nc.vector); self.wsi(nc.scalar); self.wsi(nc.tensor)
        self.vi(v.reciprocal(self.rlnw[:, 0:KC], self.ccols[:, self.o_lnw:self.o_lnw + KC]))
        ps = self.ps[0]
        t.matmul(ps[:, 0:NL], self.ones[0:1, :], self.rows[0:1, 0:NL], start=True, stop=True)
        t.matmul(ps[:, 32:32 + 16], self.ones[0:1, :], self.rows[0:1, self.o_reye:self.o_reye + 16],
                 start=True, stop=True)
        t.matmul(ps[:, 64:65], self.ones[0:1, :], self.rows[0:1, self.o_rgate:self.o_rgate + 1],
                 start=True, stop=True)
        pc = self.pi(t.matmul(ps[:, 96:96 + NH], self.ones[0:1, :], self.rows[0:1, self.o_rdh:self.o_rdh + NH],
                              start=True, stop=True))
        self.wp(nc.vector, pc)
        self.vi(v.tensor_copy(self.base_rep[:], ps[:, 0:NL]))
        self.vi(v.tensor_copy(self.drep[:], ps[:, 96:96 + NH]))
        self.wp(nc.scalar, pc)
        ga = self.ai(s.activation(self.gcol[:], ps[:, 64:65], ACT.Sigmoid))
        self.wa(nc.vector, ga)
        gm = self.vi(v.tensor_scalar(self.smalls[:, 130:131], self.gcol[:], -1.0, 1.0, op0=ALU.mult, op1=ALU.add))
        self.vi(v.tensor_copy(self.hbl[:], ps[:, 32:48]))
        self.wv(nc.vector)
        self.vi(v.tensor_scalar(self.hbl[:], self.hbl[:], self.smalls[:, 130:131], None, op0=ALU.mult))
        ea = self.ai(s.activation(self.rows[0:1, self.o_ralog:self.o_ralog + NH],
                                  self.rows[0:1, self.o_ralog:self.o_ralog + NH], ACT.Exp))
        self.wa(nc.vector, ea)
        na = self.vi(v.tensor_scalar(self.rows[0:1, self.o_ralog:self.o_ralog + NH],
                                     self.rows[0:1, self.o_ralog:self.o_ralog + NH], -1.0, None, op0=ALU.mult))
        self.wv(nc.tensor, na)
        pc2 = self.pi(t.matmul(ps[:, 128:128 + NH], self.ones[0:1, :],
                               self.rows[0:1, self.o_ralog:self.o_ralog + NH], start=True, stop=True))
        self.wp(nc.vector, pc2)
        self.vi(v.tensor_copy(self.negarep[:], ps[:, 128:128 + NH]))

    def phase0(self):
        nc = self.nc
        v, s, y, t = nc.vector, nc.scalar, nc.sync, nc.tensor
        T, TC, KC, D = self.T, self.TC, self.KC, self.D
        EPSL, EPSA, EPSF = self.EPS_LN, self.EPS_A, self.EPS_F
        sm = self.smalls
        for ct in range(TC + 1):
            halo = (ct == TC)
            pn = 3 if halo else 128
            r0 = 0 if halo else 3 + ct * 128
            self.wv(nc.sync)
            self.wa(nc.sync)
            self.wp(nc.sync)
            ld = self.dsi(y.dma_start(out=self.a_("p0_x", 0, D, pn=pn), in_=self.i_x[r0:r0 + pn, :]))
            self.wsi(nc.vector, ld)
            self.wsi(nc.scalar, ld)
            xt = self.a_("p0_x", 0, D, pn=pn)
            c0 = ct * 4
            stc = lambda j: self.a_("statc", c0 + j, 1, pn=pn)
            self.vi(v.tensor_reduce(sm[0:pn, 0:1], xt, axis=AX.X, op=ALU.add))
            sq = self.a_("p0_xa", 0, D, pn=pn)
            sa1 = self.ai(s.activation(sq, xt, ACT.Square))
            self.wa(nc.vector, sa1)
            self.vi(v.tensor_reduce(sm[0:pn, 1:2], sq, axis=AX.X, op=ALU.add))
            self.wv(nc.vector)
            self.vi(v.tensor_scalar(stc(0), sm[0:pn, 0:1], 1.0 / D, None, op0=ALU.mult))
            self.vi(v.tensor_scalar(sm[0:pn, 1:2], sm[0:pn, 1:2], 1.0 / D, None, op0=ALU.mult))
            self.wv(nc.vector)
            self.vi(v.tensor_tensor(sm[0:pn, 2:3], stc(0), stc(0), op=ALU.mult))
            self.wv(nc.vector)
            self.vi(v.tensor_tensor(sm[0:pn, 3:4], sm[0:pn, 1:2], sm[0:pn, 2:3], op=ALU.subtract))
            self.wv(nc.vector)
            self.vi(v.tensor_scalar(sm[0:pn, 4:5], sm[0:pn, 3:4], EPSL, None, op0=ALU.add))
            self.wv(nc.scalar)
            sa2 = self.ai(s.activation(sm[0:pn, 5:6], sm[0:pn, 4:5], ACT.Sqrt))
            self.wa(nc.vector, sa2)
            self.vi(v.reciprocal(stc(1), sm[0:pn, 5:6]))
            self.wv(nc.vector)
            self.vi(v.tensor_tensor(sm[0:pn, 6:7], sm[0:pn, 3:4], stc(1), op=ALU.mult))
            self.wv(nc.vector)
            self.vi(v.tensor_tensor(sm[0:pn, 6:7], sm[0:pn, 6:7], stc(1), op=ALU.mult))
            self.wv(nc.vector)
            self.vi(v.tensor_scalar(sm[0:pn, 7:8], sm[0:pn, 6:7], EPSA, None, op0=ALU.add))
            self.vi(v.tensor_scalar(sm[0:pn, 8:9], sm[0:pn, 6:7], EPSF, None, op0=ALU.add))
            self.wv(nc.scalar)
            sa3 = self.ai(s.activation(sm[0:pn, 9:10], sm[0:pn, 7:8], ACT.Sqrt))
            self.wa(nc.vector, sa3)
            self.vi(v.reciprocal(sm[0:pn, 10:11], sm[0:pn, 9:10]))
            self.wv(nc.vector)
            self.vi(v.tensor_tensor(stc(2), stc(1), sm[0:pn, 10:11], op=ALU.mult))
            self.vi(v.reciprocal(sm[0:pn, 11:12], sm[0:pn, 8:9]))
            self.wv(nc.vector)
            self.vi(v.tensor_tensor(sm[0:pn, 12:13], sm[0:pn, 7:8], sm[0:pn, 11:12], op=ALU.mult))
            self.wv(nc.scalar)
            sa4 = self.ai(s.activation(stc(3), sm[0:pn, 12:13], ACT.Sqrt))
            self.wa(nc.vector, sa4)
            self.wv(nc.vector)
            self.vi(v.tensor_scalar(self.a_("p0_xa", 0, D, pn=pn), xt, stc(0), stc(2),
                                    op0=ALU.subtract, op1=ALU.mult))
            self.wv(nc.tensor)
            if not hasattr(self, "p0_ev"):
                self.p0_ev = [0, 0]
            for k in range(KC):
                par = k % 2
                pb = self.ps[6 + par]
                self.wv(nc.tensor, self.p0_ev[par])
                pc = self.pi(t.transpose(pb[0:128, 0:pn], self.a_("p0_xa", k * 128, 128, pn=pn),
                                         self.ident[0:pn, 0:pn]))
                self.wp(nc.vector, pc)
                dst = self.tw_off(k, 0 if halo else 3 + ct * 128)
                ev = self.vi(v.tensor_scalar(self.xa[:, dst:dst + pn], pb[0:128, 0:pn],
                                             self.ccols[:, self.o_lnw + k:self.o_lnw + k + 1], None,
                                             op0=ALU.mult))
                self.p0_ev[par] = ev

    def wslab_groups(self, phase):
        XC = self.XC
        if phase == 1:
            groups = [[self.NC_B, self.NC_C], [self.NC_DT, XC]]
            xs = list(range(XC + 1, 2 * XC))
            for i in range(0, len(xs), 2):
                groups.append(xs[i:i + 2])
        else:
            zs = list(range(XC))
            groups = []
            for i in range(0, len(zs), 3):
                groups.append(zs[i:i + 3])
        return groups

    def nc_cols(self, ncid):
        w = self.NH if ncid == self.NC_DT else 128
        return ncid * 128, w

    def emit_group(self, g, gsrc=None, with_logits=False, halo=True, bank3=True):
        nc = self.nc
        v, s, y, t = nc.vector, nc.scalar, nc.sync, nc.tensor
        KC = self.KC
        c0 = self.nc_cols(g[0])[0]
        wsum = sum(self.nc_cols(n)[1] for n in g)
        if not hasattr(self, "w_last_mm"):
            self.w_last_mm = [0, 0]
        last_mm = self.w_last_mm
        for k in range(KC):
            par = k % 2
            self.wp(nc.sync, last_mm[par])
            gten, gidx, padw = gsrc
            r0 = (gidx * KC + k) * 128
            self.dwi(par, y.dma_start(out=self.a_("wraw", par * 384, padw),
                                      in_=gten[r0:r0 + 128, 0:padw]))
            self.wwi(nc.tensor, par)
            if k == 0:
                self.wv(nc.tensor, self.bank_free)
                if with_logits:
                    self.wv(nc.tensor, self.wlg_ready)
            mm = None
            for i, ncid in enumerate(g):
                oc = sum(self.nc_cols(n)[1] for n in g[:i])
                w = self.nc_cols(ncid)[1]
                lhs = self.a_("wraw", par * 384 + oc, w)
                bb = 3 * i if bank3 else 2 * i
                for j, (to, tn) in enumerate(self.ttiles):
                    mm = t.matmul(self.ps[bb + j][0:w, 0:tn], lhs,
                                  self.xa[:, self.tw_off(k, 3 + to):self.tw_off(k, 3 + to) + tn],
                                  start=(k == 0), stop=(k == KC - 1))
                if halo and ncid != self.NC_DT:
                    mm = t.matmul(self.ps[3 * i + 2][0:w, 0:3], lhs,
                                  self.xa[:, self.tw_off(k, 0):self.tw_off(k, 0) + 3],
                                  start=(k == 0), stop=(k == KC - 1))
            if with_logits:
                lhs = self.a_("wlgall", k * self.NL, self.NL)
                for j, (to, tn) in enumerate(self.ttiles):
                    mm = t.matmul(self.ps[6 + j][0:self.NL, 0:tn], lhs,
                                  self.xa[:, self.tw_off(k, 3 + to):self.tw_off(k, 3 + to) + tn],
                                  start=(k == 0), stop=(k == KC - 1))
            last_mm[par] = self.pi(mm)
        return self.c_sp

    def phase1(self):
        nc = self.nc
        v, g_ = nc.vector, nc.gpsimd
        T = self.T
        self.wv(nc.sync); self.wa(nc.sync); self.wp(nc.sync)
        self.wv(nc.gpsimd); self.wa(nc.gpsimd); self.wp(nc.gpsimd)
        self.bank_free = self.c_sv
        z0 = self.vi(v.memset(self.a_("hloc", 0, self.DSSM), 0.0))
        self.wv(nc.vector, z0)
        for gi1, g in enumerate(self.wslab_groups(1)):
            done = self.emit_group(g, gsrc=(self.i_w1p1, gi1, 256), halo=True, bank3=True)
            self.wp(nc.vector, done)
            stgs = []
            for i, ncid in enumerate(g):
                stg = "stage" if i == 0 else "stageB"
                stgs.append(stg)
                w = self.nc_cols(ncid)[1]
                mupc = self.ccols[0:w, self.o_mup + ncid:self.o_mup + ncid + 1]
                if ncid != self.NC_DT:
                    self.vi(v.tensor_scalar(self.a_(stg, 0, 3, pn=w), self.ps[3 * i + 2][0:w, 0:3],
                                            mupc, None, op0=ALU.mult))
                for j, (to, tn) in enumerate(self.ttiles):
                    self.vi(v.tensor_scalar(self.a_(stg, 3 + to, tn, pn=w), self.ps[3 * i + j][0:w, 0:tn],
                                            mupc, None, op0=ALU.mult))
            self.bank_free = self.c_sv
            for i, ncid in enumerate(g):
                stg = stgs[i]
                self.wv(nc.vector)
                if ncid == self.NC_DT:
                    self.proc_dt(stg)
                elif ncid == self.NC_B:
                    if "dbg_stB" in self.dbg:
                        self.wv(nc.gpsimd)
                        self.dso(nc.gpsimd.dma_start(out=self.dbg["dbg_stB"].ap(), in_=self.a_(stg, 0, 1028)))
                    self.proc_conv(ncid, self.bfm[:], stg)
                    if "dbg_accB" in self.dbg:
                        self.wv(nc.gpsimd); self.wa(nc.gpsimd)
                        self.dso(nc.gpsimd.dma_start(out=self.dbg["dbg_accB"].ap(), in_=self.a_("acc", 0, self.T)))
                    self.trans_chunks(self.bfm[:], "btok")
                elif ncid == self.NC_C:
                    self.proc_conv(ncid, self.cfm[:], stg)
                else:
                    self.proc_xs(ncid - self.XC, stg)
        if "dbg_hloc" in self.dbg:
            self.wv(nc.gpsimd)
            self.dso(g_.dma_start(out=self.dbg["dbg_hloc"].ap(), in_=self.a_("hloc", 0, self.DSSM)))
        if "dbg_lt" in self.dbg:
            self.wv(nc.gpsimd)
            self.dso(g_.dma_start(out=self.dbg["dbg_lt"].ap(), in_=self.a_("LT", 0, self.NCH * self.NH)))
        if "dbg_wf" in self.dbg:
            self.wv(nc.gpsimd)
            self.dso(g_.dma_start(out=self.dbg["dbg_wf"].ap(), in_=self.a_("wfacT", 0, self.NCH * self.NH)))
        if "dbg_bfm" in self.dbg:
            self.wv(nc.gpsimd)
            self.dso(g_.dma_start(out=self.dbg["dbg_bfm"].ap(), in_=self.bfm[:]))
        if "dbg_xa" in self.dbg:
            self.wv(nc.gpsimd)
            self.dso(g_.dma_start(out=self.dbg["dbg_xa"].ap(), in_=self.xa[:]))

    def trans_chunks(self, src, dstname):
        nc = self.nc
        v, t = nc.vector, nc.tensor
        for cc in range(self.NCH):
            self.wv(nc.tensor)
            pc = self.pi(t.transpose(self.ps[7][0:128, 0:128], src[:, cc * 128:(cc + 1) * 128], self.ident[:]))
            self.wp(nc.vector, pc)
            self.vi(v.tensor_copy(self.a_(dstname, cc * 128, 128), self.ps[7][0:128, 0:128]))

    def proc_conv(self, ncid, dst, stg):
        nc = self.nc
        v, s = nc.vector, nc.scalar
        T = self.T
        cchunk = ncid - self.XC
        base = self.o_convc + cchunk * self.K
        acc = self.a_("acc", 0, T)
        self.vi(v.tensor_scalar(acc, self.a_(stg, 0, T), self.ccols[:, base:base + 1], None, op0=ALU.mult))
        for k in range(1, self.K):
            self.wv(nc.vector)
            self.vi(v.scalar_tensor_tensor(acc, self.a_(stg, k, T), self.ccols[:, base + k:base + k + 1],
                                           acc, op0=ALU.mult, op1=ALU.add))
        self.wv(nc.scalar)
        self.ai(s.activation(dst, acc, ACT.Silu))
        self.wa(nc.vector)
        self.wa(nc.tensor)

    def proc_dt(self, stg):
        nc = self.nc
        v, s, t = nc.vector, nc.scalar, nc.tensor
        NH, NCH, T = self.NH, self.NCH, self.T
        x = self.a_(stg, 3, T, pn=NH)
        self.vi(v.tensor_scalar(x, x, self.dtb_sb[0:NH, :], None, op0=ALU.add))
        ax = self.a_("dt_ax", 0, T, pn=NH)
        mx = self.a_("dt_mx", 0, T, pn=NH)
        self.wv(nc.scalar)
        a1 = self.ai(s.activation(ax, x, ACT.Abs))
        self.wa(nc.scalar, a1)
        a2 = self.ai(s.activation(ax, ax, ACT.Exp, scale=-1.0))
        self.wa(nc.scalar, a2)
        a3 = self.ai(s.activation(ax, ax, ACT.Ln, bias=1.0))
        self.wv(nc.vector)
        self.vi(v.tensor_scalar(mx, x, 0.0, None, op0=ALU.max))
        self.wa(nc.vector, a3)
        self.wv(nc.vector)
        spv = self.vi(v.tensor_tensor(x, ax, mx, op=ALU.add))
        self.wv(nc.tensor, spv)
        for cc in range(NCH):
            pc = self.pi(t.transpose(self.ps[7][0:128, 0:NH], x[0:NH, cc * 128:(cc + 1) * 128],
                                     self.ident[0:NH, 0:NH]))
            self.wp(nc.vector, pc)
            ev = self.vi(v.tensor_copy(self.a_("dtT", cc * NH, NH), self.ps[7][0:128, 0:NH]))
            self.wv(nc.vector)
            av = self.vi(v.tensor_tensor(self.a_("acumT", cc * NH, NH), self.a_("dtT", cc * NH, NH),
                                         self.negarep[:, 0:NH], op=ALU.mult))
            self.wv(nc.tensor, av)
            pc = self.pi(t.matmul(self.ps[7][0:128, 128:128 + NH], self.tril1[:],
                                  self.a_("acumT", cc * NH, NH), start=True, stop=True))
            self.wp(nc.vector, pc)
            lt = self.vi(v.tensor_copy(self.a_("LT", cc * NH, NH), self.ps[7][0:128, 128:128 + NH]))
            self.wv(nc.tensor, lt)
            pc = self.pi(t.transpose(self.ps[7][0:NH, 256:384], self.a_("LT", cc * NH, NH), self.ident[:]))
            self.wp(nc.vector, pc)
            lc = self.vi(v.tensor_copy(self.smalls[0:NH, 16:17], self.ps[7][0:NH, 256 + 127:256 + 128]))
            self.wv(nc.vector)
            if cc == 0:
                self.vi(v.tensor_copy(self.smalls[0:NH, 17:18], self.smalls[0:NH, 16:17]))
            else:
                self.vi(v.tensor_tensor(self.smalls[0:NH, 17:18], self.smalls[0:NH, 17:18],
                                        self.smalls[0:NH, 16:17], op=ALU.add))
            self.wa(nc.scalar, 0); self.wv(nc.scalar)
            ec = self.ai(s.activation(self.smalls[0:NH, 18:19], self.smalls[0:NH, 16:17], ACT.Exp))
            self.wa(nc.vector, ec)
            d1 = self.vi(v.tensor_scalar(self.smalls[0:NH, 64:64 + NH], self.ident[0:NH, 0:NH],
                                         self.smalls[0:NH, 18:19], None, op0=ALU.mult))
            self.wv(nc.tensor, d1)
            pc = self.pi(t.matmul(self.ps[7][0:128, 384:384 + NH], self.ones[0:NH, :],
                                  self.smalls[0:NH, 64:64 + NH], start=True, stop=True))
            self.wp(nc.vector, pc)
            self.vi(v.tensor_copy(self.a_("dARep", cc * NH, NH), self.ps[7][0:128, 384:384 + NH]))
            self.wv(nc.vector)
            d2 = self.vi(v.tensor_scalar(self.smalls[0:NH, 64:64 + NH], self.ident[0:NH, 0:NH],
                                         self.smalls[0:NH, 16:17], None, op0=ALU.mult))
            self.wv(nc.tensor, d2)
            pc = self.pi(t.matmul(self.ps[7][0:128, 384:384 + NH], self.ones[0:NH, :],
                                  self.smalls[0:NH, 64:64 + NH], start=True, stop=True))
            self.wp(nc.vector, pc)
            lr = self.vi(v.tensor_copy(self.a_("lendrep", 0, NH), self.ps[7][0:128, 384:384 + NH]))
            self.wv(nc.vector, lr)
            sv = self.vi(v.tensor_tensor(self.a_("wfacT", cc * NH, NH), self.a_("lendrep", 0, NH),
                                         self.a_("LT", cc * NH, NH), op=ALU.subtract))
            self.wv(nc.scalar, sv)
            ae = self.ai(s.activation(self.a_("wfacT", cc * NH, NH), self.a_("wfacT", cc * NH, NH), ACT.Exp))
            self.wa(nc.vector, ae)
            self.vi(v.tensor_tensor(self.a_("wfacT", cc * NH, NH), self.a_("wfacT", cc * NH, NH),
                                    self.a_("dtT", cc * NH, NH), op=ALU.mult))
            self.wv(nc.tensor)
        self.wv(nc.scalar)
        af = self.ai(s.activation(self.a_("acumT", 0, NCH * NH), self.a_("LT", 0, NCH * NH), ACT.Exp))
        self.wa(nc.vector, af)
        self.wa(nc.tensor, af)

    def proc_xs(self, kx, stg):
        nc = self.nc
        v, s, y, g_, t = nc.vector, nc.scalar, nc.sync, nc.gpsimd, nc.tensor
        NH, NCH, T, HP, HD = self.NH, self.NCH, self.T, self.HP, self.HD
        xsil = self.a_("xsil", 0, T)
        base = self.o_convc + kx * self.K
        acc = self.a_("acc", 0, T)
        self.vi(v.tensor_scalar(acc, self.a_(stg, 0, T), self.ccols[:, base:base + 1], None, op0=ALU.mult))
        for k in range(1, self.K):
            self.wv(nc.vector)
            self.vi(v.scalar_tensor_tensor(acc, self.a_(stg, k, T), self.ccols[:, base + k:base + k + 1],
                                           acc, op0=ALU.mult, op1=ALU.add))
        self.wv(nc.scalar)
        sil = self.ai(s.activation(xsil, acc, ACT.Silu))
        self.wa(nc.tensor, sil)
        self.wa(nc.vector, sil)
        if not hasattr(self, "xs_pv"):
            self.xs_pv = [0, 0]; self.xs_sv = [0, 0]; self.xs_dx = [0, 0]; self.xs_ds = [0, 0]
            self.xs_mm = [0, 0]
        for cc in range(NCH):
            par = cc % 2
            pb = self.ps[6 + par]
            piece = self.a_("piece", par * 128, 128)
            xw = self.a_("xw", par * 128, 128)
            sst = self.a_("sstage", par * 128, 128)
            self.wv(nc.tensor, self.xs_pv[par])
            pc = self.pi(t.transpose(pb[0:128, 0:128], xsil[:, cc * 128:(cc + 1) * 128], self.ident[:]))
            self.wp(nc.vector, pc)
            self.wso(nc.vector, self.xs_dx[par])
            pv = self.vi(v.tensor_copy(piece, pb[0:128, 0:128]))
            self.wv(nc.gpsimd, pv)
            self.xs_dx[par] = self.dso(g_.dma_start(
                out=self.d_xstok[cc * 128:(cc + 1) * 128, kx * 128:(kx + 1) * 128], in_=piece))
            wf = self.a_("wfacT", cc * NH + kx * HP, HP)
            wfv = self.bcastv(wf, HP, HD)
            self.wp(nc.vector, self.xs_mm[par])
            xv = self.vi(v.tensor_tensor(xw.rearrange("p (h d) -> p h d", h=HP),
                                         piece.rearrange("p (h d) -> p h d", h=HP), wfv, op=ALU.mult))
            self.xs_pv[par] = xv
            self.wv(nc.tensor, self.xs_sv[par])
            self.wv(nc.tensor, xv)
            pc2 = self.pi(t.matmul(pb[0:128, 128:256], self.a_("btok", cc * 128, 128), xw,
                                   start=True, stop=True))
            self.xs_mm[par] = pc2
            self.wp(nc.vector, pc2)
            for h in range(HP):
                col = kx * 128 + h * HD
                self.vi(v.scalar_tensor_tensor(self.a_("hloc", col, HD), self.a_("hloc", col, HD),
                                               self.a_("dARep", cc * NH + kx * HP + h, 1),
                                               pb[0:128, 128 + h * HD:128 + (h + 1) * HD],
                                               op0=ALU.mult, op1=ALU.add))
            self.wso(nc.vector, self.xs_ds[par])
            sv = self.vi(v.tensor_copy(sst, pb[0:128, 128:256]))
            self.xs_sv[par] = sv
            self.wv(nc.gpsimd, sv)
            self.xs_ds[par] = self.dso(g_.dma_start(
                out=self.d_s[cc * 128:(cc + 1) * 128, kx * 128:(kx + 1) * 128], in_=sst))

    def collective(self):
        nc = self.nc
        g_ = nc.gpsimd
        v = nc.vector
        self.wv(nc.vector)
        self.wso(nc.vector)
        zr = self.vi(v.memset(self.AR[0:1, 4096:4096 + self.DSSM], 0.0))
        self.wv(nc.gpsimd)
        self.dso(g_.dma_start(out=self.d_ccin[0:128, :], in_=self.a_("hloc", 0, self.DSSM)))
        self.dso(g_.dma_start(out=self.d_ccin[128:129, :], in_=self.AR[0:1, 4096:4096 + self.DSSM]))
        self.wso(nc.gpsimd)
        self.dso(g_.dma_start(out=self.d_ccin[128:129, 0:self.NH], in_=self.smalls[0:self.NH, 17:18]))
        self.wso(nc.gpsimd)
        g_.collective_compute(
            "AllGather", ALU.bypass,
            replica_groups=[list(range(self.NCORES))],
            ins=[self.d_ccin.ap()],
            outs=[self.d_ccout.ap()],
        ).then_inc(self.s_cc, 1)
        self.c_cc += 1

    def phase2(self):
        nc = self.nc
        v, s, y, g_, t = nc.vector, nc.scalar, nc.sync, nc.gpsimd, nc.tensor
        T, TC, NL = self.T, self.TC, self.NL
        self.wv(nc.sync); self.wa(nc.sync); self.wp(nc.sync)
        ld = self.dsi(y.dma_start(out=self.a_("wlgall", 0, self.KC * NL), in_=self.i_wcat.ap()))
        self.wsi(nc.vector, ld)
        for k in range(self.KC):
            self.vi(v.tensor_scalar(self.a_("wlgall", k * NL, NL), self.a_("wlgall", k * NL, NL),
                                    self.rlnw[:, k:k + 1], None, op0=ALU.mult))
        self.wlg_ready = self.c_sv
        groups = self.wslab_groups(2)
        for gi, g in enumerate(groups):
            done = self.emit_group(g, gsrc=(self.i_w1p2, gi, 384), with_logits=(gi == 0),
                                   halo=False, bank3=False)
            for i, ncid in enumerate(g):
                self.wp(nc.vector, done)
                self.wv(nc.vector)
                self.wa(nc.vector)
                mupc = self.ccols[:, self.o_mup + ncid:self.o_mup + ncid + 1]
                for j, (to, tn) in enumerate(self.ttiles):
                    self.vi(v.tensor_scalar(self.a_("z_st", to, tn), self.ps[2 * i + j][0:128, 0:tn],
                                            mupc, None, op0=ALU.mult))
                self.wv(nc.scalar)
                sz = self.ai(s.activation(self.a_("sz_st", 0, T), self.a_("z_st", 0, T), ACT.Silu))
                self.wa(nc.gpsimd, sz)
                self.dso(g_.dma_start(out=self.d_sz[ncid * 128:(ncid + 1) * 128, :], in_=self.a_("sz_st", 0, T)))
                self.wso(nc.scalar)
            if gi == 0:
                self.wp(nc.vector, done)
                for j, (to, tn) in enumerate(self.ttiles):
                    self.vi(v.tensor_copy(self.a_("lg_st", to, tn, pn=NL), self.ps[6 + j][0:NL, 0:tn]))
                self.logits_done = self.c_sv
            self.bank_free = self.c_sv
        self.wso(nc.vector)
        for ct in range(TC):
            self.wv(nc.tensor)
            pc = self.pi(t.transpose(self.ps[0][0:128, 0:NL], self.a_("lg_st", ct * 128, 128, pn=NL),
                                     self.ident[0:NL, 0:NL]))
            self.wp(nc.vector, pc)
            sf = self.a_("statc", ct * 4 + 3, 1)
            sv = self.vi(v.scalar_tensor_tensor(self.s_("logT", ct * NL, NL), self.ps[0][0:128, 0:NL], sf,
                                                self.base_rep[:], op0=ALU.mult, op1=ALU.add))
            self.wv(nc.tensor, sv)
        if "dbg_logt" in self.dbg:
            self.wv(nc.gpsimd)
            self.dso(g_.dma_start(out=self.dbg["dbg_logt"].ap(), in_=self.s_("logT", 0, TC * NL)))
        self.sink_gen = self.sinkhorn_steps()

    def sinkhorn_steps(self):
        nc = self.nc
        v, s = nc.vector, nc.scalar
        TC, NL, NSTR = self.TC, self.NL, self.NSTR
        NN = NSTR * NSTR
        P = lambda: self.s_("P", 0, TC * NN)
        pv = lambda: P().rearrange("p (c i j) -> p c i j", i=NSTR, j=NSTR)
        pvt = lambda: P().rearrange("p (c i j) -> p c j i", i=NSTR, j=NSTR)
        rs = lambda: self.s_("rs", 0, TC * NSTR)
        rsb = lambda: rs().rearrange("p (c n) -> p c n", n=NSTR)
        _r = rsb()
        rsv = bass.AP(_r.tensor, _r.offset, [list(_r.ap[0]), list(_r.ap[1]), list(_r.ap[2]), [0, NSTR]])
        csv = bass.AP(_r.tensor, _r.offset, [list(_r.ap[0]), list(_r.ap[1]), [0, NSTR], list(_r.ap[2])])
        tau_i = 1.0 / self.TAU
        for ct in range(TC):
            self.wv(nc.vector)
            self.vi(v.tensor_scalar(self.s_("P", ct * NN, NN), self.s_("logT", ct * NL, NN), tau_i, None,
                                    op0=ALU.mult))
        self.wv(nc.vector)
        self.vi(v.tensor_reduce(rsb(), pv(), axis=AX.X, op=ALU.max))
        self.wv(nc.vector)
        sb1 = self.vi(v.tensor_tensor(pv(), pv(), rsv, op=ALU.subtract))
        self.wv(nc.scalar, sb1)
        ex = self.ai(s.activation(P(), P(), ACT.Exp))
        self.wa(nc.vector, ex)
        yield
        for it in range(self.SINK_IT):
            self.wv(nc.vector)
            self.vi(v.tensor_reduce(rsb(), pv(), axis=AX.X, op=ALU.add))
            self.wv(nc.vector)
            self.vi(v.reciprocal(rs(), rs()))
            self.wv(nc.vector)
            self.vi(v.tensor_tensor(pv(), pv(), rsv, op=ALU.mult))
            self.wv(nc.vector)
            self.vi(v.tensor_reduce(rsb(), pvt(), axis=AX.X, op=ALU.add))
            self.wv(nc.vector)
            self.vi(v.reciprocal(rs(), rs()))
            self.wv(nc.vector)
            self.vi(v.tensor_tensor(pv(), pv(), csv, op=ALU.mult))
            if it % 8 == 7:
                yield
        for ct in range(TC):
            self.wv(nc.vector)
            self.vi(v.scalar_tensor_tensor(self.s_("P", ct * NN, NN), self.s_("P", ct * NN, NN), self.gcol[:],
                                           self.hbl[:], op0=ALU.mult, op1=ALU.add))
        if "dbg_p" in self.dbg:
            self.wv(nc.gpsimd)
            self.dso(self.nc.gpsimd.dma_start(out=self.dbg["dbg_p"].ap(), in_=self.s_("P", 0, TC * NN)))
        r1 = lambda: self.s_("rs", 0, TC)
        r1b = lambda: r1().rearrange("p (c o) -> p c o", o=1)
        _q = r1b()
        r1v = bass.AP(_q.tensor, _q.offset, [list(_q.ap[0]), list(_q.ap[1]), [0, NSTR]])
        yield
        for nm, off in (("hp", NN), ("ho", NN + NSTR)):
            for ct in range(TC):
                self.wv(nc.vector)
                self.vi(v.tensor_copy(self.s_(nm, ct * NSTR, NSTR), self.s_("logT", ct * NL + off, NSTR)))
            dv = self.s_(nm, 0, TC * NSTR).rearrange("p (c n) -> p c n", n=NSTR)
            self.wv(nc.vector)
            self.vi(v.tensor_reduce(r1b(), dv, axis=AX.X, op=ALU.max))
            self.wv(nc.vector)
            self.vi(v.tensor_tensor(dv, dv, r1v, op=ALU.subtract))
            self.wv(nc.scalar)
            e2 = self.ai(s.activation(self.s_(nm, 0, TC * NSTR), self.s_(nm, 0, TC * NSTR), ACT.Exp))
            self.wa(nc.vector, e2)
            self.vi(v.tensor_reduce(r1b(), dv, axis=AX.X, op=ALU.add))
            self.wv(nc.vector)
            self.vi(v.reciprocal(r1(), r1()))
            self.wv(nc.vector)
            self.vi(v.tensor_tensor(dv, dv, r1v, op=ALU.mult))

    def ffn_mix(self):
        nc = self.nc
        v, s, y, g_, t = nc.vector, nc.scalar, nc.sync, nc.gpsimd, nc.tensor
        T, TC, D, NSTR, CDIM, CC = self.T, self.TC, self.D, self.NSTR, self.CDIM, self.CC
        NN = NSTR * NSTR
        for ct in range(TC):
            self.wv(nc.sync); self.wp(nc.sync)
            ld = self.dsi(y.dma_start(out=self.a_("fx", 0, D), in_=self.i_x[3 + ct * 128:3 + (ct + 1) * 128, :]))
            self.wsi(nc.vector, ld)
            xt = self.a_("fx", 0, D)
            xn = self.a_("fxn", 0, D)
            m = self.a_("statc", ct * 4 + 0, 1)
            r = self.a_("statc", ct * 4 + 1, 1)
            xv = self.vi(v.tensor_scalar(xn, xt, m, r, op0=ALU.subtract, op1=ALU.mult))
            xp = self.a_("xp", 0, CDIM)
            self.wv(nc.vector, xv)
            self.vi(v.tensor_scalar(xp, xn[:, 0:CDIM], self.s_("hp", ct * NSTR, 1), None, op0=ALU.mult))
            for n in range(1, NSTR):
                self.wv(nc.vector)
                self.vi(v.scalar_tensor_tensor(xp, xn[:, n * CDIM:(n + 1) * CDIM],
                                               self.s_("hp", ct * NSTR + n, 1), xp, op0=ALU.mult, op1=ALU.add))
            self.wv(nc.tensor)
            for cc_ in range(CC):
                w = min(128, CDIM - cc_ * 128)
                pc = self.pi(t.transpose(self.ps[1][0:w, 0:128], xp[:, cc_ * 128:cc_ * 128 + w], self.ident[:]))
                self.wp(nc.vector, pc)
                ev = self.vi(v.tensor_copy(self.a_("xpt", cc_ * 128, 128, pn=w), self.ps[1][0:w, 0:128]))
                self.wv(nc.tensor, ev)
            nby = (CDIM + 511) // 512
            for nb in range(nby):
                nn = min(512, CDIM - nb * 512)
                mm = None
                for cc_ in range(CC):
                    w = min(128, CDIM - cc_ * 128)
                    self.wv(nc.sync); self.wp(nc.sync)
                    wl = self.dsi(y.dma_start(out=self.a_("wbr", 0, nn, pn=w),
                                              in_=self.i_wbrb[(nb * CC + cc_) * 128:(nb * CC + cc_) * 128 + w,
                                                              0:nn]))
                    self.wsi(nc.tensor, wl)
                    mm = self.pi(t.matmul(self.ps[2][0:128, 0:nn], self.a_("xpt", cc_ * 128, 128, pn=w),
                                  self.a_("wbr", 0, nn, pn=w), start=(cc_ == 0), stop=(cc_ == CC - 1)))
                pc = mm
                self.wp(nc.vector, pc)
                self.vi(v.tensor_copy(self.a_("ybr", nb * 512, nn), self.ps[2][0:128, 0:nn]))
            ybr = self.a_("ybr", 0, CDIM)
            for i in range(NSTR):
                acc = self.a_("xp", 0, CDIM)
                self.wv(nc.vector)
                self.vi(v.scalar_tensor_tensor(acc, xn[:, 0:CDIM], self.s_("P", ct * NN + i * NSTR, 1),
                                               xt[:, i * CDIM:(i + 1) * CDIM], op0=ALU.mult, op1=ALU.add))
                for j in range(1, NSTR):
                    self.wv(nc.vector)
                    self.vi(v.scalar_tensor_tensor(acc, xn[:, j * CDIM:(j + 1) * CDIM],
                                                   self.s_("P", ct * NN + i * NSTR + j, 1), acc,
                                                   op0=ALU.mult, op1=ALU.add))
                self.wv(nc.vector)
                fv = self.vi(v.scalar_tensor_tensor(acc, ybr, self.s_("ho", ct * NSTR + i, 1), acc,
                                                    op0=ALU.mult, op1=ALU.add))
                self.wv(nc.gpsimd, fv)
                self.dso(g_.dma_start(out=self.d_ffnx[ct * 128:(ct + 1) * 128, i * CDIM:(i + 1) * CDIM], in_=acc))
                self.wso(nc.vector)

    def phase3(self):
        nc = self.nc
        v, s, y, g_, t = nc.vector, nc.scalar, nc.sync, nc.gpsimd, nc.tensor
        NH, NCH, XC, HD, HP = self.NH, self.NCH, self.XC, self.HD, self.HP
        for e in (nc.tensor, nc.vector, nc.scalar, nc.sync, nc.gpsimd):
            e.wait_ge(self.s_cc, self.c_cc)
        self.wv(nc.sync); self.wp(nc.sync); self.wa(nc.sync)
        self.wso(nc.sync); self.wso(nc.scalar)
        NC_ = self.NCORES
        lda = self.smalls[0:NC_, 0:NH]
        src = bass.AP(self.d_ccout.ap().tensor, 128 * self.DSSM, [[129 * self.DSSM, NC_], [1, NH]])
        ld = self.dsi(y.dma_start(out=lda, in_=src))
        self.wsi(nc.tensor, ld)
        pc = self.pi(t.matmul(self.ps[0][0:NC_, 0:NH], self.rank_sb[0:NC_, 0:NC_], lda, start=True, stop=True))
        self.wp(nc.scalar, pc)
        fx = self.ai(s.activation(self.smalls[0:NC_, 64:64 + NH], self.ps[0][0:NC_, 0:NH], ACT.Exp))
        self.wa(nc.vector, fx)
        fz = self.vi(v.tensor_scalar(self.smalls[0:NC_, 64:64 + NH], self.smalls[0:NC_, 64:64 + NH],
                                     self.zmask_sb[0:NC_, :], None, op0=ALU.mult))
        hg = self.a_("hg", 0, self.DSSM)
        self.vi(v.memset(hg, 0.0))
        self.wv(nc.tensor, fz)
        pft = self.pi(t.transpose(self.ps[1][0:NH, 0:NC_], self.smalls[0:NC_, 64:64 + NH],
                                  self.ident[0:NC_, 0:NC_]))
        self.wp(nc.vector, pft)
        ftv = self.vi(v.tensor_copy(self.smalls[0:NH, 20:20 + NC_], self.ps[1][0:NH, 0:NC_]))
        for j in range(self.NCORES):
            self.wv(nc.vector)
            dgv = self.vi(v.tensor_scalar(self.smalls[0:NH, 64:64 + NH], self.ident[0:NH, 0:NH],
                                          self.smalls[0:NH, 20 + j:21 + j], None, op0=ALU.mult))
            self.wv(nc.tensor)
            pc = self.pi(t.matmul(self.ps[0][0:128, 0:NH], self.ones[0:NH, :],
                                  self.smalls[0:NH, 64:64 + NH], start=True, stop=True))
            self.wv(nc.scalar); self.wp(nc.scalar)
            ldS = self.dsd(nc.scalar.dma_start(out=self.a_("sread", 0, self.DSSM),
                                               in_=self.d_ccout[j * 129:j * 129 + 128, :]))
            self.wsd(nc.vector, ldS)
            self.wp(nc.vector, pc)
            frv = self.bcastv(bass.AP(self.ps[0].ap().tensor, self.ps[0].ap().offset,
                                      [[512, 128], [1, NH]]), NH, HD)
            self.wv(nc.vector)
            self.vi(v.tensor_tensor(self.a_("sread", 0, self.DSSM).rearrange("p (h d) -> p h d", h=NH),
                                    self.a_("sread", 0, self.DSSM).rearrange("p (h d) -> p h d", h=NH),
                                    frv, op=ALU.mult))
            self.wv(nc.vector)
            self.vi(v.tensor_tensor(hg, hg, self.a_("sread", 0, self.DSSM), op=ALU.add))
        if "dbg_hg" in self.dbg:
            self.wv(nc.gpsimd)
            self.dso(g_.dma_start(out=self.dbg["dbg_hg"].ap(), in_=hg))
        for cc in range(NCH):
            next(self.sink_gen, None)
            if cc > 0:
                dv = self.a_("dARep", (cc - 1) * NH, NH)
                dvv = self.bcastv(dv, NH, HD)
                self.wv(nc.vector)
                self.vi(v.tensor_tensor(hg.rearrange("p (h d) -> p h d", h=NH),
                                        hg.rearrange("p (h d) -> p h d", h=NH), dvv, op=ALU.mult))
                self.wv(nc.scalar); self.wp(nc.scalar)
                ldS = self.dsd(nc.scalar.dma_start(out=self.a_("sread", 0, self.DSSM),
                                                   in_=self.d_s[(cc - 1) * 128:cc * 128, :]))
                self.wsd(nc.vector, ldS)
                self.wv(nc.vector)
                self.vi(v.tensor_tensor(hg, hg, self.a_("sread", 0, self.DSSM), op=ALU.add))
            self.wv(nc.sync); self.wp(nc.sync)
            ldx = self.dsi(y.dma_start(out=self.a_("sread", 0, self.DSSM),
                                       in_=self.d_xstok[cc * 128:(cc + 1) * 128, :]))
            xstok = self.a_("sread", 0, self.DSSM)
            pcg = self.pi(t.matmul(self.ps[1][0:128, 0:128], self.bfm[:, cc * 128:(cc + 1) * 128],
                                   self.cfm[:, cc * 128:(cc + 1) * 128], start=True, stop=True))
            self.wp(nc.vector, pcg)
            gv = self.vi(v.tensor_copy(self.a_("gst", 0, 128), self.ps[1][0:128, 0:128]))
            self.wv(nc.vector)
            nv = self.vi(v.tensor_scalar(self.a_("nlt", 0, NH), self.a_("LT", cc * NH, NH), -1.0, None,
                                         op0=ALU.mult))
            self.wsi(nc.vector, ldx)
            self.wv(nc.tensor, nv)
            if not hasattr(self, "p3_ae"):
                self.p3_ae = [0, 0]; self.p3_mv = [0, 0]; self.p3_mm = [0, 0]
                self.p3_evac = 0; self.p3_ytr = [0, 0]; self.p3_ydma = [0, 0]
            mvh = [0] * NH

            def emit_T1EM(h):
                par = h % 2
                pT = self.ps[0] if par == 0 else self.ps[3]
                _c = self.a_("LT", cc * NH + h, 1)
                ltb = bass.AP(_c.tensor, _c.offset, [list(_c.ap[0]), [0, 128]])
                self.wa(nc.tensor, self.p3_ae[par])
                t.matmul(pT[0:128, 0:128], ltb, self.ident[:], start=True, stop=False)
                pc = self.pi(t.matmul(pT[0:128, 0:128], self.ident[:], self.trineg[:],
                                      start=False, stop=True))
                est = self.a_("est", par * 128, 128)
                mst = self.a_("mst", par * 128, 128)
                self.wp(nc.scalar, pc)
                self.wv(nc.scalar, max(self.p3_mv[par], nv))
                ae = self.ai(s.activation(est, pT[0:128, 0:128], ACT.Exp,
                                          bias=self.a_("nlt", h, 1)))
                self.p3_ae[par] = ae
                self.wa(nc.vector, ae)
                self.wp(nc.vector, self.p3_mm[par])
                mv = self.vi(v.scalar_tensor_tensor(mst, est, self.a_("dtT", cc * NH + h, 1),
                                                    self.a_("gst", 0, 128), op0=ALU.mult, op1=ALU.mult))
                self.p3_mv[par] = mv
                mvh[h] = mv

            emit_T1EM(0)
            for h in range(NH):
                if h + 1 < NH:
                    emit_T1EM(h + 1)
                hh = h % 8
                if hh == 0:
                    self.wv(nc.tensor, self.p3_evac)
                self.wv(nc.tensor, mvh[h])
                mst = self.a_("mst", (h % 2) * 128, 128)
                t.matmul(self.ps[4][0:128, hh * 64:hh * 64 + 64], mst, xstok[:, h * 64:h * 64 + 64],
                         start=True, stop=True)
                self.p3_mm[h % 2] = self.pi(t.matmul(self.ps[5][0:128, hh * 64:hh * 64 + 64],
                                 self.cfm[:, cc * 128:(cc + 1) * 128], hg[:, h * 64:h * 64 + 64],
                                 start=True, stop=True))
                if hh == 7:
                    hgp = h // 8
                    self.wp(nc.vector)
                    for hh2 in range(8):
                        h2 = hgp * 8 + hh2
                        self.vi(v.tensor_scalar(self.a_("yst3", h2 * 64, 64),
                                                self.ps[5][0:128, hh2 * 64:hh2 * 64 + 64],
                                                self.a_("acumT", cc * NH + h2, 1), None, op0=ALU.mult))
                        self.vi(v.tensor_tensor(self.a_("yst3", h2 * 64, 64), self.a_("yst3", h2 * 64, 64),
                                                self.ps[4][0:128, hh2 * 64:hh2 * 64 + 64], op=ALU.add))
                        self.vi(v.scalar_tensor_tensor(self.a_("yst3", h2 * 64, 64),
                                                       xstok[:, h2 * 64:h2 * 64 + 64],
                                                       self.drep[:, h2:h2 + 1], self.a_("yst3", h2 * 64, 64),
                                                       op0=ALU.mult, op1=ALU.add))
                    self.p3_evac = self.c_sv
            if "dbg_y" in self.dbg and cc == 0:
                self.wv(nc.gpsimd)
                self.dso(g_.dma_start(out=self.dbg["dbg_y"].ap(), in_=self.a_("yst3", 0, self.DSSM)))
            for kx in range(XC):
                self.wv(nc.tensor)
                pc = self.pi(t.transpose(self.ps[2][0:128, 0:128], self.a_("yst3", kx * 128, 128), self.ident[:]))
                self.wp(nc.vector, pc)
                yv = self.vi(v.tensor_copy(self.a_("ytr", 0, 128), self.ps[2][0:128, 0:128]))
                self.wv(nc.gpsimd, yv)
                self.dso(g_.dma_start(out=self.d_yfm[kx * 128:(kx + 1) * 128, cc * 128:(cc + 1) * 128],
                                      in_=self.a_("ytr", 0, 128)))
                self.wso(nc.vector)
        for _ in self.sink_gen:
            pass

    def phase4(self):
        nc = self.nc
        v, s, y, g_, t = nc.vector, nc.scalar, nc.sync, nc.gpsimd, nc.tensor
        XC, T, TW = self.XC, self.T, self.TW
        self.wso(nc.sync); self.wso(nc.scalar)
        for kx in range(XC):
            self.wv(nc.sync); self.wa(nc.sync)
            yl = self.dsi(y.dma_start(out=self.a_("y4", 0, T), in_=self.d_yfm[kx * 128:(kx + 1) * 128, :]))
            self.wv(nc.scalar); self.wp(nc.scalar)
            zl = self.dsd(nc.scalar.dma_start(out=self.a_("sz4", 0, T), in_=self.d_sz[kx * 128:(kx + 1) * 128, :]))
            self.wsi(nc.vector, yl)
            self.wsd(nc.vector, zl)
            gv = self.vi(v.tensor_tensor(self.a_("yg4", 0, T), self.a_("y4", 0, T), self.a_("sz4", 0, T),
                                         op=ALU.mult))
            self.wv(nc.scalar, gv)
            sqv = self.ai(s.activation(self.a_("sq4", 0, T), self.a_("yg4", 0, T), ACT.Square))
            self.wa(nc.tensor, sqv)
            for j, (to, tn) in enumerate(self.ttiles):
                self.pi(t.matmul(self.ps[2 + j][0:1, 0:tn], self.ones[0:128, 0:1],
                                 self.a_("sq4", to, tn), start=(kx == 0), stop=(kx == XC - 1)))
        self.wp(nc.vector)
        row = self.a_("row4", 0, T, pn=1)
        for j, (to, tn) in enumerate(self.ttiles):
            self.vi(v.tensor_copy(self.a_("row4", to, tn, pn=1), self.ps[2 + j][0:1, 0:tn]))
        self.wv(nc.vector)
        self.vi(v.tensor_scalar(row, row, 1.0 / self.DSSM, self.EPS_A, op0=ALU.mult, op1=ALU.add))
        self.wv(nc.scalar)
        sa = self.ai(s.activation(row, row, ACT.Sqrt))
        self.wa(nc.vector, sa)
        rv = self.vi(v.reciprocal(row, row))
        self.wv(nc.tensor, rv)
        for j, (to, tn) in enumerate(self.ttiles):
            pc = self.pi(t.matmul(self.ps[0][0:128, 0:tn], self.ones[0:1, :], self.a_("row4", to, tn, pn=1),
                                  start=True, stop=True))
            self.wp(nc.vector, pc)
            ev = self.vi(v.tensor_copy(self.a_("rrep", to, tn), self.ps[0][0:128, 0:tn]))
            self.wv(nc.tensor, ev)
        for kx in range(XC):
            self.wv(nc.sync)
            yl = self.dsi(y.dma_start(out=self.a_("y4", 0, T), in_=self.d_yfm[kx * 128:(kx + 1) * 128, :]))
            self.wv(nc.scalar); self.wp(nc.scalar)
            zl = self.dsd(nc.scalar.dma_start(out=self.a_("sz4", 0, T), in_=self.d_sz[kx * 128:(kx + 1) * 128, :]))
            self.wsi(nc.vector, yl)
            self.wsd(nc.vector, zl)
            gv = self.vi(v.tensor_tensor(self.a_("yg4", 0, T), self.a_("y4", 0, T), self.a_("sz4", 0, T),
                                         op=ALU.mult))
            self.wv(nc.vector, gv)
            nw = self.ccols[:, self.o_nrmw + kx:self.o_nrmw + kx + 1]
            self.vi(v.scalar_tensor_tensor(self.xa[:, kx * TW + 3:kx * TW + 3 + T], self.a_("yg4", 0, T),
                                           nw, self.a_("rrep", 0, T), op0=ALU.mult, op1=ALU.mult))
        if "dbg_yn" in self.dbg:
            self.wv(nc.gpsimd)
            self.dso(g_.dma_start(out=self.dbg["dbg_yn"].ap(), in_=self.xa[:, 0:XC * TW]))

    def phase5(self):
        nc = self.nc
        v, s, y, g_, t = nc.vector, nc.scalar, nc.sync, nc.gpsimd, nc.tensor
        XC, T, D, TC = self.XC, self.T, self.D, self.TC
        nwid = min(512, D)
        NB = (D + nwid - 1) // nwid
        assert TC <= 8
        self.wv(nc.sync); self.wa(nc.sync); self.wp(nc.sync)
        self.wso(nc.sync); self.wso(nc.scalar)
        last = [0, 0]
        for nb in range(NB):
            for k in range(XC):
                par = k % 2
                self.wp(nc.sync, last[par])
                r0 = (nb * XC + k) * 128
                self.dwi(par, y.dma_start(out=self.a_("w5", par * 512, nwid),
                                           in_=self.i_w2b[r0:r0 + 128, 0:nwid]))
                self.wwi(nc.tensor, par)
                if k == 0:
                    self.wv(nc.tensor)
                mm = None
                for ct in range(TC):
                    mm = t.matmul(self.ps[ct][0:128, 0:nwid], self.xa[0:128, self.tw_off(k, 3 + ct * 128):
                                                                      self.tw_off(k, 3 + ct * 128) + 128],
                                  self.a_("w5", par * 512, nwid), start=(k == 0), stop=(k == XC - 1))
                last[par] = self.pi(mm)
            if not hasattr(self, "p5_ov"):
                self.p5_ov = [0] * TC; self.p5_od = [0, 0]
            for ct in range(TC):
                self.wv(nc.scalar); self.wp(nc.scalar)
                fl = self.dsd(nc.scalar.dma_start(out=self.a_("ffnx5", (ct % 2) * 512, nwid),
                                                  in_=self.d_ffnx[ct * 128:(ct + 1) * 128,
                                                                  nb * nwid:nb * nwid + nwid]))
                self.wsd(nc.vector, fl)
                self.wp(nc.vector, last[(XC - 1) % 2])
                self.wso(nc.vector, self.p5_od[ct % 2])
                ov = self.vi(v.tensor_tensor(self.a_("ost5", (ct % 2) * 512, nwid), self.ps[ct][0:128, 0:nwid],
                                             self.a_("ffnx5", (ct % 2) * 512, nwid), op=ALU.add))
                self.p5_ov[ct] = ov
                self.wv(nc.gpsimd, ov)
                self.p5_od[ct % 2] = self.dso(g_.dma_start(
                    out=self.o_out[ct * 128:(ct + 1) * 128, nb * nwid:nb * nwid + nwid],
                    in_=self.a_("ost5", (ct % 2) * 512, nwid)))


def kernel(**inputs):
    cfg = real_cfg()
    b = Build(cfg)
    maps = b.host_inputs(inputs)
    nc = b.build(dbg=False)
    res = run_bass_kernel_spmd(nc, maps, core_ids=list(range(cfg["NCORES"])))
    outs = [r["out"] for r in res.results]
    full = np.concatenate(outs, 0).reshape(1, cfg["NCORES"] * cfg["T"], cfg["D"])
    return np.ascontiguousarray(full.astype(np.float32))


# version 18
# speedup vs baseline: 1.2753x; 1.0259x over previous
import numpy as np
import concourse.bass as bass
import concourse.mybir as mybir
from concourse.bass_utils import run_bass_kernel_spmd

F32 = mybir.dt.float32
BF16 = mybir.dt.bfloat16
ALU = mybir.AluOpType
ACT = mybir.ActivationFunctionType
AX = mybir.AxisListType

NEG_BIG = -1.0e30


def real_cfg():
    return dict(
        NCORES=8, D=4096, DSSM=4096, S=128, NH=64, HD=64, K=4, T=1024, Q=128,
        NSTR=4, CDIM=1024, SINK_IT=50, TAU=0.1,
        EPS_LN=1e-5, EPS_A=1e-5, EPS_F=1.1920929e-07,
        MM="fp32",
    )


class Build:
    def __init__(self, cfg):
        self.cfg = cfg
        for k_, v_ in cfg.items():
            setattr(self, k_, v_)
        assert self.S == 128 and self.HD == 64 and self.K == 4 and self.Q == 128
        self.KC = self.D // 128
        self.XC = self.DSSM // 128
        self.NCH = self.T // self.Q
        self.TC = self.T // 128
        self.CONV = self.DSSM + 2 * self.S
        self.PROJ = self.DSSM + self.CONV + self.NH
        self.TW = self.T + 4
        self.NL = self.NSTR * self.NSTR + 2 * self.NSTR
        self.CC = (self.CDIM + 127) // 128
        self.HP = 128 // self.HD
        self.NC_B = 2 * self.XC
        self.NC_C = 2 * self.XC + 1
        self.NC_DT = 2 * self.XC + 2
        self.NCHK = 2 * self.XC + 3
        self.ttiles = []
        o = 0
        while o < self.T:
            n = min(512, self.T - o)
            self.ttiles.append((o, n))
            o += n
        self.NTT = len(self.ttiles)
        assert self.NTT <= 2
        A = {}
        A["p0_x"] = 0; A["p0_xa"] = 4096
        A["stage"] = 0
        A["xsil"] = 1056
        A["acc"] = 2080
        A["btok"] = 3104
        A["piece"] = 7424; A["xw"] = 7680; A["sstage"] = 7936
        A["stageB"] = 12288
        A["wlgall"] = 3264
        A["dt_ax"] = 4608; A["dt_mx"] = 5632
        A["wraw"] = 6656
        A["hloc"] = 8192
        A["z_st"] = 0; A["sz_st"] = 1088; A["lg_st"] = 2176; A["wlg"] = 3264
        A["logT"] = 8192; A["P"] = 8448; A["rs"] = 8640; A["hp"] = 8704; A["ho"] = 8768
        A["fx"] = 0; A["fxn"] = 4096
        A["xp"] = 8832; A["xpt"] = 9856; A["wbr"] = 10880; A["ybr"] = 11456
        A["hg"] = 0; A["sread"] = 4096; A["yst3"] = 8192
        A["gst"] = 12288; A["nlt"] = 12416; A["lrow"] = 12480
        A["est"] = 12736; A["mst"] = 12992; A["ytr"] = 13248
        A["ldiag"] = 12608
        A["y4"] = 0; A["sz4"] = 4096; A["yg4"] = 8192; A["sq4"] = 9216
        A["row4"] = 10304; A["rrep"] = 11392
        A["w5"] = 0; A["ffnx5"] = 2048; A["ost5"] = 3072
        A["LT"] = 13440; A["dtT"] = 13952; A["acumT"] = 14464
        A["wfacT"] = 14976; A["dARep"] = 15488
        A["statc"] = 16000; A["lendrep"] = 16128
        self.A = A

    def host_inputs(self, inputs):
        x = np.ascontiguousarray(np.asarray(inputs["L0_attn_hidden_states"], np.float32).reshape(-1, self.D))
        L = x.shape[0]
        assert L == self.NCORES * self.T, (L, self.NCORES, self.T)
        wcat = np.ascontiguousarray(
            np.concatenate([np.asarray(inputs["L0_ffn_W_res"], np.float32),
                            np.asarray(inputs["L0_ffn_W_pre"], np.float32),
                            np.asarray(inputs["L0_ffn_W_post"], np.float32)], 0).T)
        wcat = np.ascontiguousarray(
            wcat.reshape(self.KC, 128, self.NL).transpose(1, 0, 2).reshape(128, self.KC * self.NL))
        basec = np.ascontiguousarray(
            np.concatenate([np.asarray(inputs["L0_ffn_H_res_base"], np.float32).reshape(-1),
                            np.asarray(inputs["L0_ffn_H_pre_base"], np.float32),
                            np.asarray(inputs["L0_ffn_H_post_base"], np.float32)])[None, :])
        sidx = np.arange(128)[:, None]; tidx = np.arange(128)[None, :]
        KC = self.KC
        w1 = np.asarray(inputs["L0_attn_in_proj_w"], np.float32)
        w2 = np.asarray(inputs["L0_attn_out_proj_w"], np.float32)
        wbr0 = np.asarray(inputs["L0_ffn_W_branch"], np.float32)

        def blocktiles(groups, padw):
            out = np.zeros((len(groups) * KC * 128, padw), np.float32)
            for gi, g in enumerate(groups):
                oc = 0
                for ncid in g:
                    c0, w = self.nc_cols(ncid)
                    blk = w1[:, c0:c0 + w].reshape(KC, 128, w)
                    out.reshape(len(groups), KC, 128, padw)[gi, :, :, oc:oc + w] = blk
                    oc += w
            return np.ascontiguousarray(out)

        w1p1 = blocktiles(self.wslab_groups(1), 256)
        w1p2 = blocktiles(self.wslab_groups(2), 384)
        nwid = min(512, self.D)
        NB = (self.D + nwid - 1) // nwid
        w2b = np.ascontiguousarray(
            w2.reshape(self.XC, 128, NB, nwid).transpose(2, 0, 1, 3).reshape(NB * self.XC * 128, nwid))
        nby = (self.CDIM + 511) // 512
        wbrb = np.ascontiguousarray(
            wbr0.reshape(self.CC, 128, nby, 512).transpose(2, 0, 1, 3).reshape(nby * self.CC * 128, 512))
        common = dict(
            w1p1=w1p1, w1p2=w1p2, w2b=w2b, wbrb=wbrb,
            wcat=wcat, basec=basec,
            eyer=np.eye(self.NSTR, dtype=np.float32).reshape(1, -1),
            ident=np.eye(128, dtype=np.float32),
            allones=np.ones((128, 128), np.float32),
            trineg=np.where(sidx <= tidx, 0.0, NEG_BIG).astype(np.float32),
            tril1=(sidx <= tidx).astype(np.float32),
            convw=np.ascontiguousarray(np.asarray(inputs["L0_attn_conv_w"], np.float32)),
            alog=np.asarray(inputs["L0_attn_A_log"], np.float32).reshape(1, -1),
            dtb=np.asarray(inputs["L0_attn_dt_bias"], np.float32).reshape(-1, 1),
            dh=np.asarray(inputs["L0_attn_D"], np.float32).reshape(1, -1),
            nrmw=np.asarray(inputs["L0_attn_norm_w"], np.float32).reshape(-1),
            lnw=np.asarray(inputs["L0_attn_ln_w"], np.float32).reshape(-1),
            mup=np.asarray(inputs["L0_attn_mup_vector"], np.float32).reshape(-1),
            gate=np.asarray(inputs["L0_ffn_gate"], np.float32).reshape(1, 1),
        )
        maps = []
        for i in range(self.NCORES):
            lo = i * self.T
            halo = x[lo - 3:lo] if i > 0 else np.zeros((3, self.D), np.float32)
            rm = np.zeros((8, 8), np.float32)
            zm = np.zeros((8, 1), np.float32)
            for j in range(8):
                if j < i:
                    zm[j, 0] = 1.0
                    for k in range(j + 1, i):
                        rm[k, j] = 1.0
            m = dict(common)
            m.update(xsh=np.ascontiguousarray(np.concatenate([halo, x[lo:lo + self.T]], 0)),
                     rankm=rm, zmask=zm)
            maps.append(m)
        return maps

    def build(self, dbg=False):
        nc = bass.Bass("TRN2", target_bir_lowering=False, debug=False)
        self.nc = nc
        D, DSSM, NH, T, NL, NSTR, CDIM = self.D, self.DSSM, self.NH, self.T, self.NL, self.NSTR, self.CDIM
        din = lambda n, s: nc.dram_tensor(n, s, F32, kind="ExternalInput")
        self.i_x = din("xsh", [T + 3, D])
        self.NG1 = len(self.wslab_groups(1))
        self.NG2 = len(self.wslab_groups(2))
        self.NB5 = (D + min(512, D) - 1) // min(512, D)
        self.NBY = (CDIM + 511) // 512
        self.i_w1p1 = din("w1p1", [self.NG1 * self.KC * 128, 256])
        self.i_w1p2 = din("w1p2", [self.NG2 * self.KC * 128, 384])
        self.i_w2b = din("w2b", [self.NB5 * self.XC * 128, min(512, D)])
        self.i_wbrb = din("wbrb", [self.NBY * self.CC * 128, 512])
        self.i_wcat = din("wcat", [128, self.KC * NL])
        self.i_conv = din("convw", [self.CONV, self.K])
        self.i_alog = din("alog", [1, NH])
        self.i_dtb = din("dtb", [NH, 1])
        self.i_dh = din("dh", [1, NH])
        self.i_nrmw = din("nrmw", [DSSM])
        self.i_lnw = din("lnw", [D])
        self.i_mup = din("mup", [self.PROJ])
        self.i_gate = din("gate", [1, 1])
        self.i_basec = din("basec", [1, NL])
        self.i_eyer = din("eyer", [1, NSTR * NSTR])
        self.i_ident = din("ident", [128, 128])
        self.i_ones = din("allones", [128, 128])
        self.i_trineg = din("trineg", [128, 128])
        self.i_tril1 = din("tril1", [128, 128])
        self.i_rankm = din("rankm", [8, 8])
        self.i_zmask = din("zmask", [8, 1])
        self.o_out = nc.dram_tensor("out", [T, D], F32, kind="ExternalOutput")
        self.d_xstok = nc.dram_tensor("xstok_d", [T, DSSM], F32)
        self.d_s = nc.dram_tensor("s_d", [self.NCH * 128, DSSM], F32)
        self.d_sz = nc.dram_tensor("sz_d", [DSSM, T], F32)
        self.d_yfm = nc.dram_tensor("yfm_d", [DSSM, T], F32)
        self.d_ffnx = nc.dram_tensor("ffnx_d", [T, D], F32)
        self.d_ccin = nc.dram_tensor("ccin", [129, DSSM], F32)
        self.d_ccout = nc.dram_tensor("ccout", [self.NCORES * 129, DSSM], F32)
        self.dbg = {}
        if dbg:
            for nm, sh in [("dbg_xa", [128, self.KC * self.TW]), ("dbg_lt", [128, self.NCH * NH]),
                           ("dbg_wf", [128, self.NCH * NH]),
                           ("dbg_bfm", [128, T]), ("dbg_hloc", [128, DSSM]),
                           ("dbg_stB", [128, 1028]), ("dbg_accB", [128, T]),
                           ("dbg_logt", [128, self.TC * NL]), ("dbg_p", [128, self.TC * NSTR * NSTR]),
                           ("dbg_y", [128, DSSM]), ("dbg_yn", [128, self.XC * self.TW]),
                           ("dbg_hg", [128, DSSM])]:
                self.dbg[nm] = nc.dram_tensor(nm, sh, F32, kind="ExternalOutput")

        import contextlib
        st = contextlib.ExitStack()
        sb = lambda n, s, d=F32: st.enter_context(nc.sbuf_tensor(n, s, d))
        if self.MM == "fp32":
            self.xa = sb("xa", [128, self.KC * self.TW])
        else:
            raise NotImplementedError
        self.AR = sb("AR", [128, 16192])
        self.bfm = sb("bfm", [128, T])
        self.cfm = sb("cfm", [128, T])
        self.ident = sb("ident_sb", [128, 128])
        self.ones = sb("ones_sb", [128, 128])
        self.trineg = sb("trineg_sb", [128, 128])
        self.tril1 = sb("tril1_sb", [128, 128])
        ncc = self.KC + self.XC + self.NCHK + (self.CONV // 128) * self.K + 8
        self.ccols = sb("ccols", [128, ncc])
        self.o_lnw = 0
        self.o_nrmw = self.KC
        self.o_mup = self.KC + self.XC
        self.o_convc = self.KC + self.XC + self.NCHK
        self.smalls = sb("smalls", [128, 136])
        self.rank_sb = sb("rank_sb", [8, 8])
        self.zmask_sb = sb("zmask_sb", [8, 1])
        self.rows = sb("rows_sb", [1, 192])
        self.dtb_sb = sb("dtb_sb", [NH, 1])
        self.base_rep = sb("base_rep", [128, NL])
        self.hbl = sb("hbl", [128, NSTR * NSTR])
        self.gcol = sb("gcol", [128, 1])
        self.drep = sb("drep", [128, NH])
        self.negarep = sb("negarep", [128, NH])
        self.rlnw = sb("rlnw", [128, self.KC])
        self.sink_sb = sb("sink_sb", [128, 448])
        self.SK = {"logT": 0, "P": 192, "rs": 320, "hp": 352, "ho": 384}
        self.ps = [st.enter_context(nc.psum_tensor(f"ps{i}", [128, 512], F32)) for i in range(8)]
        sem = lambda n: st.enter_context(nc.semaphore(n))
        self.s_si = sem("si"); self.c_si = 0
        self.s_so = sem("so"); self.c_so = 0
        self.s_sd = sem("sd"); self.c_sd = 0
        self.s_sv = sem("sv"); self.c_sv = 0
        self.s_sa = sem("sa"); self.c_sa = 0
        self.s_sp = sem("sp"); self.c_sp = 0
        self.s_cc = sem("scc"); self.c_cc = 0
        self.s_w = [sem("sw0"), sem("sw1")]; self.c_w = [0, 0]
        blk = st.enter_context(nc.Block())

        @blk.gpsimd
        def _(g):
            self.emit_all()
        st.close()
        return nc

    def vi(self, instr):
        instr.then_inc(self.s_sv, 1); self.c_sv += 1; return self.c_sv

    def ai(self, instr):
        instr.then_inc(self.s_sa, 1); self.c_sa += 1; return self.c_sa

    def pi(self, instr):
        instr.then_inc(self.s_sp, 1); self.c_sp += 1; return self.c_sp

    def dsi(self, instr):
        instr.then_inc(self.s_si, 16); self.c_si += 16; return self.c_si

    def dso(self, instr):
        instr.then_inc(self.s_so, 16); self.c_so += 16; return self.c_so

    def dsd(self, instr):
        instr.then_inc(self.s_sd, 16); self.c_sd += 16; return self.c_sd

    def dwi(self, par, instr):
        instr.then_inc(self.s_w[par], 16); self.c_w[par] += 16; return self.c_w[par]

    def wwi(self, eng, par, n=None):
        n = self.c_w[par] if n is None else n
        if n > 0: eng.wait_ge(self.s_w[par], n)

    def wv(self, eng, n=None):
        n = self.c_sv if n is None else n
        if n > 0: eng.wait_ge(self.s_sv, n)

    def wa(self, eng, n=None):
        n = self.c_sa if n is None else n
        if n > 0: eng.wait_ge(self.s_sa, n)

    def wp(self, eng, n=None):
        n = self.c_sp if n is None else n
        if n > 0: eng.wait_ge(self.s_sp, n)

    def wsi(self, eng, n=None):
        n = self.c_si if n is None else n
        if n > 0: eng.wait_ge(self.s_si, n)

    def wso(self, eng, n=None):
        n = self.c_so if n is None else n
        if n > 0: eng.wait_ge(self.s_so, n)

    def wsd(self, eng, n=None):
        n = self.c_sd if n is None else n
        if n > 0: eng.wait_ge(self.s_sd, n)

    def s_(self, name, c0=0, cn=None, p0=0, pn=128):
        o = self.SK[name] + c0
        return self.sink_sb[p0:p0 + pn, o:o + cn]

    def a_(self, name, c0=0, cn=None, p0=0, pn=128):
        base = self.A[name]
        cn = 1 if cn is None else cn
        return self.AR[p0:p0 + pn, base + c0:base + c0 + cn]

    def bcastv(self, ap2d, ngrp, rep):
        return bass.AP(ap2d.tensor, ap2d.offset,
                       [list(ap2d.ap[0]), [ap2d.ap[1][0], ngrp], [0, rep]])

    def tw_off(self, k, col):
        return k * self.TW + col

    def emit_all(self):
        self.load_consts()
        self.phase0()
        self.phase1()
        self.collective()
        self.phase2()
        self.phase3()
        self.ffn_mix()
        self.phase4()
        self.phase5()
        nc = self.nc
        nc.sync.wait_ge(self.s_si, self.c_si)
        nc.scalar.wait_ge(self.s_sd, self.c_sd)
        nc.gpsimd.wait_ge(self.s_so, self.c_so)

    def load_consts(self):
        nc = self.nc
        v, s, y, t = nc.vector, nc.scalar, nc.sync, nc.tensor
        NH, KC, XC, NL, NSTR = self.NH, self.KC, self.XC, self.NL, self.NSTR
        self.dsi(y.dma_start(out=self.ident[:], in_=self.i_ident.ap()))
        self.dsi(y.dma_start(out=self.ones[:], in_=self.i_ones.ap()))
        self.dsi(y.dma_start(out=self.trineg[:], in_=self.i_trineg.ap()))
        self.dsi(y.dma_start(out=self.tril1[:], in_=self.i_tril1.ap()))
        self.dsi(y.dma_start(out=self.rank_sb[:], in_=self.i_rankm.ap()))
        self.dsi(y.dma_start(out=self.zmask_sb[:], in_=self.i_zmask.ap()))
        self.dsi(y.dma_start(out=self.dtb_sb[:], in_=self.i_dtb.ap()))
        self.o_rbase, self.o_reye, self.o_rgate, self.o_ralog, self.o_rdh = 0, NL, NL + 16, NL + 24, NL + 24 + NH
        self.dsi(y.dma_start(out=self.rows[0:1, 0:NL], in_=self.i_basec.ap()))
        self.dsi(y.dma_start(out=self.rows[0:1, self.o_reye:self.o_reye + NSTR * NSTR], in_=self.i_eyer.ap()))
        self.dsi(y.dma_start(out=self.rows[0:1, self.o_rgate:self.o_rgate + 1], in_=self.i_gate.ap()))
        self.dsi(y.dma_start(out=self.rows[0:1, self.o_ralog:self.o_ralog + NH], in_=self.i_alog.ap()))
        self.dsi(y.dma_start(out=self.rows[0:1, self.o_rdh:self.o_rdh + NH], in_=self.i_dh.ap()))
        self.dsi(y.dma_start(out=self.ccols[:, self.o_lnw:self.o_lnw + KC],
                             in_=self.i_lnw.ap().rearrange("(c p) -> p c", p=128), allow_slow_non_contiguous=True))
        self.dsi(y.dma_start(out=self.ccols[:, self.o_nrmw:self.o_nrmw + XC],
                             in_=self.i_nrmw.ap().rearrange("(c p) -> p c", p=128), allow_slow_non_contiguous=True))
        nfull = self.PROJ // 128
        self.dsi(y.dma_start(out=self.ccols[:, self.o_mup:self.o_mup + nfull],
                             in_=self.i_mup[0:nfull * 128].rearrange("(c p) -> p c", p=128), allow_slow_non_contiguous=True))
        if self.PROJ > nfull * 128:
            self.dsi(y.dma_start(out=self.ccols[0:NH, self.o_mup + nfull:self.o_mup + nfull + 1],
                                 in_=self.i_mup[nfull * 128:nfull * 128 + NH].rearrange("(p o) -> p o", o=1), allow_slow_non_contiguous=True))
        ncv = self.CONV // 128
        self.dsi(y.dma_start(out=self.ccols[:, self.o_convc:self.o_convc + ncv * self.K].rearrange(
                                 "p (c k) -> p c k", k=self.K),
                             in_=self.i_conv.ap().rearrange("(c p) k -> p c k", p=128),
                             allow_slow_non_contiguous=True))
        self.wsi(nc.vector); self.wsi(nc.scalar); self.wsi(nc.tensor)
        self.vi(v.reciprocal(self.rlnw[:, 0:KC], self.ccols[:, self.o_lnw:self.o_lnw + KC]))
        ps = self.ps[0]
        t.matmul(ps[:, 0:NL], self.ones[0:1, :], self.rows[0:1, 0:NL], start=True, stop=True)
        t.matmul(ps[:, 32:32 + 16], self.ones[0:1, :], self.rows[0:1, self.o_reye:self.o_reye + 16],
                 start=True, stop=True)
        t.matmul(ps[:, 64:65], self.ones[0:1, :], self.rows[0:1, self.o_rgate:self.o_rgate + 1],
                 start=True, stop=True)
        pc = self.pi(t.matmul(ps[:, 96:96 + NH], self.ones[0:1, :], self.rows[0:1, self.o_rdh:self.o_rdh + NH],
                              start=True, stop=True))
        self.wp(nc.vector, pc)
        self.vi(v.tensor_copy(self.base_rep[:], ps[:, 0:NL]))
        self.vi(v.tensor_copy(self.drep[:], ps[:, 96:96 + NH]))
        self.wp(nc.scalar, pc)
        ga = self.ai(s.activation(self.gcol[:], ps[:, 64:65], ACT.Sigmoid))
        self.wa(nc.vector, ga)
        gm = self.vi(v.tensor_scalar(self.smalls[:, 130:131], self.gcol[:], -1.0, 1.0, op0=ALU.mult, op1=ALU.add))
        self.vi(v.tensor_copy(self.hbl[:], ps[:, 32:48]))
        self.wv(nc.vector)
        self.vi(v.tensor_scalar(self.hbl[:], self.hbl[:], self.smalls[:, 130:131], None, op0=ALU.mult))
        ea = self.ai(s.activation(self.rows[0:1, self.o_ralog:self.o_ralog + NH],
                                  self.rows[0:1, self.o_ralog:self.o_ralog + NH], ACT.Exp))
        self.wa(nc.vector, ea)
        na = self.vi(v.tensor_scalar(self.rows[0:1, self.o_ralog:self.o_ralog + NH],
                                     self.rows[0:1, self.o_ralog:self.o_ralog + NH], -1.0, None, op0=ALU.mult))
        self.wv(nc.tensor, na)
        pc2 = self.pi(t.matmul(ps[:, 128:128 + NH], self.ones[0:1, :],
                               self.rows[0:1, self.o_ralog:self.o_ralog + NH], start=True, stop=True))
        self.wp(nc.vector, pc2)
        self.vi(v.tensor_copy(self.negarep[:], ps[:, 128:128 + NH]))

    def phase0(self):
        nc = self.nc
        v, s, y, t = nc.vector, nc.scalar, nc.sync, nc.tensor
        T, TC, KC, D = self.T, self.TC, self.KC, self.D
        EPSL, EPSA, EPSF = self.EPS_LN, self.EPS_A, self.EPS_F
        sm = self.smalls
        for ct in range(TC + 1):
            halo = (ct == TC)
            pn = 3 if halo else 128
            r0 = 0 if halo else 3 + ct * 128
            self.wv(nc.sync)
            self.wa(nc.sync)
            self.wp(nc.sync)
            ld = self.dsi(y.dma_start(out=self.a_("p0_x", 0, D, pn=pn), in_=self.i_x[r0:r0 + pn, :]))
            self.wsi(nc.vector, ld)
            self.wsi(nc.scalar, ld)
            xt = self.a_("p0_x", 0, D, pn=pn)
            c0 = ct * 4
            stc = lambda j: self.a_("statc", c0 + j, 1, pn=pn)
            self.vi(v.tensor_reduce(sm[0:pn, 0:1], xt, axis=AX.X, op=ALU.add))
            sq = self.a_("p0_xa", 0, D, pn=pn)
            sa1 = self.ai(s.activation(sq, xt, ACT.Square))
            self.wa(nc.vector, sa1)
            self.vi(v.tensor_reduce(sm[0:pn, 1:2], sq, axis=AX.X, op=ALU.add))
            self.wv(nc.vector)
            self.vi(v.tensor_scalar(stc(0), sm[0:pn, 0:1], 1.0 / D, None, op0=ALU.mult))
            self.vi(v.tensor_scalar(sm[0:pn, 1:2], sm[0:pn, 1:2], 1.0 / D, None, op0=ALU.mult))
            self.wv(nc.vector)
            self.vi(v.tensor_tensor(sm[0:pn, 2:3], stc(0), stc(0), op=ALU.mult))
            self.wv(nc.vector)
            self.vi(v.tensor_tensor(sm[0:pn, 3:4], sm[0:pn, 1:2], sm[0:pn, 2:3], op=ALU.subtract))
            self.wv(nc.vector)
            self.vi(v.tensor_scalar(sm[0:pn, 4:5], sm[0:pn, 3:4], EPSL, None, op0=ALU.add))
            self.wv(nc.scalar)
            sa2 = self.ai(s.activation(sm[0:pn, 5:6], sm[0:pn, 4:5], ACT.Sqrt))
            self.wa(nc.vector, sa2)
            self.vi(v.reciprocal(stc(1), sm[0:pn, 5:6]))
            self.wv(nc.vector)
            self.vi(v.tensor_tensor(sm[0:pn, 6:7], sm[0:pn, 3:4], stc(1), op=ALU.mult))
            self.wv(nc.vector)
            self.vi(v.tensor_tensor(sm[0:pn, 6:7], sm[0:pn, 6:7], stc(1), op=ALU.mult))
            self.wv(nc.vector)
            self.vi(v.tensor_scalar(sm[0:pn, 7:8], sm[0:pn, 6:7], EPSA, None, op0=ALU.add))
            self.vi(v.tensor_scalar(sm[0:pn, 8:9], sm[0:pn, 6:7], EPSF, None, op0=ALU.add))
            self.wv(nc.scalar)
            sa3 = self.ai(s.activation(sm[0:pn, 9:10], sm[0:pn, 7:8], ACT.Sqrt))
            self.wa(nc.vector, sa3)
            self.vi(v.reciprocal(sm[0:pn, 10:11], sm[0:pn, 9:10]))
            self.wv(nc.vector)
            self.vi(v.tensor_tensor(stc(2), stc(1), sm[0:pn, 10:11], op=ALU.mult))
            self.vi(v.reciprocal(sm[0:pn, 11:12], sm[0:pn, 8:9]))
            self.wv(nc.vector)
            self.vi(v.tensor_tensor(sm[0:pn, 12:13], sm[0:pn, 7:8], sm[0:pn, 11:12], op=ALU.mult))
            self.wv(nc.scalar)
            sa4 = self.ai(s.activation(stc(3), sm[0:pn, 12:13], ACT.Sqrt))
            self.wa(nc.vector, sa4)
            self.wv(nc.vector)
            self.vi(v.tensor_scalar(self.a_("p0_xa", 0, D, pn=pn), xt, stc(0), stc(2),
                                    op0=ALU.subtract, op1=ALU.mult))
            self.wv(nc.tensor)
            if not hasattr(self, "p0_ev"):
                self.p0_ev = [0, 0]
            for k in range(KC):
                par = k % 2
                pb = self.ps[6 + par]
                self.wv(nc.tensor, self.p0_ev[par])
                pc = self.pi(t.transpose(pb[0:128, 0:pn], self.a_("p0_xa", k * 128, 128, pn=pn),
                                         self.ident[0:pn, 0:pn]))
                self.wp(nc.vector, pc)
                dst = self.tw_off(k, 0 if halo else 3 + ct * 128)
                ev = self.vi(v.tensor_scalar(self.xa[:, dst:dst + pn], pb[0:128, 0:pn],
                                             self.ccols[:, self.o_lnw + k:self.o_lnw + k + 1], None,
                                             op0=ALU.mult))
                self.p0_ev[par] = ev

    def wslab_groups(self, phase):
        XC = self.XC
        if phase == 1:
            groups = [[self.NC_B, self.NC_C], [self.NC_DT, XC]]
            xs = list(range(XC + 1, 2 * XC))
            for i in range(0, len(xs), 2):
                groups.append(xs[i:i + 2])
        else:
            zs = list(range(XC))
            groups = []
            for i in range(0, len(zs), 3):
                groups.append(zs[i:i + 3])
        return groups

    def nc_cols(self, ncid):
        w = self.NH if ncid == self.NC_DT else 128
        return ncid * 128, w

    def emit_group(self, g, gsrc=None, with_logits=False, halo=True, bank3=True):
        nc = self.nc
        v, s, y, t = nc.vector, nc.scalar, nc.sync, nc.tensor
        KC = self.KC
        c0 = self.nc_cols(g[0])[0]
        wsum = sum(self.nc_cols(n)[1] for n in g)
        if not hasattr(self, "w_last_mm"):
            self.w_last_mm = [0, 0]
        last_mm = self.w_last_mm
        for k in range(KC):
            par = k % 2
            self.wp(nc.sync, last_mm[par])
            gten, gidx, padw = gsrc
            r0 = (gidx * KC + k) * 128
            self.dwi(par, y.dma_start(out=self.a_("wraw", par * 384, padw),
                                      in_=gten[r0:r0 + 128, 0:padw]))
            self.wwi(nc.tensor, par)
            if k == 0:
                self.wv(nc.tensor, self.bank_free)
                if with_logits:
                    self.wv(nc.tensor, self.wlg_ready)
            mm = None
            for i, ncid in enumerate(g):
                oc = sum(self.nc_cols(n)[1] for n in g[:i])
                w = self.nc_cols(ncid)[1]
                lhs = self.a_("wraw", par * 384 + oc, w)
                bb = 3 * i if bank3 else 2 * i
                for j, (to, tn) in enumerate(self.ttiles):
                    mm = t.matmul(self.ps[bb + j][0:w, 0:tn], lhs,
                                  self.xa[:, self.tw_off(k, 3 + to):self.tw_off(k, 3 + to) + tn],
                                  start=(k == 0), stop=(k == KC - 1))
                if halo and ncid != self.NC_DT:
                    mm = t.matmul(self.ps[3 * i + 2][0:w, 0:3], lhs,
                                  self.xa[:, self.tw_off(k, 0):self.tw_off(k, 0) + 3],
                                  start=(k == 0), stop=(k == KC - 1))
            if with_logits:
                lhs = self.a_("wlgall", k * self.NL, self.NL)
                for j, (to, tn) in enumerate(self.ttiles):
                    mm = t.matmul(self.ps[6 + j][0:self.NL, 0:tn], lhs,
                                  self.xa[:, self.tw_off(k, 3 + to):self.tw_off(k, 3 + to) + tn],
                                  start=(k == 0), stop=(k == KC - 1))
            last_mm[par] = self.pi(mm)
        return self.c_sp

    def phase1(self):
        nc = self.nc
        v, g_ = nc.vector, nc.gpsimd
        T = self.T
        self.wv(nc.sync); self.wa(nc.sync); self.wp(nc.sync)
        self.wv(nc.gpsimd); self.wa(nc.gpsimd); self.wp(nc.gpsimd)
        self.bank_free = self.c_sv
        z0 = self.vi(v.memset(self.a_("hloc", 0, self.DSSM), 0.0))
        self.wv(nc.vector, z0)
        for gi1, g in enumerate(self.wslab_groups(1)):
            done = self.emit_group(g, gsrc=(self.i_w1p1, gi1, 256), halo=True, bank3=True)
            self.wp(nc.vector, done)
            stgs = []
            for i, ncid in enumerate(g):
                stg = "stage" if i == 0 else "stageB"
                stgs.append(stg)
                w = self.nc_cols(ncid)[1]
                mupc = self.ccols[0:w, self.o_mup + ncid:self.o_mup + ncid + 1]
                if ncid != self.NC_DT:
                    self.vi(v.tensor_scalar(self.a_(stg, 0, 3, pn=w), self.ps[3 * i + 2][0:w, 0:3],
                                            mupc, None, op0=ALU.mult))
                for j, (to, tn) in enumerate(self.ttiles):
                    self.vi(v.tensor_scalar(self.a_(stg, 3 + to, tn, pn=w), self.ps[3 * i + j][0:w, 0:tn],
                                            mupc, None, op0=ALU.mult))
            self.bank_free = self.c_sv
            for i, ncid in enumerate(g):
                stg = stgs[i]
                self.wv(nc.vector)
                if ncid == self.NC_DT:
                    self.proc_dt(stg)
                elif ncid == self.NC_B:
                    if "dbg_stB" in self.dbg:
                        self.wv(nc.gpsimd)
                        self.dso(nc.gpsimd.dma_start(out=self.dbg["dbg_stB"].ap(), in_=self.a_(stg, 0, 1028)))
                    self.proc_conv(ncid, self.bfm[:], stg)
                    if "dbg_accB" in self.dbg:
                        self.wv(nc.gpsimd); self.wa(nc.gpsimd)
                        self.dso(nc.gpsimd.dma_start(out=self.dbg["dbg_accB"].ap(), in_=self.a_("acc", 0, self.T)))
                    self.trans_chunks(self.bfm[:], "btok")
                elif ncid == self.NC_C:
                    self.proc_conv(ncid, self.cfm[:], stg)
                else:
                    self.proc_xs(ncid - self.XC, stg)
        if "dbg_hloc" in self.dbg:
            self.wv(nc.gpsimd)
            self.dso(g_.dma_start(out=self.dbg["dbg_hloc"].ap(), in_=self.a_("hloc", 0, self.DSSM)))
        if "dbg_lt" in self.dbg:
            self.wv(nc.gpsimd)
            self.dso(g_.dma_start(out=self.dbg["dbg_lt"].ap(), in_=self.a_("LT", 0, self.NCH * self.NH)))
        if "dbg_wf" in self.dbg:
            self.wv(nc.gpsimd)
            self.dso(g_.dma_start(out=self.dbg["dbg_wf"].ap(), in_=self.a_("wfacT", 0, self.NCH * self.NH)))
        if "dbg_bfm" in self.dbg:
            self.wv(nc.gpsimd)
            self.dso(g_.dma_start(out=self.dbg["dbg_bfm"].ap(), in_=self.bfm[:]))
        if "dbg_xa" in self.dbg:
            self.wv(nc.gpsimd)
            self.dso(g_.dma_start(out=self.dbg["dbg_xa"].ap(), in_=self.xa[:]))

    def trans_chunks(self, src, dstname):
        nc = self.nc
        v, t = nc.vector, nc.tensor
        for cc in range(self.NCH):
            self.wv(nc.tensor)
            pc = self.pi(t.transpose(self.ps[7][0:128, 0:128], src[:, cc * 128:(cc + 1) * 128], self.ident[:]))
            self.wp(nc.vector, pc)
            self.vi(v.tensor_copy(self.a_(dstname, cc * 128, 128), self.ps[7][0:128, 0:128]))

    def proc_conv(self, ncid, dst, stg):
        nc = self.nc
        v, s = nc.vector, nc.scalar
        T = self.T
        cchunk = ncid - self.XC
        base = self.o_convc + cchunk * self.K
        acc = self.a_("acc", 0, T)
        self.vi(v.tensor_scalar(acc, self.a_(stg, 0, T), self.ccols[:, base:base + 1], None, op0=ALU.mult))
        for k in range(1, self.K):
            self.wv(nc.vector)
            self.vi(v.scalar_tensor_tensor(acc, self.a_(stg, k, T), self.ccols[:, base + k:base + k + 1],
                                           acc, op0=ALU.mult, op1=ALU.add))
        self.wv(nc.scalar)
        self.ai(s.activation(dst, acc, ACT.Silu))
        self.wa(nc.vector)
        self.wa(nc.tensor)

    def proc_dt(self, stg):
        nc = self.nc
        v, s, t = nc.vector, nc.scalar, nc.tensor
        NH, NCH, T = self.NH, self.NCH, self.T
        x = self.a_(stg, 3, T, pn=NH)
        self.vi(v.tensor_scalar(x, x, self.dtb_sb[0:NH, :], None, op0=ALU.add))
        ax = self.a_("dt_ax", 0, T, pn=NH)
        mx = self.a_("dt_mx", 0, T, pn=NH)
        self.wv(nc.scalar)
        a1 = self.ai(s.activation(ax, x, ACT.Abs))
        self.wa(nc.scalar, a1)
        a2 = self.ai(s.activation(ax, ax, ACT.Exp, scale=-1.0))
        self.wa(nc.scalar, a2)
        a3 = self.ai(s.activation(ax, ax, ACT.Ln, bias=1.0))
        self.wv(nc.vector)
        self.vi(v.tensor_scalar(mx, x, 0.0, None, op0=ALU.max))
        self.wa(nc.vector, a3)
        self.wv(nc.vector)
        spv = self.vi(v.tensor_tensor(x, ax, mx, op=ALU.add))
        self.wv(nc.tensor, spv)
        for cc in range(NCH):
            pc = self.pi(t.transpose(self.ps[7][0:128, 0:NH], x[0:NH, cc * 128:(cc + 1) * 128],
                                     self.ident[0:NH, 0:NH]))
            self.wp(nc.vector, pc)
            ev = self.vi(v.tensor_copy(self.a_("dtT", cc * NH, NH), self.ps[7][0:128, 0:NH]))
            self.wv(nc.vector)
            av = self.vi(v.tensor_tensor(self.a_("acumT", cc * NH, NH), self.a_("dtT", cc * NH, NH),
                                         self.negarep[:, 0:NH], op=ALU.mult))
            self.wv(nc.tensor, av)
            pc = self.pi(t.matmul(self.ps[7][0:128, 128:128 + NH], self.tril1[:],
                                  self.a_("acumT", cc * NH, NH), start=True, stop=True))
            self.wp(nc.vector, pc)
            lt = self.vi(v.tensor_copy(self.a_("LT", cc * NH, NH), self.ps[7][0:128, 128:128 + NH]))
            self.wv(nc.tensor, lt)
            pc = self.pi(t.transpose(self.ps[7][0:NH, 256:384], self.a_("LT", cc * NH, NH), self.ident[:]))
            self.wp(nc.vector, pc)
            lc = self.vi(v.tensor_copy(self.smalls[0:NH, 16:17], self.ps[7][0:NH, 256 + 127:256 + 128]))
            self.wv(nc.vector)
            if cc == 0:
                self.vi(v.tensor_copy(self.smalls[0:NH, 17:18], self.smalls[0:NH, 16:17]))
            else:
                self.vi(v.tensor_tensor(self.smalls[0:NH, 17:18], self.smalls[0:NH, 17:18],
                                        self.smalls[0:NH, 16:17], op=ALU.add))
            self.wa(nc.scalar, 0); self.wv(nc.scalar)
            ec = self.ai(s.activation(self.smalls[0:NH, 18:19], self.smalls[0:NH, 16:17], ACT.Exp))
            self.wa(nc.vector, ec)
            d1 = self.vi(v.tensor_scalar(self.smalls[0:NH, 64:64 + NH], self.ident[0:NH, 0:NH],
                                         self.smalls[0:NH, 18:19], None, op0=ALU.mult))
            self.wv(nc.tensor, d1)
            pc = self.pi(t.matmul(self.ps[7][0:128, 384:384 + NH], self.ones[0:NH, :],
                                  self.smalls[0:NH, 64:64 + NH], start=True, stop=True))
            self.wp(nc.vector, pc)
            self.vi(v.tensor_copy(self.a_("dARep", cc * NH, NH), self.ps[7][0:128, 384:384 + NH]))
            self.wv(nc.vector)
            d2 = self.vi(v.tensor_scalar(self.smalls[0:NH, 64:64 + NH], self.ident[0:NH, 0:NH],
                                         self.smalls[0:NH, 16:17], None, op0=ALU.mult))
            self.wv(nc.tensor, d2)
            pc = self.pi(t.matmul(self.ps[7][0:128, 384:384 + NH], self.ones[0:NH, :],
                                  self.smalls[0:NH, 64:64 + NH], start=True, stop=True))
            self.wp(nc.vector, pc)
            lr = self.vi(v.tensor_copy(self.a_("lendrep", 0, NH), self.ps[7][0:128, 384:384 + NH]))
            self.wv(nc.vector, lr)
            sv = self.vi(v.tensor_tensor(self.a_("wfacT", cc * NH, NH), self.a_("lendrep", 0, NH),
                                         self.a_("LT", cc * NH, NH), op=ALU.subtract))
            self.wv(nc.scalar, sv)
            ae = self.ai(s.activation(self.a_("wfacT", cc * NH, NH), self.a_("wfacT", cc * NH, NH), ACT.Exp))
            self.wa(nc.vector, ae)
            self.vi(v.tensor_tensor(self.a_("wfacT", cc * NH, NH), self.a_("wfacT", cc * NH, NH),
                                    self.a_("dtT", cc * NH, NH), op=ALU.mult))
            self.wv(nc.tensor)
        self.wv(nc.scalar)
        af = self.ai(s.activation(self.a_("acumT", 0, NCH * NH), self.a_("LT", 0, NCH * NH), ACT.Exp))
        self.wa(nc.vector, af)
        self.wa(nc.tensor, af)

    def proc_xs(self, kx, stg):
        nc = self.nc
        v, s, y, g_, t = nc.vector, nc.scalar, nc.sync, nc.gpsimd, nc.tensor
        NH, NCH, T, HP, HD = self.NH, self.NCH, self.T, self.HP, self.HD
        xsil = self.a_("xsil", 0, T)
        base = self.o_convc + kx * self.K
        acc = self.a_("acc", 0, T)
        self.vi(v.tensor_scalar(acc, self.a_(stg, 0, T), self.ccols[:, base:base + 1], None, op0=ALU.mult))
        for k in range(1, self.K):
            self.wv(nc.vector)
            self.vi(v.scalar_tensor_tensor(acc, self.a_(stg, k, T), self.ccols[:, base + k:base + k + 1],
                                           acc, op0=ALU.mult, op1=ALU.add))
        self.wv(nc.scalar)
        sil = self.ai(s.activation(xsil, acc, ACT.Silu))
        self.wa(nc.tensor, sil)
        self.wa(nc.vector, sil)
        if not hasattr(self, "xs_pv"):
            self.xs_pv = [0, 0]; self.xs_sv = [0, 0]; self.xs_dx = [0, 0]; self.xs_ds = [0, 0]
            self.xs_mm = [0, 0]
        for cc in range(NCH):
            par = cc % 2
            pb = self.ps[6 + par]
            piece = self.a_("piece", par * 128, 128)
            xw = self.a_("xw", par * 128, 128)
            sst = self.a_("sstage", par * 128, 128)
            self.wv(nc.tensor, self.xs_pv[par])
            pc = self.pi(t.transpose(pb[0:128, 0:128], xsil[:, cc * 128:(cc + 1) * 128], self.ident[:]))
            self.wp(nc.vector, pc)
            self.wso(nc.vector, self.xs_dx[par])
            pv = self.vi(v.tensor_copy(piece, pb[0:128, 0:128]))
            self.wv(nc.gpsimd, pv)
            self.xs_dx[par] = self.dso(g_.dma_start(
                out=self.d_xstok[cc * 128:(cc + 1) * 128, kx * 128:(kx + 1) * 128], in_=piece))
            wf = self.a_("wfacT", cc * NH + kx * HP, HP)
            wfv = self.bcastv(wf, HP, HD)
            self.wp(nc.vector, self.xs_mm[par])
            xv = self.vi(v.tensor_tensor(xw.rearrange("p (h d) -> p h d", h=HP),
                                         piece.rearrange("p (h d) -> p h d", h=HP), wfv, op=ALU.mult))
            self.xs_pv[par] = xv
            self.wv(nc.tensor, self.xs_sv[par])
            self.wv(nc.tensor, xv)
            pc2 = self.pi(t.matmul(pb[0:128, 128:256], self.a_("btok", cc * 128, 128), xw,
                                   start=True, stop=True))
            self.xs_mm[par] = pc2
            self.wp(nc.vector, pc2)
            for h in range(HP):
                col = kx * 128 + h * HD
                self.vi(v.scalar_tensor_tensor(self.a_("hloc", col, HD), self.a_("hloc", col, HD),
                                               self.a_("dARep", cc * NH + kx * HP + h, 1),
                                               pb[0:128, 128 + h * HD:128 + (h + 1) * HD],
                                               op0=ALU.mult, op1=ALU.add))
            self.wso(nc.vector, self.xs_ds[par])
            sv = self.vi(v.tensor_copy(sst, pb[0:128, 128:256]))
            self.xs_sv[par] = sv
            self.wv(nc.gpsimd, sv)
            self.xs_ds[par] = self.dso(g_.dma_start(
                out=self.d_s[cc * 128:(cc + 1) * 128, kx * 128:(kx + 1) * 128], in_=sst))

    def collective(self):
        nc = self.nc
        g_ = nc.gpsimd
        v = nc.vector
        self.wv(nc.vector)
        self.wso(nc.vector)
        zr = self.vi(v.memset(self.AR[0:1, 4096:4096 + self.DSSM], 0.0))
        self.wv(nc.gpsimd)
        self.dso(g_.dma_start(out=self.d_ccin[0:128, :], in_=self.a_("hloc", 0, self.DSSM)))
        self.dso(g_.dma_start(out=self.d_ccin[128:129, :], in_=self.AR[0:1, 4096:4096 + self.DSSM]))
        self.wso(nc.gpsimd)
        self.dso(g_.dma_start(out=self.d_ccin[128:129, 0:self.NH], in_=self.smalls[0:self.NH, 17:18]))
        self.wso(nc.gpsimd)
        g_.collective_compute(
            "AllGather", ALU.bypass,
            replica_groups=[list(range(self.NCORES))],
            ins=[self.d_ccin.ap()],
            outs=[self.d_ccout.ap()],
        ).then_inc(self.s_cc, 1)
        self.c_cc += 1

    def phase2(self):
        nc = self.nc
        v, s, y, g_, t = nc.vector, nc.scalar, nc.sync, nc.gpsimd, nc.tensor
        T, TC, NL = self.T, self.TC, self.NL
        self.wv(nc.sync); self.wa(nc.sync); self.wp(nc.sync)
        ld = self.dsi(y.dma_start(out=self.a_("wlgall", 0, self.KC * NL), in_=self.i_wcat.ap()))
        self.wsi(nc.vector, ld)
        for k in range(self.KC):
            self.vi(v.tensor_scalar(self.a_("wlgall", k * NL, NL), self.a_("wlgall", k * NL, NL),
                                    self.rlnw[:, k:k + 1], None, op0=ALU.mult))
        self.wlg_ready = self.c_sv
        groups = self.wslab_groups(2)
        for gi, g in enumerate(groups):
            done = self.emit_group(g, gsrc=(self.i_w1p2, gi, 384), with_logits=(gi == 0),
                                   halo=False, bank3=False)
            for i, ncid in enumerate(g):
                self.wp(nc.vector, done)
                self.wv(nc.vector)
                self.wa(nc.vector)
                mupc = self.ccols[:, self.o_mup + ncid:self.o_mup + ncid + 1]
                for j, (to, tn) in enumerate(self.ttiles):
                    self.vi(v.tensor_scalar(self.a_("z_st", to, tn), self.ps[2 * i + j][0:128, 0:tn],
                                            mupc, None, op0=ALU.mult))
                self.wv(nc.scalar)
                sz = self.ai(s.activation(self.a_("sz_st", 0, T), self.a_("z_st", 0, T), ACT.Silu))
                self.wa(nc.gpsimd, sz)
                self.dso(g_.dma_start(out=self.d_sz[ncid * 128:(ncid + 1) * 128, :], in_=self.a_("sz_st", 0, T)))
                self.wso(nc.scalar)
            if gi == 0:
                self.wp(nc.vector, done)
                for j, (to, tn) in enumerate(self.ttiles):
                    self.vi(v.tensor_copy(self.a_("lg_st", to, tn, pn=NL), self.ps[6 + j][0:NL, 0:tn]))
                self.logits_done = self.c_sv
            self.bank_free = self.c_sv
        self.wso(nc.vector)
        for ct in range(TC):
            self.wv(nc.tensor)
            pc = self.pi(t.transpose(self.ps[0][0:128, 0:NL], self.a_("lg_st", ct * 128, 128, pn=NL),
                                     self.ident[0:NL, 0:NL]))
            self.wp(nc.vector, pc)
            sf = self.a_("statc", ct * 4 + 3, 1)
            sv = self.vi(v.scalar_tensor_tensor(self.s_("logT", ct * NL, NL), self.ps[0][0:128, 0:NL], sf,
                                                self.base_rep[:], op0=ALU.mult, op1=ALU.add))
            self.wv(nc.tensor, sv)
        if "dbg_logt" in self.dbg:
            self.wv(nc.gpsimd)
            self.dso(g_.dma_start(out=self.dbg["dbg_logt"].ap(), in_=self.s_("logT", 0, TC * NL)))
        self.sink_gen = self.sinkhorn_steps()

    def sinkhorn_steps(self):
        nc = self.nc
        v, s = nc.vector, nc.scalar
        TC, NL, NSTR = self.TC, self.NL, self.NSTR
        NN = NSTR * NSTR
        P = lambda: self.s_("P", 0, TC * NN)
        pv = lambda: P().rearrange("p (c i j) -> p c i j", i=NSTR, j=NSTR)
        pvt = lambda: P().rearrange("p (c i j) -> p c j i", i=NSTR, j=NSTR)
        rs = lambda: self.s_("rs", 0, TC * NSTR)
        rsb = lambda: rs().rearrange("p (c n) -> p c n", n=NSTR)
        _r = rsb()
        rsv = bass.AP(_r.tensor, _r.offset, [list(_r.ap[0]), list(_r.ap[1]), list(_r.ap[2]), [0, NSTR]])
        csv = bass.AP(_r.tensor, _r.offset, [list(_r.ap[0]), list(_r.ap[1]), [0, NSTR], list(_r.ap[2])])
        tau_i = 1.0 / self.TAU
        for ct in range(TC):
            self.wv(nc.vector)
            self.vi(v.tensor_scalar(self.s_("P", ct * NN, NN), self.s_("logT", ct * NL, NN), tau_i, None,
                                    op0=ALU.mult))
        self.wv(nc.vector)
        self.vi(v.tensor_reduce(rsb(), pv(), axis=AX.X, op=ALU.max))
        self.wv(nc.vector)
        sb1 = self.vi(v.tensor_tensor(pv(), pv(), rsv, op=ALU.subtract))
        self.wv(nc.scalar, sb1)
        ex = self.ai(s.activation(P(), P(), ACT.Exp))
        self.wa(nc.vector, ex)
        yield
        for it in range(self.SINK_IT):
            self.wv(nc.vector)
            self.vi(v.tensor_reduce(rsb(), pv(), axis=AX.X, op=ALU.add))
            self.wv(nc.vector)
            self.vi(v.reciprocal(rs(), rs()))
            self.wv(nc.vector)
            self.vi(v.tensor_tensor(pv(), pv(), rsv, op=ALU.mult))
            self.wv(nc.vector)
            self.vi(v.tensor_reduce(rsb(), pvt(), axis=AX.X, op=ALU.add))
            self.wv(nc.vector)
            self.vi(v.reciprocal(rs(), rs()))
            self.wv(nc.vector)
            self.vi(v.tensor_tensor(pv(), pv(), csv, op=ALU.mult))
            if it % 8 == 7:
                yield
        for ct in range(TC):
            self.wv(nc.vector)
            self.vi(v.scalar_tensor_tensor(self.s_("P", ct * NN, NN), self.s_("P", ct * NN, NN), self.gcol[:],
                                           self.hbl[:], op0=ALU.mult, op1=ALU.add))
        if "dbg_p" in self.dbg:
            self.wv(nc.gpsimd)
            self.dso(self.nc.gpsimd.dma_start(out=self.dbg["dbg_p"].ap(), in_=self.s_("P", 0, TC * NN)))
        r1 = lambda: self.s_("rs", 0, TC)
        r1b = lambda: r1().rearrange("p (c o) -> p c o", o=1)
        _q = r1b()
        r1v = bass.AP(_q.tensor, _q.offset, [list(_q.ap[0]), list(_q.ap[1]), [0, NSTR]])
        yield
        for nm, off in (("hp", NN), ("ho", NN + NSTR)):
            for ct in range(TC):
                self.wv(nc.vector)
                self.vi(v.tensor_copy(self.s_(nm, ct * NSTR, NSTR), self.s_("logT", ct * NL + off, NSTR)))
            dv = self.s_(nm, 0, TC * NSTR).rearrange("p (c n) -> p c n", n=NSTR)
            self.wv(nc.vector)
            self.vi(v.tensor_reduce(r1b(), dv, axis=AX.X, op=ALU.max))
            self.wv(nc.vector)
            self.vi(v.tensor_tensor(dv, dv, r1v, op=ALU.subtract))
            self.wv(nc.scalar)
            e2 = self.ai(s.activation(self.s_(nm, 0, TC * NSTR), self.s_(nm, 0, TC * NSTR), ACT.Exp))
            self.wa(nc.vector, e2)
            self.vi(v.tensor_reduce(r1b(), dv, axis=AX.X, op=ALU.add))
            self.wv(nc.vector)
            self.vi(v.reciprocal(r1(), r1()))
            self.wv(nc.vector)
            self.vi(v.tensor_tensor(dv, dv, r1v, op=ALU.mult))

    def ffn_mix(self):
        nc = self.nc
        v, s, y, g_, t = nc.vector, nc.scalar, nc.sync, nc.gpsimd, nc.tensor
        T, TC, D, NSTR, CDIM, CC = self.T, self.TC, self.D, self.NSTR, self.CDIM, self.CC
        NN = NSTR * NSTR
        for ct in range(TC):
            self.wv(nc.sync); self.wp(nc.sync)
            ld = self.dsi(y.dma_start(out=self.a_("fx", 0, D), in_=self.i_x[3 + ct * 128:3 + (ct + 1) * 128, :]))
            self.wsi(nc.vector, ld)
            xt = self.a_("fx", 0, D)
            xn = self.a_("fxn", 0, D)
            m = self.a_("statc", ct * 4 + 0, 1)
            r = self.a_("statc", ct * 4 + 1, 1)
            xv = self.vi(v.tensor_scalar(xn, xt, m, r, op0=ALU.subtract, op1=ALU.mult))
            xp = self.a_("xp", 0, CDIM)
            self.wv(nc.vector, xv)
            self.vi(v.tensor_scalar(xp, xn[:, 0:CDIM], self.s_("hp", ct * NSTR, 1), None, op0=ALU.mult))
            for n in range(1, NSTR):
                self.wv(nc.vector)
                self.vi(v.scalar_tensor_tensor(xp, xn[:, n * CDIM:(n + 1) * CDIM],
                                               self.s_("hp", ct * NSTR + n, 1), xp, op0=ALU.mult, op1=ALU.add))
            self.wv(nc.tensor)
            for cc_ in range(CC):
                w = min(128, CDIM - cc_ * 128)
                pc = self.pi(t.transpose(self.ps[1][0:w, 0:128], xp[:, cc_ * 128:cc_ * 128 + w], self.ident[:]))
                self.wp(nc.vector, pc)
                ev = self.vi(v.tensor_copy(self.a_("xpt", cc_ * 128, 128, pn=w), self.ps[1][0:w, 0:128]))
                self.wv(nc.tensor, ev)
            nby = (CDIM + 511) // 512
            if not hasattr(self, "f_wlast"):
                self.f_wlast = [0, 0]
            for nb in range(nby):
                nn = min(512, CDIM - nb * 512)
                mm = None
                for cc_ in range(CC):
                    w = min(128, CDIM - cc_ * 128)
                    par = cc_ % 2
                    wreg = self.a_("wbr", 0, nn, pn=w) if par == 0 else self.a_("stageB", 0, nn, pn=w)
                    self.wv(nc.sync); self.wp(nc.sync, self.f_wlast[par])
                    wl = self.dsi(y.dma_start(out=wreg,
                                              in_=self.i_wbrb[(nb * CC + cc_) * 128:(nb * CC + cc_) * 128 + w,
                                                              0:nn]))
                    self.wsi(nc.tensor, wl)
                    if cc_ == 0:
                        self.wv(nc.tensor)
                    mm = self.pi(t.matmul(self.ps[2][0:128, 0:nn], self.a_("xpt", cc_ * 128, 128, pn=w),
                                  wreg, start=(cc_ == 0), stop=(cc_ == CC - 1)))
                    self.f_wlast[par] = mm
                pc = mm
                self.wp(nc.vector, pc)
                self.vi(v.tensor_copy(self.a_("ybr", nb * 512, nn), self.ps[2][0:128, 0:nn]))
            ybr = self.a_("ybr", 0, CDIM)
            for i in range(NSTR):
                acc = self.a_("xp", 0, CDIM)
                self.wv(nc.vector)
                self.vi(v.scalar_tensor_tensor(acc, xn[:, 0:CDIM], self.s_("P", ct * NN + i * NSTR, 1),
                                               xt[:, i * CDIM:(i + 1) * CDIM], op0=ALU.mult, op1=ALU.add))
                for j in range(1, NSTR):
                    self.wv(nc.vector)
                    self.vi(v.scalar_tensor_tensor(acc, xn[:, j * CDIM:(j + 1) * CDIM],
                                                   self.s_("P", ct * NN + i * NSTR + j, 1), acc,
                                                   op0=ALU.mult, op1=ALU.add))
                self.wv(nc.vector)
                fv = self.vi(v.scalar_tensor_tensor(acc, ybr, self.s_("ho", ct * NSTR + i, 1), acc,
                                                    op0=ALU.mult, op1=ALU.add))
                self.wv(nc.gpsimd, fv)
                self.dso(g_.dma_start(out=self.d_ffnx[ct * 128:(ct + 1) * 128, i * CDIM:(i + 1) * CDIM], in_=acc))
                self.wso(nc.vector)

    def phase3(self):
        nc = self.nc
        v, s, y, g_, t = nc.vector, nc.scalar, nc.sync, nc.gpsimd, nc.tensor
        NH, NCH, XC, HD, HP = self.NH, self.NCH, self.XC, self.HD, self.HP
        for e in (nc.tensor, nc.vector, nc.scalar, nc.sync, nc.gpsimd):
            e.wait_ge(self.s_cc, self.c_cc)
        self.wv(nc.sync); self.wp(nc.sync); self.wa(nc.sync)
        self.wso(nc.sync); self.wso(nc.scalar)
        NC_ = self.NCORES
        lda = self.smalls[0:NC_, 0:NH]
        src = bass.AP(self.d_ccout.ap().tensor, 128 * self.DSSM, [[129 * self.DSSM, NC_], [1, NH]])
        ld = self.dsi(y.dma_start(out=lda, in_=src))
        self.wsi(nc.tensor, ld)
        pc = self.pi(t.matmul(self.ps[0][0:NC_, 0:NH], self.rank_sb[0:NC_, 0:NC_], lda, start=True, stop=True))
        self.wp(nc.scalar, pc)
        fx = self.ai(s.activation(self.smalls[0:NC_, 64:64 + NH], self.ps[0][0:NC_, 0:NH], ACT.Exp))
        self.wa(nc.vector, fx)
        fz = self.vi(v.tensor_scalar(self.smalls[0:NC_, 64:64 + NH], self.smalls[0:NC_, 64:64 + NH],
                                     self.zmask_sb[0:NC_, :], None, op0=ALU.mult))
        hg = self.a_("hg", 0, self.DSSM)
        self.vi(v.memset(hg, 0.0))
        self.wv(nc.tensor, fz)
        pft = self.pi(t.transpose(self.ps[1][0:NH, 0:NC_], self.smalls[0:NC_, 64:64 + NH],
                                  self.ident[0:NC_, 0:NC_]))
        self.wp(nc.vector, pft)
        ftv = self.vi(v.tensor_copy(self.smalls[0:NH, 20:20 + NC_], self.ps[1][0:NH, 0:NC_]))
        for j in range(self.NCORES):
            self.wv(nc.vector)
            dgv = self.vi(v.tensor_scalar(self.smalls[0:NH, 64:64 + NH], self.ident[0:NH, 0:NH],
                                          self.smalls[0:NH, 20 + j:21 + j], None, op0=ALU.mult))
            self.wv(nc.tensor)
            pc = self.pi(t.matmul(self.ps[0][0:128, 0:NH], self.ones[0:NH, :],
                                  self.smalls[0:NH, 64:64 + NH], start=True, stop=True))
            self.wv(nc.scalar); self.wp(nc.scalar)
            ldS = self.dsd(nc.scalar.dma_start(out=self.a_("sread", 0, self.DSSM),
                                               in_=self.d_ccout[j * 129:j * 129 + 128, :]))
            self.wsd(nc.vector, ldS)
            self.wp(nc.vector, pc)
            frv = self.bcastv(bass.AP(self.ps[0].ap().tensor, self.ps[0].ap().offset,
                                      [[512, 128], [1, NH]]), NH, HD)
            self.wv(nc.vector)
            self.vi(v.tensor_tensor(self.a_("sread", 0, self.DSSM).rearrange("p (h d) -> p h d", h=NH),
                                    self.a_("sread", 0, self.DSSM).rearrange("p (h d) -> p h d", h=NH),
                                    frv, op=ALU.mult))
            self.wv(nc.vector)
            self.vi(v.tensor_tensor(hg, hg, self.a_("sread", 0, self.DSSM), op=ALU.add))
        if "dbg_hg" in self.dbg:
            self.wv(nc.gpsimd)
            self.dso(g_.dma_start(out=self.dbg["dbg_hg"].ap(), in_=hg))
        for cc in range(NCH):
            next(self.sink_gen, None)
            if cc > 0:
                dv = self.a_("dARep", (cc - 1) * NH, NH)
                dvv = self.bcastv(dv, NH, HD)
                self.wv(nc.vector)
                self.vi(v.tensor_tensor(hg.rearrange("p (h d) -> p h d", h=NH),
                                        hg.rearrange("p (h d) -> p h d", h=NH), dvv, op=ALU.mult))
                self.wv(nc.scalar); self.wp(nc.scalar)
                ldS = self.dsd(nc.scalar.dma_start(out=self.a_("sread", 0, self.DSSM),
                                                   in_=self.d_s[(cc - 1) * 128:cc * 128, :]))
                self.wsd(nc.vector, ldS)
                self.wv(nc.vector)
                self.vi(v.tensor_tensor(hg, hg, self.a_("sread", 0, self.DSSM), op=ALU.add))
            self.wv(nc.sync); self.wp(nc.sync)
            ldx = self.dsi(y.dma_start(out=self.a_("sread", 0, self.DSSM),
                                       in_=self.d_xstok[cc * 128:(cc + 1) * 128, :]))
            xstok = self.a_("sread", 0, self.DSSM)
            pcg = self.pi(t.matmul(self.ps[1][0:128, 0:128], self.bfm[:, cc * 128:(cc + 1) * 128],
                                   self.cfm[:, cc * 128:(cc + 1) * 128], start=True, stop=True))
            self.wp(nc.vector, pcg)
            gv = self.vi(v.tensor_copy(self.a_("gst", 0, 128), self.ps[1][0:128, 0:128]))
            self.wv(nc.vector)
            nv = self.vi(v.tensor_scalar(self.a_("nlt", 0, NH), self.a_("LT", cc * NH, NH), -1.0, None,
                                         op0=ALU.mult))
            self.wsi(nc.vector, ldx)
            self.wv(nc.tensor, nv)
            if not hasattr(self, "p3_ae"):
                self.p3_ae = [0, 0]; self.p3_mv = [0, 0]; self.p3_mm = [0, 0]
                self.p3_evac = 0; self.p3_ytr = [0, 0]; self.p3_ydma = [0, 0]
            mvh = [0] * NH

            def emit_T1EM(h):
                par = h % 2
                pT = self.ps[0] if par == 0 else self.ps[3]
                _c = self.a_("LT", cc * NH + h, 1)
                ltb = bass.AP(_c.tensor, _c.offset, [list(_c.ap[0]), [0, 128]])
                self.wa(nc.tensor, self.p3_ae[par])
                t.matmul(pT[0:128, 0:128], ltb, self.ident[:], start=True, stop=False)
                pc = self.pi(t.matmul(pT[0:128, 0:128], self.ident[:], self.trineg[:],
                                      start=False, stop=True))
                est = self.a_("est", par * 128, 128)
                mst = self.a_("mst", par * 128, 128)
                self.wp(nc.scalar, pc)
                self.wv(nc.scalar, max(self.p3_mv[par], nv))
                ae = self.ai(s.activation(est, pT[0:128, 0:128], ACT.Exp,
                                          bias=self.a_("nlt", h, 1)))
                self.p3_ae[par] = ae
                self.wa(nc.vector, ae)
                self.wp(nc.vector, self.p3_mm[par])
                mv = self.vi(v.scalar_tensor_tensor(mst, est, self.a_("dtT", cc * NH + h, 1),
                                                    self.a_("gst", 0, 128), op0=ALU.mult, op1=ALU.mult))
                self.p3_mv[par] = mv
                mvh[h] = mv

            emit_T1EM(0)
            for h in range(NH):
                if h + 1 < NH:
                    emit_T1EM(h + 1)
                hh = h % 8
                if hh == 0:
                    self.wv(nc.tensor, self.p3_evac)
                self.wv(nc.tensor, mvh[h])
                mst = self.a_("mst", (h % 2) * 128, 128)
                t.matmul(self.ps[4][0:128, hh * 64:hh * 64 + 64], mst, xstok[:, h * 64:h * 64 + 64],
                         start=True, stop=True)
                self.p3_mm[h % 2] = self.pi(t.matmul(self.ps[5][0:128, hh * 64:hh * 64 + 64],
                                 self.cfm[:, cc * 128:(cc + 1) * 128], hg[:, h * 64:h * 64 + 64],
                                 start=True, stop=True))
                if hh == 7:
                    hgp = h // 8
                    self.wp(nc.vector)
                    for hh2 in range(8):
                        h2 = hgp * 8 + hh2
                        self.vi(v.tensor_scalar(self.a_("yst3", h2 * 64, 64),
                                                self.ps[5][0:128, hh2 * 64:hh2 * 64 + 64],
                                                self.a_("acumT", cc * NH + h2, 1), None, op0=ALU.mult))
                        self.vi(v.tensor_tensor(self.a_("yst3", h2 * 64, 64), self.a_("yst3", h2 * 64, 64),
                                                self.ps[4][0:128, hh2 * 64:hh2 * 64 + 64], op=ALU.add))
                        self.vi(v.scalar_tensor_tensor(self.a_("yst3", h2 * 64, 64),
                                                       xstok[:, h2 * 64:h2 * 64 + 64],
                                                       self.drep[:, h2:h2 + 1], self.a_("yst3", h2 * 64, 64),
                                                       op0=ALU.mult, op1=ALU.add))
                    self.p3_evac = self.c_sv
            if "dbg_y" in self.dbg and cc == 0:
                self.wv(nc.gpsimd)
                self.dso(g_.dma_start(out=self.dbg["dbg_y"].ap(), in_=self.a_("yst3", 0, self.DSSM)))
            for kx in range(XC):
                self.wv(nc.tensor)
                pc = self.pi(t.transpose(self.ps[2][0:128, 0:128], self.a_("yst3", kx * 128, 128), self.ident[:]))
                self.wp(nc.vector, pc)
                yv = self.vi(v.tensor_copy(self.a_("ytr", 0, 128), self.ps[2][0:128, 0:128]))
                self.wv(nc.gpsimd, yv)
                self.dso(g_.dma_start(out=self.d_yfm[kx * 128:(kx + 1) * 128, cc * 128:(cc + 1) * 128],
                                      in_=self.a_("ytr", 0, 128)))
                self.wso(nc.vector)
        for _ in self.sink_gen:
            pass

    def phase4(self):
        nc = self.nc
        v, s, y, g_, t = nc.vector, nc.scalar, nc.sync, nc.gpsimd, nc.tensor
        XC, T, TW = self.XC, self.T, self.TW
        self.wso(nc.sync); self.wso(nc.scalar)
        for kx in range(XC):
            self.wv(nc.sync); self.wa(nc.sync)
            yl = self.dsi(y.dma_start(out=self.a_("y4", 0, T), in_=self.d_yfm[kx * 128:(kx + 1) * 128, :]))
            self.wv(nc.scalar); self.wp(nc.scalar)
            zl = self.dsd(nc.scalar.dma_start(out=self.a_("sz4", 0, T), in_=self.d_sz[kx * 128:(kx + 1) * 128, :]))
            self.wsi(nc.vector, yl)
            self.wsd(nc.vector, zl)
            gv = self.vi(v.tensor_tensor(self.a_("yg4", 0, T), self.a_("y4", 0, T), self.a_("sz4", 0, T),
                                         op=ALU.mult))
            self.wv(nc.scalar, gv)
            sqv = self.ai(s.activation(self.a_("sq4", 0, T), self.a_("yg4", 0, T), ACT.Square))
            self.wa(nc.tensor, sqv)
            for j, (to, tn) in enumerate(self.ttiles):
                self.pi(t.matmul(self.ps[2 + j][0:1, 0:tn], self.ones[0:128, 0:1],
                                 self.a_("sq4", to, tn), start=(kx == 0), stop=(kx == XC - 1)))
        self.wp(nc.vector)
        row = self.a_("row4", 0, T, pn=1)
        for j, (to, tn) in enumerate(self.ttiles):
            self.vi(v.tensor_copy(self.a_("row4", to, tn, pn=1), self.ps[2 + j][0:1, 0:tn]))
        self.wv(nc.vector)
        self.vi(v.tensor_scalar(row, row, 1.0 / self.DSSM, self.EPS_A, op0=ALU.mult, op1=ALU.add))
        self.wv(nc.scalar)
        sa = self.ai(s.activation(row, row, ACT.Sqrt))
        self.wa(nc.vector, sa)
        rv = self.vi(v.reciprocal(row, row))
        self.wv(nc.tensor, rv)
        for j, (to, tn) in enumerate(self.ttiles):
            pc = self.pi(t.matmul(self.ps[0][0:128, 0:tn], self.ones[0:1, :], self.a_("row4", to, tn, pn=1),
                                  start=True, stop=True))
            self.wp(nc.vector, pc)
            ev = self.vi(v.tensor_copy(self.a_("rrep", to, tn), self.ps[0][0:128, 0:tn]))
            self.wv(nc.tensor, ev)
        for kx in range(XC):
            self.wv(nc.sync)
            yl = self.dsi(y.dma_start(out=self.a_("y4", 0, T), in_=self.d_yfm[kx * 128:(kx + 1) * 128, :]))
            self.wv(nc.scalar); self.wp(nc.scalar)
            zl = self.dsd(nc.scalar.dma_start(out=self.a_("sz4", 0, T), in_=self.d_sz[kx * 128:(kx + 1) * 128, :]))
            self.wsi(nc.vector, yl)
            self.wsd(nc.vector, zl)
            gv = self.vi(v.tensor_tensor(self.a_("yg4", 0, T), self.a_("y4", 0, T), self.a_("sz4", 0, T),
                                         op=ALU.mult))
            self.wv(nc.vector, gv)
            nw = self.ccols[:, self.o_nrmw + kx:self.o_nrmw + kx + 1]
            self.vi(v.scalar_tensor_tensor(self.xa[:, kx * TW + 3:kx * TW + 3 + T], self.a_("yg4", 0, T),
                                           nw, self.a_("rrep", 0, T), op0=ALU.mult, op1=ALU.mult))
        if "dbg_yn" in self.dbg:
            self.wv(nc.gpsimd)
            self.dso(g_.dma_start(out=self.dbg["dbg_yn"].ap(), in_=self.xa[:, 0:XC * TW]))

    def phase5(self):
        nc = self.nc
        v, s, y, g_, t = nc.vector, nc.scalar, nc.sync, nc.gpsimd, nc.tensor
        XC, T, D, TC = self.XC, self.T, self.D, self.TC
        nwid = min(512, D)
        NB = (D + nwid - 1) // nwid
        assert TC <= 8
        self.wv(nc.sync); self.wa(nc.sync); self.wp(nc.sync)
        self.wso(nc.sync); self.wso(nc.scalar)
        last = [0, 0]
        for nb in range(NB):
            for k in range(XC):
                par = k % 2
                self.wp(nc.sync, last[par])
                r0 = (nb * XC + k) * 128
                self.dwi(par, y.dma_start(out=self.a_("w5", par * 512, nwid),
                                           in_=self.i_w2b[r0:r0 + 128, 0:nwid]))
                self.wwi(nc.tensor, par)
                if k == 0:
                    self.wv(nc.tensor)
                mm = None
                for ct in range(TC):
                    mm = t.matmul(self.ps[ct][0:128, 0:nwid], self.xa[0:128, self.tw_off(k, 3 + ct * 128):
                                                                      self.tw_off(k, 3 + ct * 128) + 128],
                                  self.a_("w5", par * 512, nwid), start=(k == 0), stop=(k == XC - 1))
                last[par] = self.pi(mm)
            if not hasattr(self, "p5_ov"):
                self.p5_ov = [0] * TC; self.p5_od = [0, 0]
            for ct in range(TC):
                self.wv(nc.scalar); self.wp(nc.scalar)
                fl = self.dsd(nc.scalar.dma_start(out=self.a_("ffnx5", (ct % 2) * 512, nwid),
                                                  in_=self.d_ffnx[ct * 128:(ct + 1) * 128,
                                                                  nb * nwid:nb * nwid + nwid]))
                self.wsd(nc.vector, fl)
                self.wp(nc.vector, last[(XC - 1) % 2])
                self.wso(nc.vector, self.p5_od[ct % 2])
                ov = self.vi(v.tensor_tensor(self.a_("ost5", (ct % 2) * 512, nwid), self.ps[ct][0:128, 0:nwid],
                                             self.a_("ffnx5", (ct % 2) * 512, nwid), op=ALU.add))
                self.p5_ov[ct] = ov
                self.wv(nc.gpsimd, ov)
                self.p5_od[ct % 2] = self.dso(g_.dma_start(
                    out=self.o_out[ct * 128:(ct + 1) * 128, nb * nwid:nb * nwid + nwid],
                    in_=self.a_("ost5", (ct % 2) * 512, nwid)))


def kernel(**inputs):
    cfg = real_cfg()
    b = Build(cfg)
    maps = b.host_inputs(inputs)
    nc = b.build(dbg=False)
    res = run_bass_kernel_spmd(nc, maps, core_ids=list(range(cfg["NCORES"])))
    outs = [r["out"] for r in res.results]
    full = np.concatenate(outs, 0).reshape(1, cfg["NCORES"] * cfg["T"], cfg["D"])
    return np.ascontiguousarray(full.astype(np.float32))


# version 19
# speedup vs baseline: 1.2799x; 1.0036x over previous
import numpy as np
import concourse.bass as bass
import concourse.mybir as mybir
from concourse.bass_utils import run_bass_kernel_spmd

F32 = mybir.dt.float32
BF16 = mybir.dt.bfloat16
ALU = mybir.AluOpType
ACT = mybir.ActivationFunctionType
AX = mybir.AxisListType

NEG_BIG = -1.0e30


def real_cfg():
    return dict(
        NCORES=8, D=4096, DSSM=4096, S=128, NH=64, HD=64, K=4, T=1024, Q=128,
        NSTR=4, CDIM=1024, SINK_IT=50, TAU=0.1,
        EPS_LN=1e-5, EPS_A=1e-5, EPS_F=1.1920929e-07,
        MM="fp32",
    )


class Build:
    def __init__(self, cfg):
        self.cfg = cfg
        for k_, v_ in cfg.items():
            setattr(self, k_, v_)
        assert self.S == 128 and self.HD == 64 and self.K == 4 and self.Q == 128
        self.KC = self.D // 128
        self.XC = self.DSSM // 128
        self.NCH = self.T // self.Q
        self.TC = self.T // 128
        self.CONV = self.DSSM + 2 * self.S
        self.PROJ = self.DSSM + self.CONV + self.NH
        self.TW = self.T + 4
        self.NL = self.NSTR * self.NSTR + 2 * self.NSTR
        self.CC = (self.CDIM + 127) // 128
        self.HP = 128 // self.HD
        self.NC_B = 2 * self.XC
        self.NC_C = 2 * self.XC + 1
        self.NC_DT = 2 * self.XC + 2
        self.NCHK = 2 * self.XC + 3
        self.ttiles = []
        o = 0
        while o < self.T:
            n = min(512, self.T - o)
            self.ttiles.append((o, n))
            o += n
        self.NTT = len(self.ttiles)
        assert self.NTT <= 2
        A = {}
        A["p0_x"] = 0; A["p0_xa"] = 4096
        A["stage"] = 0
        A["xsil"] = 1056
        A["acc"] = 2080
        A["btok"] = 3104
        A["piece"] = 7424; A["xw"] = 7680; A["sstage"] = 7936
        A["stageB"] = 12288
        A["wlgall"] = 3264
        A["dt_ax"] = 4608; A["dt_mx"] = 5632
        A["wraw"] = 6656
        A["hloc"] = 8192
        A["z_st"] = 0; A["sz_st"] = 1088; A["lg_st"] = 2176; A["wlg"] = 3264
        A["logT"] = 8192; A["P"] = 8448; A["rs"] = 8640; A["hp"] = 8704; A["ho"] = 8768
        A["fx"] = 0; A["fxn"] = 4096
        A["xp"] = 8832; A["xpt"] = 9856; A["wbr"] = 10880; A["ybr"] = 11456
        A["hg"] = 0; A["sread"] = 4096; A["yst3"] = 8192
        A["gst"] = 12288; A["nlt"] = 12416; A["lrow"] = 12480
        A["est"] = 12736; A["mst"] = 12992; A["ytr"] = 13248
        A["ldiag"] = 12608
        A["y4"] = 0; A["sz4"] = 4096; A["yg4"] = 8192; A["sq4"] = 9216
        A["row4"] = 10304; A["rrep"] = 11392
        A["w5"] = 0; A["ffnx5"] = 2048; A["ost5"] = 3072
        A["LT"] = 13440; A["dtT"] = 13952; A["acumT"] = 14464
        A["wfacT"] = 14976; A["dARep"] = 15488
        A["statc"] = 16000; A["lendrep"] = 16128
        self.A = A

    def host_inputs(self, inputs):
        x = np.ascontiguousarray(np.asarray(inputs["L0_attn_hidden_states"], np.float32).reshape(-1, self.D))
        L = x.shape[0]
        assert L == self.NCORES * self.T, (L, self.NCORES, self.T)
        wcat = np.ascontiguousarray(
            np.concatenate([np.asarray(inputs["L0_ffn_W_res"], np.float32),
                            np.asarray(inputs["L0_ffn_W_pre"], np.float32),
                            np.asarray(inputs["L0_ffn_W_post"], np.float32)], 0).T)
        wcat = np.ascontiguousarray(
            wcat.reshape(self.KC, 128, self.NL).transpose(1, 0, 2).reshape(128, self.KC * self.NL))
        basec = np.ascontiguousarray(
            np.concatenate([np.asarray(inputs["L0_ffn_H_res_base"], np.float32).reshape(-1),
                            np.asarray(inputs["L0_ffn_H_pre_base"], np.float32),
                            np.asarray(inputs["L0_ffn_H_post_base"], np.float32)])[None, :])
        sidx = np.arange(128)[:, None]; tidx = np.arange(128)[None, :]
        KC = self.KC
        w1 = np.asarray(inputs["L0_attn_in_proj_w"], np.float32)
        w2 = np.asarray(inputs["L0_attn_out_proj_w"], np.float32)
        wbr0 = np.asarray(inputs["L0_ffn_W_branch"], np.float32)

        def blocktiles(groups, padw):
            out = np.zeros((len(groups) * KC * 128, padw), np.float32)
            for gi, g in enumerate(groups):
                oc = 0
                for ncid in g:
                    c0, w = self.nc_cols(ncid)
                    blk = w1[:, c0:c0 + w].reshape(KC, 128, w)
                    out.reshape(len(groups), KC, 128, padw)[gi, :, :, oc:oc + w] = blk
                    oc += w
            return np.ascontiguousarray(out)

        w1p1 = blocktiles(self.wslab_groups(1), 256)
        w1p2 = blocktiles(self.wslab_groups(2), 384)
        nwid = min(512, self.D)
        NB = (self.D + nwid - 1) // nwid
        w2b = np.ascontiguousarray(
            w2.reshape(self.XC, 128, NB, nwid).transpose(2, 0, 1, 3).reshape(NB * self.XC * 128, nwid))
        nby = (self.CDIM + 511) // 512
        wbrb = np.ascontiguousarray(
            wbr0.reshape(self.CC, 128, nby, 512).transpose(2, 0, 1, 3).reshape(nby * self.CC * 128, 512))
        common = dict(
            w1p1=w1p1, w1p2=w1p2, w2b=w2b, wbrb=wbrb,
            wcat=wcat, basec=basec,
            eyer=np.eye(self.NSTR, dtype=np.float32).reshape(1, -1),
            ident=np.eye(128, dtype=np.float32),
            allones=np.ones((128, 128), np.float32),
            trineg=np.where(sidx <= tidx, 0.0, NEG_BIG).astype(np.float32),
            tril1=(sidx <= tidx).astype(np.float32),
            convw=np.ascontiguousarray(np.asarray(inputs["L0_attn_conv_w"], np.float32)),
            alog=np.asarray(inputs["L0_attn_A_log"], np.float32).reshape(1, -1),
            dtb=np.asarray(inputs["L0_attn_dt_bias"], np.float32).reshape(-1, 1),
            dh=np.asarray(inputs["L0_attn_D"], np.float32).reshape(1, -1),
            nrmw=np.asarray(inputs["L0_attn_norm_w"], np.float32).reshape(-1),
            lnw=np.asarray(inputs["L0_attn_ln_w"], np.float32).reshape(-1),
            mup=np.asarray(inputs["L0_attn_mup_vector"], np.float32).reshape(-1),
            gate=np.asarray(inputs["L0_ffn_gate"], np.float32).reshape(1, 1),
        )
        maps = []
        for i in range(self.NCORES):
            lo = i * self.T
            halo = x[lo - 3:lo] if i > 0 else np.zeros((3, self.D), np.float32)
            rm = np.zeros((8, 8), np.float32)
            zm = np.zeros((8, 1), np.float32)
            for j in range(8):
                if j < i:
                    zm[j, 0] = 1.0
                    for k in range(j + 1, i):
                        rm[k, j] = 1.0
            m = dict(common)
            m.update(xsh=np.ascontiguousarray(np.concatenate([halo, x[lo:lo + self.T]], 0)),
                     rankm=rm, zmask=zm)
            maps.append(m)
        return maps

    def build(self, dbg=False):
        nc = bass.Bass("TRN2", target_bir_lowering=False, debug=False)
        self.nc = nc
        D, DSSM, NH, T, NL, NSTR, CDIM = self.D, self.DSSM, self.NH, self.T, self.NL, self.NSTR, self.CDIM
        din = lambda n, s: nc.dram_tensor(n, s, F32, kind="ExternalInput")
        self.i_x = din("xsh", [T + 3, D])
        self.NG1 = len(self.wslab_groups(1))
        self.NG2 = len(self.wslab_groups(2))
        self.NB5 = (D + min(512, D) - 1) // min(512, D)
        self.NBY = (CDIM + 511) // 512
        self.i_w1p1 = din("w1p1", [self.NG1 * self.KC * 128, 256])
        self.i_w1p2 = din("w1p2", [self.NG2 * self.KC * 128, 384])
        self.i_w2b = din("w2b", [self.NB5 * self.XC * 128, min(512, D)])
        self.i_wbrb = din("wbrb", [self.NBY * self.CC * 128, 512])
        self.i_wcat = din("wcat", [128, self.KC * NL])
        self.i_conv = din("convw", [self.CONV, self.K])
        self.i_alog = din("alog", [1, NH])
        self.i_dtb = din("dtb", [NH, 1])
        self.i_dh = din("dh", [1, NH])
        self.i_nrmw = din("nrmw", [DSSM])
        self.i_lnw = din("lnw", [D])
        self.i_mup = din("mup", [self.PROJ])
        self.i_gate = din("gate", [1, 1])
        self.i_basec = din("basec", [1, NL])
        self.i_eyer = din("eyer", [1, NSTR * NSTR])
        self.i_ident = din("ident", [128, 128])
        self.i_ones = din("allones", [128, 128])
        self.i_trineg = din("trineg", [128, 128])
        self.i_tril1 = din("tril1", [128, 128])
        self.i_rankm = din("rankm", [8, 8])
        self.i_zmask = din("zmask", [8, 1])
        self.o_out = nc.dram_tensor("out", [T, D], F32, kind="ExternalOutput")
        self.d_xstok = nc.dram_tensor("xstok_d", [T, DSSM], F32)
        self.d_s = nc.dram_tensor("s_d", [self.NCH * 128, DSSM], F32)
        self.d_sz = nc.dram_tensor("sz_d", [DSSM, T], F32)
        self.d_yfm = nc.dram_tensor("yfm_d", [DSSM, T], F32)
        self.d_ffnx = nc.dram_tensor("ffnx_d", [T, D], F32)
        self.d_ccin = nc.dram_tensor("ccin", [129, DSSM], F32)
        self.d_ccout = nc.dram_tensor("ccout", [self.NCORES * 129, DSSM], F32)
        self.dbg = {}
        if dbg:
            for nm, sh in [("dbg_xa", [128, self.KC * self.TW]), ("dbg_lt", [128, self.NCH * NH]),
                           ("dbg_wf", [128, self.NCH * NH]),
                           ("dbg_bfm", [128, T]), ("dbg_hloc", [128, DSSM]),
                           ("dbg_stB", [128, 1028]), ("dbg_accB", [128, T]),
                           ("dbg_logt", [128, self.TC * NL]), ("dbg_p", [128, self.TC * NSTR * NSTR]),
                           ("dbg_y", [128, DSSM]), ("dbg_yn", [128, self.XC * self.TW]),
                           ("dbg_hg", [128, DSSM])]:
                self.dbg[nm] = nc.dram_tensor(nm, sh, F32, kind="ExternalOutput")

        import contextlib
        st = contextlib.ExitStack()
        sb = lambda n, s, d=F32: st.enter_context(nc.sbuf_tensor(n, s, d))
        if self.MM == "fp32":
            self.xa = sb("xa", [128, self.KC * self.TW])
        else:
            raise NotImplementedError
        self.AR = sb("AR", [128, 16192])
        self.bfm = sb("bfm", [128, T])
        self.cfm = sb("cfm", [128, T])
        self.ident = sb("ident_sb", [128, 128])
        self.ones = sb("ones_sb", [128, 128])
        self.trineg = sb("trineg_sb", [128, 128])
        self.tril1 = sb("tril1_sb", [128, 128])
        ncc = self.KC + self.XC + self.NCHK + (self.CONV // 128) * self.K + 8
        self.ccols = sb("ccols", [128, ncc])
        self.o_lnw = 0
        self.o_nrmw = self.KC
        self.o_mup = self.KC + self.XC
        self.o_convc = self.KC + self.XC + self.NCHK
        self.smalls = sb("smalls", [128, 136])
        self.rank_sb = sb("rank_sb", [8, 8])
        self.zmask_sb = sb("zmask_sb", [8, 1])
        self.rows = sb("rows_sb", [1, 192])
        self.dtb_sb = sb("dtb_sb", [NH, 1])
        self.base_rep = sb("base_rep", [128, NL])
        self.hbl = sb("hbl", [128, NSTR * NSTR])
        self.gcol = sb("gcol", [128, 1])
        self.drep = sb("drep", [128, NH])
        self.negarep = sb("negarep", [128, NH])
        self.rlnw = sb("rlnw", [128, self.KC])
        self.sink_sb = sb("sink_sb", [128, 448])
        self.SK = {"logT": 0, "P": 192, "rs": 320, "hp": 352, "ho": 384}
        self.ps = [st.enter_context(nc.psum_tensor(f"ps{i}", [128, 512], F32)) for i in range(8)]
        sem = lambda n: st.enter_context(nc.semaphore(n))
        self.s_si = sem("si"); self.c_si = 0
        self.s_so = sem("so"); self.c_so = 0
        self.s_sd = sem("sd"); self.c_sd = 0
        self.s_sv = sem("sv"); self.c_sv = 0
        self.s_sa = sem("sa"); self.c_sa = 0
        self.s_sp = sem("sp"); self.c_sp = 0
        self.s_cc = sem("scc"); self.c_cc = 0
        self.s_w = [sem("sw0"), sem("sw1")]; self.c_w = [0, 0]
        blk = st.enter_context(nc.Block())

        @blk.gpsimd
        def _(g):
            self.emit_all()
        st.close()
        return nc

    def vi(self, instr):
        instr.then_inc(self.s_sv, 1); self.c_sv += 1; return self.c_sv

    def ai(self, instr):
        instr.then_inc(self.s_sa, 1); self.c_sa += 1; return self.c_sa

    def pi(self, instr):
        instr.then_inc(self.s_sp, 1); self.c_sp += 1; return self.c_sp

    def dsi(self, instr):
        instr.then_inc(self.s_si, 16); self.c_si += 16; return self.c_si

    def dso(self, instr):
        instr.then_inc(self.s_so, 16); self.c_so += 16; return self.c_so

    def dsd(self, instr):
        instr.then_inc(self.s_sd, 16); self.c_sd += 16; return self.c_sd

    def dwi(self, par, instr):
        instr.then_inc(self.s_w[par], 16); self.c_w[par] += 16; return self.c_w[par]

    def wwi(self, eng, par, n=None):
        n = self.c_w[par] if n is None else n
        if n > 0: eng.wait_ge(self.s_w[par], n)

    def wv(self, eng, n=None):
        n = self.c_sv if n is None else n
        if n > 0: eng.wait_ge(self.s_sv, n)

    def wa(self, eng, n=None):
        n = self.c_sa if n is None else n
        if n > 0: eng.wait_ge(self.s_sa, n)

    def wp(self, eng, n=None):
        n = self.c_sp if n is None else n
        if n > 0: eng.wait_ge(self.s_sp, n)

    def wsi(self, eng, n=None):
        n = self.c_si if n is None else n
        if n > 0: eng.wait_ge(self.s_si, n)

    def wso(self, eng, n=None):
        n = self.c_so if n is None else n
        if n > 0: eng.wait_ge(self.s_so, n)

    def wsd(self, eng, n=None):
        n = self.c_sd if n is None else n
        if n > 0: eng.wait_ge(self.s_sd, n)

    def s_(self, name, c0=0, cn=None, p0=0, pn=128):
        o = self.SK[name] + c0
        return self.sink_sb[p0:p0 + pn, o:o + cn]

    def a_(self, name, c0=0, cn=None, p0=0, pn=128):
        base = self.A[name]
        cn = 1 if cn is None else cn
        return self.AR[p0:p0 + pn, base + c0:base + c0 + cn]

    def bcastv(self, ap2d, ngrp, rep):
        return bass.AP(ap2d.tensor, ap2d.offset,
                       [list(ap2d.ap[0]), [ap2d.ap[1][0], ngrp], [0, rep]])

    def tw_off(self, k, col):
        return k * self.TW + col

    def emit_all(self):
        self.load_consts()
        self.phase0()
        self.phase1()
        self.collective()
        self.phase2()
        self.phase3()
        self.ffn_mix()
        self.phase4()
        self.phase5()
        nc = self.nc
        nc.sync.wait_ge(self.s_si, self.c_si)
        nc.scalar.wait_ge(self.s_sd, self.c_sd)
        nc.gpsimd.wait_ge(self.s_so, self.c_so)

    def load_consts(self):
        nc = self.nc
        v, s, y, t = nc.vector, nc.scalar, nc.sync, nc.tensor
        NH, KC, XC, NL, NSTR = self.NH, self.KC, self.XC, self.NL, self.NSTR
        self.dsi(y.dma_start(out=self.ident[:], in_=self.i_ident.ap()))
        self.dsi(y.dma_start(out=self.ones[:], in_=self.i_ones.ap()))
        self.dsi(y.dma_start(out=self.trineg[:], in_=self.i_trineg.ap()))
        self.dsi(y.dma_start(out=self.tril1[:], in_=self.i_tril1.ap()))
        self.dsi(y.dma_start(out=self.rank_sb[:], in_=self.i_rankm.ap()))
        self.dsi(y.dma_start(out=self.zmask_sb[:], in_=self.i_zmask.ap()))
        self.dsi(y.dma_start(out=self.dtb_sb[:], in_=self.i_dtb.ap()))
        self.o_rbase, self.o_reye, self.o_rgate, self.o_ralog, self.o_rdh = 0, NL, NL + 16, NL + 24, NL + 24 + NH
        self.dsi(y.dma_start(out=self.rows[0:1, 0:NL], in_=self.i_basec.ap()))
        self.dsi(y.dma_start(out=self.rows[0:1, self.o_reye:self.o_reye + NSTR * NSTR], in_=self.i_eyer.ap()))
        self.dsi(y.dma_start(out=self.rows[0:1, self.o_rgate:self.o_rgate + 1], in_=self.i_gate.ap()))
        self.dsi(y.dma_start(out=self.rows[0:1, self.o_ralog:self.o_ralog + NH], in_=self.i_alog.ap()))
        self.dsi(y.dma_start(out=self.rows[0:1, self.o_rdh:self.o_rdh + NH], in_=self.i_dh.ap()))
        self.dsi(y.dma_start(out=self.ccols[:, self.o_lnw:self.o_lnw + KC],
                             in_=self.i_lnw.ap().rearrange("(c p) -> p c", p=128), allow_slow_non_contiguous=True))
        self.dsi(y.dma_start(out=self.ccols[:, self.o_nrmw:self.o_nrmw + XC],
                             in_=self.i_nrmw.ap().rearrange("(c p) -> p c", p=128), allow_slow_non_contiguous=True))
        nfull = self.PROJ // 128
        self.dsi(y.dma_start(out=self.ccols[:, self.o_mup:self.o_mup + nfull],
                             in_=self.i_mup[0:nfull * 128].rearrange("(c p) -> p c", p=128), allow_slow_non_contiguous=True))
        if self.PROJ > nfull * 128:
            self.dsi(y.dma_start(out=self.ccols[0:NH, self.o_mup + nfull:self.o_mup + nfull + 1],
                                 in_=self.i_mup[nfull * 128:nfull * 128 + NH].rearrange("(p o) -> p o", o=1), allow_slow_non_contiguous=True))
        ncv = self.CONV // 128
        self.dsi(y.dma_start(out=self.ccols[:, self.o_convc:self.o_convc + ncv * self.K].rearrange(
                                 "p (c k) -> p c k", k=self.K),
                             in_=self.i_conv.ap().rearrange("(c p) k -> p c k", p=128),
                             allow_slow_non_contiguous=True))
        self.wsi(nc.vector); self.wsi(nc.scalar); self.wsi(nc.tensor)
        self.vi(v.reciprocal(self.rlnw[:, 0:KC], self.ccols[:, self.o_lnw:self.o_lnw + KC]))
        ps = self.ps[0]
        t.matmul(ps[:, 0:NL], self.ones[0:1, :], self.rows[0:1, 0:NL], start=True, stop=True)
        t.matmul(ps[:, 32:32 + 16], self.ones[0:1, :], self.rows[0:1, self.o_reye:self.o_reye + 16],
                 start=True, stop=True)
        t.matmul(ps[:, 64:65], self.ones[0:1, :], self.rows[0:1, self.o_rgate:self.o_rgate + 1],
                 start=True, stop=True)
        pc = self.pi(t.matmul(ps[:, 96:96 + NH], self.ones[0:1, :], self.rows[0:1, self.o_rdh:self.o_rdh + NH],
                              start=True, stop=True))
        self.wp(nc.vector, pc)
        self.vi(v.tensor_copy(self.base_rep[:], ps[:, 0:NL]))
        self.vi(v.tensor_copy(self.drep[:], ps[:, 96:96 + NH]))
        self.wp(nc.scalar, pc)
        ga = self.ai(s.activation(self.gcol[:], ps[:, 64:65], ACT.Sigmoid))
        self.wa(nc.vector, ga)
        gm = self.vi(v.tensor_scalar(self.smalls[:, 130:131], self.gcol[:], -1.0, 1.0, op0=ALU.mult, op1=ALU.add))
        self.vi(v.tensor_copy(self.hbl[:], ps[:, 32:48]))
        self.wv(nc.vector)
        self.vi(v.tensor_scalar(self.hbl[:], self.hbl[:], self.smalls[:, 130:131], None, op0=ALU.mult))
        ea = self.ai(s.activation(self.rows[0:1, self.o_ralog:self.o_ralog + NH],
                                  self.rows[0:1, self.o_ralog:self.o_ralog + NH], ACT.Exp))
        self.wa(nc.vector, ea)
        na = self.vi(v.tensor_scalar(self.rows[0:1, self.o_ralog:self.o_ralog + NH],
                                     self.rows[0:1, self.o_ralog:self.o_ralog + NH], -1.0, None, op0=ALU.mult))
        self.wv(nc.tensor, na)
        pc2 = self.pi(t.matmul(ps[:, 128:128 + NH], self.ones[0:1, :],
                               self.rows[0:1, self.o_ralog:self.o_ralog + NH], start=True, stop=True))
        self.wp(nc.vector, pc2)
        self.vi(v.tensor_copy(self.negarep[:], ps[:, 128:128 + NH]))

    def phase0(self):
        nc = self.nc
        v, s, y, t = nc.vector, nc.scalar, nc.sync, nc.tensor
        T, TC, KC, D = self.T, self.TC, self.KC, self.D
        EPSL, EPSA, EPSF = self.EPS_LN, self.EPS_A, self.EPS_F
        sm = self.smalls
        for ct in range(TC + 1):
            halo = (ct == TC)
            pn = 3 if halo else 128
            r0 = 0 if halo else 3 + ct * 128
            self.wv(nc.sync)
            self.wa(nc.sync)
            self.wp(nc.sync)
            ld = self.dsi(y.dma_start(out=self.a_("p0_x", 0, D, pn=pn), in_=self.i_x[r0:r0 + pn, :]))
            self.wsi(nc.vector, ld)
            self.wsi(nc.scalar, ld)
            xt = self.a_("p0_x", 0, D, pn=pn)
            c0 = ct * 4
            stc = lambda j: self.a_("statc", c0 + j, 1, pn=pn)
            self.vi(v.tensor_reduce(sm[0:pn, 0:1], xt, axis=AX.X, op=ALU.add))
            sq = self.a_("p0_xa", 0, D, pn=pn)
            sa1 = self.ai(s.activation(sq, xt, ACT.Square))
            self.wa(nc.vector, sa1)
            self.vi(v.tensor_reduce(sm[0:pn, 1:2], sq, axis=AX.X, op=ALU.add))
            self.wv(nc.vector)
            self.vi(v.tensor_scalar(stc(0), sm[0:pn, 0:1], 1.0 / D, None, op0=ALU.mult))
            self.vi(v.tensor_scalar(sm[0:pn, 1:2], sm[0:pn, 1:2], 1.0 / D, None, op0=ALU.mult))
            self.wv(nc.vector)
            self.vi(v.tensor_tensor(sm[0:pn, 2:3], stc(0), stc(0), op=ALU.mult))
            self.wv(nc.vector)
            self.vi(v.tensor_tensor(sm[0:pn, 3:4], sm[0:pn, 1:2], sm[0:pn, 2:3], op=ALU.subtract))
            self.wv(nc.vector)
            self.vi(v.tensor_scalar(sm[0:pn, 4:5], sm[0:pn, 3:4], EPSL, None, op0=ALU.add))
            self.wv(nc.scalar)
            sa2 = self.ai(s.activation(sm[0:pn, 5:6], sm[0:pn, 4:5], ACT.Sqrt))
            self.wa(nc.vector, sa2)
            self.vi(v.reciprocal(stc(1), sm[0:pn, 5:6]))
            self.wv(nc.vector)
            self.vi(v.tensor_tensor(sm[0:pn, 6:7], sm[0:pn, 3:4], stc(1), op=ALU.mult))
            self.wv(nc.vector)
            self.vi(v.tensor_tensor(sm[0:pn, 6:7], sm[0:pn, 6:7], stc(1), op=ALU.mult))
            self.wv(nc.vector)
            self.vi(v.tensor_scalar(sm[0:pn, 7:8], sm[0:pn, 6:7], EPSA, None, op0=ALU.add))
            self.vi(v.tensor_scalar(sm[0:pn, 8:9], sm[0:pn, 6:7], EPSF, None, op0=ALU.add))
            self.wv(nc.scalar)
            sa3 = self.ai(s.activation(sm[0:pn, 9:10], sm[0:pn, 7:8], ACT.Sqrt))
            self.wa(nc.vector, sa3)
            self.vi(v.reciprocal(sm[0:pn, 10:11], sm[0:pn, 9:10]))
            self.wv(nc.vector)
            self.vi(v.tensor_tensor(stc(2), stc(1), sm[0:pn, 10:11], op=ALU.mult))
            self.vi(v.reciprocal(sm[0:pn, 11:12], sm[0:pn, 8:9]))
            self.wv(nc.vector)
            self.vi(v.tensor_tensor(sm[0:pn, 12:13], sm[0:pn, 7:8], sm[0:pn, 11:12], op=ALU.mult))
            self.wv(nc.scalar)
            sa4 = self.ai(s.activation(stc(3), sm[0:pn, 12:13], ACT.Sqrt))
            self.wa(nc.vector, sa4)
            self.wv(nc.vector)
            self.vi(v.tensor_scalar(self.a_("p0_xa", 0, D, pn=pn), xt, stc(0), stc(2),
                                    op0=ALU.subtract, op1=ALU.mult))
            self.wv(nc.tensor)
            if not hasattr(self, "p0_ev"):
                self.p0_ev = [0, 0]
            for k in range(KC):
                par = k % 2
                pb = self.ps[6 + par]
                self.wv(nc.tensor, self.p0_ev[par])
                pc = self.pi(t.transpose(pb[0:128, 0:pn], self.a_("p0_xa", k * 128, 128, pn=pn),
                                         self.ident[0:pn, 0:pn]))
                self.wp(nc.vector, pc)
                dst = self.tw_off(k, 0 if halo else 3 + ct * 128)
                ev = self.vi(v.tensor_scalar(self.xa[:, dst:dst + pn], pb[0:128, 0:pn],
                                             self.ccols[:, self.o_lnw + k:self.o_lnw + k + 1], None,
                                             op0=ALU.mult))
                self.p0_ev[par] = ev

    def wslab_groups(self, phase):
        XC = self.XC
        if phase == 1:
            groups = [[self.NC_B, self.NC_C], [self.NC_DT, XC]]
            xs = list(range(XC + 1, 2 * XC))
            for i in range(0, len(xs), 2):
                groups.append(xs[i:i + 2])
        else:
            zs = list(range(XC))
            groups = []
            for i in range(0, len(zs), 3):
                groups.append(zs[i:i + 3])
        return groups

    def nc_cols(self, ncid):
        w = self.NH if ncid == self.NC_DT else 128
        return ncid * 128, w

    def emit_group(self, g, gsrc=None, with_logits=False, halo=True, bank3=True):
        nc = self.nc
        v, s, y, t = nc.vector, nc.scalar, nc.sync, nc.tensor
        KC = self.KC
        c0 = self.nc_cols(g[0])[0]
        wsum = sum(self.nc_cols(n)[1] for n in g)
        if not hasattr(self, "w_last_mm"):
            self.w_last_mm = [0, 0]
        last_mm = self.w_last_mm
        for k in range(KC):
            par = k % 2
            self.wp(nc.sync, last_mm[par])
            gten, gidx, padw = gsrc
            r0 = (gidx * KC + k) * 128
            self.dwi(par, y.dma_start(out=self.a_("wraw", par * 384, padw),
                                      in_=gten[r0:r0 + 128, 0:padw]))
            self.wwi(nc.tensor, par)
            if k == 0:
                self.wv(nc.tensor, self.bank_free)
                if with_logits:
                    self.wv(nc.tensor, self.wlg_ready)
            mm = None
            for i, ncid in enumerate(g):
                oc = sum(self.nc_cols(n)[1] for n in g[:i])
                w = self.nc_cols(ncid)[1]
                lhs = self.a_("wraw", par * 384 + oc, w)
                bb = 3 * i if bank3 else 2 * i
                for j, (to, tn) in enumerate(self.ttiles):
                    mm = t.matmul(self.ps[bb + j][0:w, 0:tn], lhs,
                                  self.xa[:, self.tw_off(k, 3 + to):self.tw_off(k, 3 + to) + tn],
                                  start=(k == 0), stop=(k == KC - 1))
                if halo and ncid != self.NC_DT:
                    mm = t.matmul(self.ps[3 * i + 2][0:w, 0:3], lhs,
                                  self.xa[:, self.tw_off(k, 0):self.tw_off(k, 0) + 3],
                                  start=(k == 0), stop=(k == KC - 1))
            if with_logits:
                lhs = self.a_("wlgall", k * self.NL, self.NL)
                for j, (to, tn) in enumerate(self.ttiles):
                    mm = t.matmul(self.ps[6 + j][0:self.NL, 0:tn], lhs,
                                  self.xa[:, self.tw_off(k, 3 + to):self.tw_off(k, 3 + to) + tn],
                                  start=(k == 0), stop=(k == KC - 1))
            last_mm[par] = self.pi(mm)
        return self.c_sp

    def phase1(self):
        nc = self.nc
        v, g_ = nc.vector, nc.gpsimd
        T = self.T
        self.wv(nc.sync); self.wa(nc.sync); self.wp(nc.sync)
        self.wv(nc.gpsimd); self.wa(nc.gpsimd); self.wp(nc.gpsimd)
        self.bank_free = self.c_sv
        z0 = self.vi(v.memset(self.a_("hloc", 0, self.DSSM), 0.0))
        self.wv(nc.vector, z0)
        for gi1, g in enumerate(self.wslab_groups(1)):
            done = self.emit_group(g, gsrc=(self.i_w1p1, gi1, 256), halo=True, bank3=True)
            self.wp(nc.vector, done)
            stgs = []
            for i, ncid in enumerate(g):
                stg = "stage" if i == 0 else "stageB"
                stgs.append(stg)
                w = self.nc_cols(ncid)[1]
                mupc = self.ccols[0:w, self.o_mup + ncid:self.o_mup + ncid + 1]
                if ncid != self.NC_DT:
                    self.vi(v.tensor_scalar(self.a_(stg, 0, 3, pn=w), self.ps[3 * i + 2][0:w, 0:3],
                                            mupc, None, op0=ALU.mult))
                for j, (to, tn) in enumerate(self.ttiles):
                    self.vi(v.tensor_scalar(self.a_(stg, 3 + to, tn, pn=w), self.ps[3 * i + j][0:w, 0:tn],
                                            mupc, None, op0=ALU.mult))
            self.bank_free = self.c_sv
            for i, ncid in enumerate(g):
                stg = stgs[i]
                self.wv(nc.vector)
                if ncid == self.NC_DT:
                    self.proc_dt(stg)
                elif ncid == self.NC_B:
                    if "dbg_stB" in self.dbg:
                        self.wv(nc.gpsimd)
                        self.dso(nc.gpsimd.dma_start(out=self.dbg["dbg_stB"].ap(), in_=self.a_(stg, 0, 1028)))
                    self.proc_conv(ncid, self.bfm[:], stg)
                    if "dbg_accB" in self.dbg:
                        self.wv(nc.gpsimd); self.wa(nc.gpsimd)
                        self.dso(nc.gpsimd.dma_start(out=self.dbg["dbg_accB"].ap(), in_=self.a_("acc", 0, self.T)))
                    self.trans_chunks(self.bfm[:], "btok")
                elif ncid == self.NC_C:
                    self.proc_conv(ncid, self.cfm[:], stg)
                else:
                    self.proc_xs(ncid - self.XC, stg)
        if "dbg_hloc" in self.dbg:
            self.wv(nc.gpsimd)
            self.dso(g_.dma_start(out=self.dbg["dbg_hloc"].ap(), in_=self.a_("hloc", 0, self.DSSM)))
        if "dbg_lt" in self.dbg:
            self.wv(nc.gpsimd)
            self.dso(g_.dma_start(out=self.dbg["dbg_lt"].ap(), in_=self.a_("LT", 0, self.NCH * self.NH)))
        if "dbg_wf" in self.dbg:
            self.wv(nc.gpsimd)
            self.dso(g_.dma_start(out=self.dbg["dbg_wf"].ap(), in_=self.a_("wfacT", 0, self.NCH * self.NH)))
        if "dbg_bfm" in self.dbg:
            self.wv(nc.gpsimd)
            self.dso(g_.dma_start(out=self.dbg["dbg_bfm"].ap(), in_=self.bfm[:]))
        if "dbg_xa" in self.dbg:
            self.wv(nc.gpsimd)
            self.dso(g_.dma_start(out=self.dbg["dbg_xa"].ap(), in_=self.xa[:]))

    def trans_chunks(self, src, dstname):
        nc = self.nc
        v, t = nc.vector, nc.tensor
        for cc in range(self.NCH):
            self.wv(nc.tensor)
            pc = self.pi(t.transpose(self.ps[7][0:128, 0:128], src[:, cc * 128:(cc + 1) * 128], self.ident[:]))
            self.wp(nc.vector, pc)
            self.vi(v.tensor_copy(self.a_(dstname, cc * 128, 128), self.ps[7][0:128, 0:128]))

    def proc_conv(self, ncid, dst, stg):
        nc = self.nc
        v, s = nc.vector, nc.scalar
        T = self.T
        cchunk = ncid - self.XC
        base = self.o_convc + cchunk * self.K
        acc = self.a_("acc", 0, T)
        self.vi(v.tensor_scalar(acc, self.a_(stg, 0, T), self.ccols[:, base:base + 1], None, op0=ALU.mult))
        for k in range(1, self.K):
            self.wv(nc.vector)
            self.vi(v.scalar_tensor_tensor(acc, self.a_(stg, k, T), self.ccols[:, base + k:base + k + 1],
                                           acc, op0=ALU.mult, op1=ALU.add))
        self.wv(nc.scalar)
        self.ai(s.activation(dst, acc, ACT.Silu))
        self.wa(nc.vector)
        self.wa(nc.tensor)

    def proc_dt(self, stg):
        nc = self.nc
        v, s, t = nc.vector, nc.scalar, nc.tensor
        NH, NCH, T = self.NH, self.NCH, self.T
        x = self.a_(stg, 3, T, pn=NH)
        self.vi(v.tensor_scalar(x, x, self.dtb_sb[0:NH, :], None, op0=ALU.add))
        ax = self.a_("dt_ax", 0, T, pn=NH)
        mx = self.a_("dt_mx", 0, T, pn=NH)
        self.wv(nc.scalar)
        a1 = self.ai(s.activation(ax, x, ACT.Abs))
        self.wa(nc.scalar, a1)
        a2 = self.ai(s.activation(ax, ax, ACT.Exp, scale=-1.0))
        self.wa(nc.scalar, a2)
        a3 = self.ai(s.activation(ax, ax, ACT.Ln, bias=1.0))
        self.wv(nc.vector)
        self.vi(v.tensor_scalar(mx, x, 0.0, None, op0=ALU.max))
        self.wa(nc.vector, a3)
        self.wv(nc.vector)
        spv = self.vi(v.tensor_tensor(x, ax, mx, op=ALU.add))
        self.wv(nc.tensor, spv)
        for cc in range(NCH):
            pc = self.pi(t.transpose(self.ps[7][0:128, 0:NH], x[0:NH, cc * 128:(cc + 1) * 128],
                                     self.ident[0:NH, 0:NH]))
            self.wp(nc.vector, pc)
            ev = self.vi(v.tensor_copy(self.a_("dtT", cc * NH, NH), self.ps[7][0:128, 0:NH]))
            self.wv(nc.vector)
            av = self.vi(v.tensor_tensor(self.a_("acumT", cc * NH, NH), self.a_("dtT", cc * NH, NH),
                                         self.negarep[:, 0:NH], op=ALU.mult))
            self.wv(nc.tensor, av)
            pc = self.pi(t.matmul(self.ps[7][0:128, 128:128 + NH], self.tril1[:],
                                  self.a_("acumT", cc * NH, NH), start=True, stop=True))
            self.wp(nc.vector, pc)
            lt = self.vi(v.tensor_copy(self.a_("LT", cc * NH, NH), self.ps[7][0:128, 128:128 + NH]))
            self.wv(nc.tensor, lt)
            pc = self.pi(t.transpose(self.ps[7][0:NH, 256:384], self.a_("LT", cc * NH, NH), self.ident[:]))
            self.wp(nc.vector, pc)
            lc = self.vi(v.tensor_copy(self.smalls[0:NH, 16:17], self.ps[7][0:NH, 256 + 127:256 + 128]))
            self.wv(nc.vector)
            if cc == 0:
                self.vi(v.tensor_copy(self.smalls[0:NH, 17:18], self.smalls[0:NH, 16:17]))
            else:
                self.vi(v.tensor_tensor(self.smalls[0:NH, 17:18], self.smalls[0:NH, 17:18],
                                        self.smalls[0:NH, 16:17], op=ALU.add))
            self.wa(nc.scalar, 0); self.wv(nc.scalar)
            ec = self.ai(s.activation(self.smalls[0:NH, 18:19], self.smalls[0:NH, 16:17], ACT.Exp))
            self.wa(nc.vector, ec)
            d1 = self.vi(v.tensor_scalar(self.smalls[0:NH, 64:64 + NH], self.ident[0:NH, 0:NH],
                                         self.smalls[0:NH, 18:19], None, op0=ALU.mult))
            self.wv(nc.tensor, d1)
            pc = self.pi(t.matmul(self.ps[7][0:128, 384:384 + NH], self.ones[0:NH, :],
                                  self.smalls[0:NH, 64:64 + NH], start=True, stop=True))
            self.wp(nc.vector, pc)
            self.vi(v.tensor_copy(self.a_("dARep", cc * NH, NH), self.ps[7][0:128, 384:384 + NH]))
            self.wv(nc.vector)
            d2 = self.vi(v.tensor_scalar(self.smalls[0:NH, 64:64 + NH], self.ident[0:NH, 0:NH],
                                         self.smalls[0:NH, 16:17], None, op0=ALU.mult))
            self.wv(nc.tensor, d2)
            pc = self.pi(t.matmul(self.ps[7][0:128, 384:384 + NH], self.ones[0:NH, :],
                                  self.smalls[0:NH, 64:64 + NH], start=True, stop=True))
            self.wp(nc.vector, pc)
            lr = self.vi(v.tensor_copy(self.a_("lendrep", 0, NH), self.ps[7][0:128, 384:384 + NH]))
            self.wv(nc.vector, lr)
            sv = self.vi(v.tensor_tensor(self.a_("wfacT", cc * NH, NH), self.a_("lendrep", 0, NH),
                                         self.a_("LT", cc * NH, NH), op=ALU.subtract))
            self.wv(nc.scalar, sv)
            ae = self.ai(s.activation(self.a_("wfacT", cc * NH, NH), self.a_("wfacT", cc * NH, NH), ACT.Exp))
            self.wa(nc.vector, ae)
            self.vi(v.tensor_tensor(self.a_("wfacT", cc * NH, NH), self.a_("wfacT", cc * NH, NH),
                                    self.a_("dtT", cc * NH, NH), op=ALU.mult))
            self.wv(nc.tensor)
        self.wv(nc.scalar)
        af = self.ai(s.activation(self.a_("acumT", 0, NCH * NH), self.a_("LT", 0, NCH * NH), ACT.Exp))
        self.wa(nc.vector, af)
        self.wa(nc.tensor, af)

    def proc_xs(self, kx, stg):
        nc = self.nc
        v, s, y, g_, t = nc.vector, nc.scalar, nc.sync, nc.gpsimd, nc.tensor
        NH, NCH, T, HP, HD = self.NH, self.NCH, self.T, self.HP, self.HD
        xsil = self.a_("xsil", 0, T)
        base = self.o_convc + kx * self.K
        acc = self.a_("acc", 0, T)
        self.vi(v.tensor_scalar(acc, self.a_(stg, 0, T), self.ccols[:, base:base + 1], None, op0=ALU.mult))
        for k in range(1, self.K):
            self.wv(nc.vector)
            self.vi(v.scalar_tensor_tensor(acc, self.a_(stg, k, T), self.ccols[:, base + k:base + k + 1],
                                           acc, op0=ALU.mult, op1=ALU.add))
        self.wv(nc.scalar)
        sil = self.ai(s.activation(xsil, acc, ACT.Silu))
        self.wa(nc.tensor, sil)
        self.wa(nc.vector, sil)
        if not hasattr(self, "xs_pv"):
            self.xs_pv = [0, 0]; self.xs_sv = [0, 0]; self.xs_dx = [0, 0]; self.xs_ds = [0, 0]
            self.xs_mm = [0, 0]
        for cc in range(NCH):
            par = cc % 2
            pb = self.ps[6 + par]
            piece = self.a_("piece", par * 128, 128)
            xw = self.a_("xw", par * 128, 128)
            sst = self.a_("sstage", par * 128, 128)
            self.wv(nc.tensor, self.xs_pv[par])
            pc = self.pi(t.transpose(pb[0:128, 0:128], xsil[:, cc * 128:(cc + 1) * 128], self.ident[:]))
            self.wp(nc.vector, pc)
            self.wso(nc.vector, self.xs_dx[par])
            pv = self.vi(v.tensor_copy(piece, pb[0:128, 0:128]))
            self.wv(nc.gpsimd, pv)
            self.xs_dx[par] = self.dso(g_.dma_start(
                out=self.d_xstok[cc * 128:(cc + 1) * 128, kx * 128:(kx + 1) * 128], in_=piece))
            wf = self.a_("wfacT", cc * NH + kx * HP, HP)
            wfv = self.bcastv(wf, HP, HD)
            self.wp(nc.vector, self.xs_mm[par])
            xv = self.vi(v.tensor_tensor(xw.rearrange("p (h d) -> p h d", h=HP),
                                         piece.rearrange("p (h d) -> p h d", h=HP), wfv, op=ALU.mult))
            self.xs_pv[par] = xv
            self.wv(nc.tensor, self.xs_sv[par])
            self.wv(nc.tensor, xv)
            pc2 = self.pi(t.matmul(pb[0:128, 128:256], self.a_("btok", cc * 128, 128), xw,
                                   start=True, stop=True))
            self.xs_mm[par] = pc2
            self.wp(nc.vector, pc2)
            for h in range(HP):
                col = kx * 128 + h * HD
                self.vi(v.scalar_tensor_tensor(self.a_("hloc", col, HD), self.a_("hloc", col, HD),
                                               self.a_("dARep", cc * NH + kx * HP + h, 1),
                                               pb[0:128, 128 + h * HD:128 + (h + 1) * HD],
                                               op0=ALU.mult, op1=ALU.add))
            self.wso(nc.vector, self.xs_ds[par])
            sv = self.vi(v.tensor_copy(sst, pb[0:128, 128:256]))
            self.xs_sv[par] = sv
            self.wv(nc.gpsimd, sv)
            self.xs_ds[par] = self.dso(g_.dma_start(
                out=self.d_s[cc * 128:(cc + 1) * 128, kx * 128:(kx + 1) * 128], in_=sst))

    def collective(self):
        nc = self.nc
        g_ = nc.gpsimd
        v = nc.vector
        self.wv(nc.vector)
        self.wso(nc.vector)
        zr = self.vi(v.memset(self.AR[0:1, 4096:4096 + self.DSSM], 0.0))
        self.wv(nc.gpsimd)
        self.dso(g_.dma_start(out=self.d_ccin[0:128, :], in_=self.a_("hloc", 0, self.DSSM)))
        self.dso(g_.dma_start(out=self.d_ccin[128:129, :], in_=self.AR[0:1, 4096:4096 + self.DSSM]))
        self.wso(nc.gpsimd)
        self.dso(g_.dma_start(out=self.d_ccin[128:129, 0:self.NH], in_=self.smalls[0:self.NH, 17:18]))
        self.wso(nc.gpsimd)
        g_.collective_compute(
            "AllGather", ALU.bypass,
            replica_groups=[list(range(self.NCORES))],
            ins=[self.d_ccin.ap()],
            outs=[self.d_ccout.ap()],
        ).then_inc(self.s_cc, 1)
        self.c_cc += 1

    def phase2(self):
        nc = self.nc
        v, s, y, g_, t = nc.vector, nc.scalar, nc.sync, nc.gpsimd, nc.tensor
        T, TC, NL = self.T, self.TC, self.NL
        self.wv(nc.sync); self.wa(nc.sync); self.wp(nc.sync)
        ld = self.dsi(y.dma_start(out=self.a_("wlgall", 0, self.KC * NL), in_=self.i_wcat.ap()))
        self.wsi(nc.vector, ld)
        for k in range(self.KC):
            self.vi(v.tensor_scalar(self.a_("wlgall", k * NL, NL), self.a_("wlgall", k * NL, NL),
                                    self.rlnw[:, k:k + 1], None, op0=ALU.mult))
        self.wlg_ready = self.c_sv
        groups = self.wslab_groups(2)
        for gi, g in enumerate(groups):
            done = self.emit_group(g, gsrc=(self.i_w1p2, gi, 384), with_logits=(gi == 0),
                                   halo=False, bank3=False)
            for i, ncid in enumerate(g):
                self.wp(nc.vector, done)
                self.wv(nc.vector)
                self.wa(nc.vector)
                mupc = self.ccols[:, self.o_mup + ncid:self.o_mup + ncid + 1]
                for j, (to, tn) in enumerate(self.ttiles):
                    self.vi(v.tensor_scalar(self.a_("z_st", to, tn), self.ps[2 * i + j][0:128, 0:tn],
                                            mupc, None, op0=ALU.mult))
                self.wv(nc.scalar)
                sz = self.ai(s.activation(self.a_("sz_st", 0, T), self.a_("z_st", 0, T), ACT.Silu))
                self.wa(nc.gpsimd, sz)
                self.dso(g_.dma_start(out=self.d_sz[ncid * 128:(ncid + 1) * 128, :], in_=self.a_("sz_st", 0, T)))
                self.wso(nc.scalar)
            if gi == 0:
                self.wp(nc.vector, done)
                for j, (to, tn) in enumerate(self.ttiles):
                    self.vi(v.tensor_copy(self.a_("lg_st", to, tn, pn=NL), self.ps[6 + j][0:NL, 0:tn]))
                self.logits_done = self.c_sv
            self.bank_free = self.c_sv
        self.wso(nc.vector)
        for ct in range(TC):
            self.wv(nc.tensor)
            pc = self.pi(t.transpose(self.ps[0][0:128, 0:NL], self.a_("lg_st", ct * 128, 128, pn=NL),
                                     self.ident[0:NL, 0:NL]))
            self.wp(nc.vector, pc)
            sf = self.a_("statc", ct * 4 + 3, 1)
            sv = self.vi(v.scalar_tensor_tensor(self.s_("logT", ct * NL, NL), self.ps[0][0:128, 0:NL], sf,
                                                self.base_rep[:], op0=ALU.mult, op1=ALU.add))
            self.wv(nc.tensor, sv)
        if "dbg_logt" in self.dbg:
            self.wv(nc.gpsimd)
            self.dso(g_.dma_start(out=self.dbg["dbg_logt"].ap(), in_=self.s_("logT", 0, TC * NL)))
        self.sink_gen = self.sinkhorn_steps()

    def sinkhorn_steps(self):
        nc = self.nc
        v, s = nc.vector, nc.scalar
        TC, NL, NSTR = self.TC, self.NL, self.NSTR
        NN = NSTR * NSTR
        P = lambda: self.s_("P", 0, TC * NN)
        pv = lambda: P().rearrange("p (c i j) -> p c i j", i=NSTR, j=NSTR)
        pvt = lambda: P().rearrange("p (c i j) -> p c j i", i=NSTR, j=NSTR)
        rs = lambda: self.s_("rs", 0, TC * NSTR)
        rsb = lambda: rs().rearrange("p (c n) -> p c n", n=NSTR)
        _r = rsb()
        rsv = bass.AP(_r.tensor, _r.offset, [list(_r.ap[0]), list(_r.ap[1]), list(_r.ap[2]), [0, NSTR]])
        csv = bass.AP(_r.tensor, _r.offset, [list(_r.ap[0]), list(_r.ap[1]), [0, NSTR], list(_r.ap[2])])
        tau_i = 1.0 / self.TAU
        for ct in range(TC):
            self.wv(nc.vector)
            self.vi(v.tensor_scalar(self.s_("P", ct * NN, NN), self.s_("logT", ct * NL, NN), tau_i, None,
                                    op0=ALU.mult))
        self.wv(nc.vector)
        self.vi(v.tensor_reduce(rsb(), pv(), axis=AX.X, op=ALU.max))
        self.wv(nc.vector)
        sb1 = self.vi(v.tensor_tensor(pv(), pv(), rsv, op=ALU.subtract))
        self.wv(nc.scalar, sb1)
        ex = self.ai(s.activation(P(), P(), ACT.Exp))
        self.wa(nc.vector, ex)
        yield
        for it in range(self.SINK_IT):
            self.wv(nc.vector)
            self.vi(v.tensor_reduce(rsb(), pv(), axis=AX.X, op=ALU.add))
            self.wv(nc.vector)
            self.vi(v.reciprocal(rs(), rs()))
            self.wv(nc.vector)
            self.vi(v.tensor_tensor(pv(), pv(), rsv, op=ALU.mult))
            self.wv(nc.vector)
            self.vi(v.tensor_reduce(rsb(), pvt(), axis=AX.X, op=ALU.add))
            self.wv(nc.vector)
            self.vi(v.reciprocal(rs(), rs()))
            self.wv(nc.vector)
            self.vi(v.tensor_tensor(pv(), pv(), csv, op=ALU.mult))
            if it % 8 == 7:
                yield
        for ct in range(TC):
            self.wv(nc.vector)
            self.vi(v.scalar_tensor_tensor(self.s_("P", ct * NN, NN), self.s_("P", ct * NN, NN), self.gcol[:],
                                           self.hbl[:], op0=ALU.mult, op1=ALU.add))
        if "dbg_p" in self.dbg:
            self.wv(nc.gpsimd)
            self.dso(self.nc.gpsimd.dma_start(out=self.dbg["dbg_p"].ap(), in_=self.s_("P", 0, TC * NN)))
        r1 = lambda: self.s_("rs", 0, TC)
        r1b = lambda: r1().rearrange("p (c o) -> p c o", o=1)
        _q = r1b()
        r1v = bass.AP(_q.tensor, _q.offset, [list(_q.ap[0]), list(_q.ap[1]), [0, NSTR]])
        yield
        for nm, off in (("hp", NN), ("ho", NN + NSTR)):
            for ct in range(TC):
                self.wv(nc.vector)
                self.vi(v.tensor_copy(self.s_(nm, ct * NSTR, NSTR), self.s_("logT", ct * NL + off, NSTR)))
            dv = self.s_(nm, 0, TC * NSTR).rearrange("p (c n) -> p c n", n=NSTR)
            self.wv(nc.vector)
            self.vi(v.tensor_reduce(r1b(), dv, axis=AX.X, op=ALU.max))
            self.wv(nc.vector)
            self.vi(v.tensor_tensor(dv, dv, r1v, op=ALU.subtract))
            self.wv(nc.scalar)
            e2 = self.ai(s.activation(self.s_(nm, 0, TC * NSTR), self.s_(nm, 0, TC * NSTR), ACT.Exp))
            self.wa(nc.vector, e2)
            self.vi(v.tensor_reduce(r1b(), dv, axis=AX.X, op=ALU.add))
            self.wv(nc.vector)
            self.vi(v.reciprocal(r1(), r1()))
            self.wv(nc.vector)
            self.vi(v.tensor_tensor(dv, dv, r1v, op=ALU.mult))

    def ffn_mix(self):
        nc = self.nc
        v, s, y, g_, t = nc.vector, nc.scalar, nc.sync, nc.gpsimd, nc.tensor
        T, TC, D, NSTR, CDIM, CC = self.T, self.TC, self.D, self.NSTR, self.CDIM, self.CC
        NN = NSTR * NSTR
        for ct in range(TC):
            self.wv(nc.sync); self.wp(nc.sync)
            ld = self.dsi(y.dma_start(out=self.a_("fx", 0, D), in_=self.i_x[3 + ct * 128:3 + (ct + 1) * 128, :]))
            self.wsi(nc.vector, ld)
            xt = self.a_("fx", 0, D)
            xn = self.a_("fxn", 0, D)
            m = self.a_("statc", ct * 4 + 0, 1)
            r = self.a_("statc", ct * 4 + 1, 1)
            xv = self.vi(v.tensor_scalar(xn, xt, m, r, op0=ALU.subtract, op1=ALU.mult))
            xp = self.a_("xp", 0, CDIM)
            self.wv(nc.vector, xv)
            self.vi(v.tensor_scalar(xp, xn[:, 0:CDIM], self.s_("hp", ct * NSTR, 1), None, op0=ALU.mult))
            for n in range(1, NSTR):
                self.wv(nc.vector)
                self.vi(v.scalar_tensor_tensor(xp, xn[:, n * CDIM:(n + 1) * CDIM],
                                               self.s_("hp", ct * NSTR + n, 1), xp, op0=ALU.mult, op1=ALU.add))
            self.wv(nc.tensor)
            for cc_ in range(CC):
                w = min(128, CDIM - cc_ * 128)
                pc = self.pi(t.transpose(self.ps[1][0:w, 0:128], xp[:, cc_ * 128:cc_ * 128 + w], self.ident[:]))
                self.wp(nc.vector, pc)
                ev = self.vi(v.tensor_copy(self.a_("xpt", cc_ * 128, 128, pn=w), self.ps[1][0:w, 0:128]))
                self.wv(nc.tensor, ev)
            nby = (CDIM + 511) // 512
            if not hasattr(self, "f_wlast"):
                self.f_wlast = [0, 0]
            for nb in range(nby):
                nn = min(512, CDIM - nb * 512)
                mm = None
                for cc_ in range(CC):
                    w = min(128, CDIM - cc_ * 128)
                    par = cc_ % 2
                    wreg = self.a_("wbr", 0, nn, pn=w) if par == 0 else self.a_("stageB", 0, nn, pn=w)
                    self.wv(nc.sync); self.wp(nc.sync, self.f_wlast[par])
                    wl = self.dsi(y.dma_start(out=wreg,
                                              in_=self.i_wbrb[(nb * CC + cc_) * 128:(nb * CC + cc_) * 128 + w,
                                                              0:nn]))
                    self.wsi(nc.tensor, wl)
                    if cc_ == 0:
                        self.wv(nc.tensor)
                    mm = self.pi(t.matmul(self.ps[2][0:128, 0:nn], self.a_("xpt", cc_ * 128, 128, pn=w),
                                  wreg, start=(cc_ == 0), stop=(cc_ == CC - 1)))
                    self.f_wlast[par] = mm
                pc = mm
                self.wp(nc.vector, pc)
                self.vi(v.tensor_copy(self.a_("ybr", nb * 512, nn), self.ps[2][0:128, 0:nn]))
            ybr = self.a_("ybr", 0, CDIM)
            for i in range(NSTR):
                acc = self.a_("xp", 0, CDIM)
                self.wv(nc.vector)
                self.vi(v.scalar_tensor_tensor(acc, xn[:, 0:CDIM], self.s_("P", ct * NN + i * NSTR, 1),
                                               xt[:, i * CDIM:(i + 1) * CDIM], op0=ALU.mult, op1=ALU.add))
                for j in range(1, NSTR):
                    self.wv(nc.vector)
                    self.vi(v.scalar_tensor_tensor(acc, xn[:, j * CDIM:(j + 1) * CDIM],
                                                   self.s_("P", ct * NN + i * NSTR + j, 1), acc,
                                                   op0=ALU.mult, op1=ALU.add))
                self.wv(nc.vector)
                fv = self.vi(v.scalar_tensor_tensor(acc, ybr, self.s_("ho", ct * NSTR + i, 1), acc,
                                                    op0=ALU.mult, op1=ALU.add))
                self.wv(nc.gpsimd, fv)
                self.dso(g_.dma_start(out=self.d_ffnx[ct * 128:(ct + 1) * 128, i * CDIM:(i + 1) * CDIM], in_=acc))
                self.wso(nc.vector)

    def phase3(self):
        nc = self.nc
        v, s, y, g_, t = nc.vector, nc.scalar, nc.sync, nc.gpsimd, nc.tensor
        NH, NCH, XC, HD, HP = self.NH, self.NCH, self.XC, self.HD, self.HP
        for e in (nc.tensor, nc.vector, nc.scalar, nc.sync, nc.gpsimd):
            e.wait_ge(self.s_cc, self.c_cc)
        self.wv(nc.sync); self.wp(nc.sync); self.wa(nc.sync)
        self.wso(nc.sync); self.wso(nc.scalar)
        NC_ = self.NCORES
        lda = self.smalls[0:NC_, 0:NH]
        src = bass.AP(self.d_ccout.ap().tensor, 128 * self.DSSM, [[129 * self.DSSM, NC_], [1, NH]])
        ld = self.dsi(y.dma_start(out=lda, in_=src))
        self.wsi(nc.tensor, ld)
        pc = self.pi(t.matmul(self.ps[0][0:NC_, 0:NH], self.rank_sb[0:NC_, 0:NC_], lda, start=True, stop=True))
        self.wp(nc.scalar, pc)
        fx = self.ai(s.activation(self.smalls[0:NC_, 64:64 + NH], self.ps[0][0:NC_, 0:NH], ACT.Exp))
        self.wa(nc.vector, fx)
        fz = self.vi(v.tensor_scalar(self.smalls[0:NC_, 64:64 + NH], self.smalls[0:NC_, 64:64 + NH],
                                     self.zmask_sb[0:NC_, :], None, op0=ALU.mult))
        hg = self.a_("hg", 0, self.DSSM)
        self.vi(v.memset(hg, 0.0))
        self.wv(nc.tensor, fz)
        pft = self.pi(t.transpose(self.ps[1][0:NH, 0:NC_], self.smalls[0:NC_, 64:64 + NH],
                                  self.ident[0:NC_, 0:NC_]))
        self.wp(nc.vector, pft)
        ftv = self.vi(v.tensor_copy(self.smalls[0:NH, 20:20 + NC_], self.ps[1][0:NH, 0:NC_]))
        for j in range(self.NCORES):
            self.wv(nc.vector)
            dgv = self.vi(v.tensor_scalar(self.smalls[0:NH, 64:64 + NH], self.ident[0:NH, 0:NH],
                                          self.smalls[0:NH, 20 + j:21 + j], None, op0=ALU.mult))
            self.wv(nc.tensor)
            pc = self.pi(t.matmul(self.ps[0][0:128, 0:NH], self.ones[0:NH, :],
                                  self.smalls[0:NH, 64:64 + NH], start=True, stop=True))
            self.wv(nc.scalar); self.wp(nc.scalar)
            ldS = self.dsd(nc.scalar.dma_start(out=self.a_("sread", 0, self.DSSM),
                                               in_=self.d_ccout[j * 129:j * 129 + 128, :]))
            self.wsd(nc.vector, ldS)
            self.wp(nc.vector, pc)
            frv = self.bcastv(bass.AP(self.ps[0].ap().tensor, self.ps[0].ap().offset,
                                      [[512, 128], [1, NH]]), NH, HD)
            self.wv(nc.vector)
            self.vi(v.tensor_tensor(self.a_("sread", 0, self.DSSM).rearrange("p (h d) -> p h d", h=NH),
                                    self.a_("sread", 0, self.DSSM).rearrange("p (h d) -> p h d", h=NH),
                                    frv, op=ALU.mult))
            self.wv(nc.vector)
            self.vi(v.tensor_tensor(hg, hg, self.a_("sread", 0, self.DSSM), op=ALU.add))
        if "dbg_hg" in self.dbg:
            self.wv(nc.gpsimd)
            self.dso(g_.dma_start(out=self.dbg["dbg_hg"].ap(), in_=hg))
        for cc in range(NCH):
            next(self.sink_gen, None)
            if cc > 0:
                dv = self.a_("dARep", (cc - 1) * NH, NH)
                dvv = self.bcastv(dv, NH, HD)
                self.wv(nc.vector)
                self.vi(v.tensor_tensor(hg.rearrange("p (h d) -> p h d", h=NH),
                                        hg.rearrange("p (h d) -> p h d", h=NH), dvv, op=ALU.mult))
                self.wv(nc.scalar); self.wp(nc.scalar)
                ldS = self.dsd(nc.scalar.dma_start(out=self.a_("sread", 0, self.DSSM),
                                                   in_=self.d_s[(cc - 1) * 128:cc * 128, :]))
                self.wsd(nc.vector, ldS)
                self.wv(nc.vector)
                self.vi(v.tensor_tensor(hg, hg, self.a_("sread", 0, self.DSSM), op=ALU.add))
            self.wv(nc.sync); self.wp(nc.sync)
            ldx = self.dsi(y.dma_start(out=self.a_("sread", 0, self.DSSM),
                                       in_=self.d_xstok[cc * 128:(cc + 1) * 128, :]))
            xstok = self.a_("sread", 0, self.DSSM)
            pcg = self.pi(t.matmul(self.ps[1][0:128, 0:128], self.bfm[:, cc * 128:(cc + 1) * 128],
                                   self.cfm[:, cc * 128:(cc + 1) * 128], start=True, stop=True))
            self.wp(nc.vector, pcg)
            gv = self.vi(v.tensor_copy(self.a_("gst", 0, 128), self.ps[1][0:128, 0:128]))
            self.wv(nc.vector)
            nv = self.vi(v.tensor_scalar(self.a_("nlt", 0, NH), self.a_("LT", cc * NH, NH), -1.0, None,
                                         op0=ALU.mult))
            self.wsi(nc.vector, ldx)
            self.wv(nc.tensor, nv)
            if not hasattr(self, "p3_ae"):
                self.p3_ae = [0, 0]; self.p3_mv = [0, 0]; self.p3_mm = [0, 0]
                self.p3_evac = 0; self.p3_ytr = [0, 0]; self.p3_ydma = [0, 0]
            mvh = [0] * NH

            def emit_T1EM(h):
                par = h % 2
                pT = self.ps[0] if par == 0 else self.ps[3]
                _c = self.a_("LT", cc * NH + h, 1)
                ltb = bass.AP(_c.tensor, _c.offset, [list(_c.ap[0]), [0, 128]])
                self.wa(nc.tensor, self.p3_ae[par])
                t.matmul(pT[0:128, 0:128], ltb, self.ident[:], start=True, stop=False)
                pc = self.pi(t.matmul(pT[0:128, 0:128], self.ident[:], self.trineg[:],
                                      start=False, stop=True))
                est = self.a_("est", par * 128, 128)
                mst = self.a_("mst", par * 128, 128)
                self.wp(nc.scalar, pc)
                self.wv(nc.scalar, max(self.p3_mv[par], nv))
                ae = self.ai(s.activation(est, pT[0:128, 0:128], ACT.Exp,
                                          bias=self.a_("nlt", h, 1)))
                self.p3_ae[par] = ae
                self.wa(nc.vector, ae)
                self.wp(nc.vector, self.p3_mm[par])
                mv = self.vi(v.scalar_tensor_tensor(mst, est, self.a_("dtT", cc * NH + h, 1),
                                                    self.a_("gst", 0, 128), op0=ALU.mult, op1=ALU.mult))
                self.p3_mv[par] = mv
                mvh[h] = mv

            emit_T1EM(0)
            for h in range(NH):
                if h + 1 < NH:
                    emit_T1EM(h + 1)
                hh = h % 8
                if hh == 0:
                    self.wv(nc.tensor, self.p3_evac)
                self.wv(nc.tensor, mvh[h])
                mst = self.a_("mst", (h % 2) * 128, 128)
                t.matmul(self.ps[4][0:128, hh * 64:hh * 64 + 64], mst, xstok[:, h * 64:h * 64 + 64],
                         start=True, stop=True)
                self.p3_mm[h % 2] = self.pi(t.matmul(self.ps[5][0:128, hh * 64:hh * 64 + 64],
                                 self.cfm[:, cc * 128:(cc + 1) * 128], hg[:, h * 64:h * 64 + 64],
                                 start=True, stop=True))
                if hh == 7:
                    hgp = h // 8
                    self.wp(nc.vector)
                    for hh2 in range(8):
                        h2 = hgp * 8 + hh2
                        self.vi(v.tensor_scalar(self.a_("yst3", h2 * 64, 64),
                                                self.ps[5][0:128, hh2 * 64:hh2 * 64 + 64],
                                                self.a_("acumT", cc * NH + h2, 1), None, op0=ALU.mult))
                        self.vi(v.tensor_tensor(self.a_("yst3", h2 * 64, 64), self.a_("yst3", h2 * 64, 64),
                                                self.ps[4][0:128, hh2 * 64:hh2 * 64 + 64], op=ALU.add))
                        self.vi(v.scalar_tensor_tensor(self.a_("yst3", h2 * 64, 64),
                                                       xstok[:, h2 * 64:h2 * 64 + 64],
                                                       self.drep[:, h2:h2 + 1], self.a_("yst3", h2 * 64, 64),
                                                       op0=ALU.mult, op1=ALU.add))
                    self.p3_evac = self.c_sv
            if "dbg_y" in self.dbg and cc == 0:
                self.wv(nc.gpsimd)
                self.dso(g_.dma_start(out=self.dbg["dbg_y"].ap(), in_=self.a_("yst3", 0, self.DSSM)))
            for kx in range(XC):
                self.wv(nc.tensor)
                pc = self.pi(t.transpose(self.ps[2][0:128, 0:128], self.a_("yst3", kx * 128, 128), self.ident[:]))
                self.wp(nc.vector, pc)
                yv = self.vi(v.tensor_copy(self.a_("ytr", 0, 128), self.ps[2][0:128, 0:128]))
                self.wv(nc.gpsimd, yv)
                self.dso(g_.dma_start(out=self.d_yfm[kx * 128:(kx + 1) * 128, cc * 128:(cc + 1) * 128],
                                      in_=self.a_("ytr", 0, 128)))
                self.wso(nc.vector)
        for _ in self.sink_gen:
            pass

    def phase4(self):
        nc = self.nc
        v, s, y, g_, t = nc.vector, nc.scalar, nc.sync, nc.gpsimd, nc.tensor
        XC, T, TW = self.XC, self.T, self.TW
        self.wso(nc.sync); self.wso(nc.scalar)
        for kx in range(XC):
            self.wv(nc.sync); self.wa(nc.sync)
            yl = self.dsi(y.dma_start(out=self.a_("y4", 0, T), in_=self.d_yfm[kx * 128:(kx + 1) * 128, :]))
            self.wv(nc.scalar); self.wp(nc.scalar)
            zl = self.dsd(nc.scalar.dma_start(out=self.a_("sz4", 0, T), in_=self.d_sz[kx * 128:(kx + 1) * 128, :]))
            self.wsi(nc.vector, yl)
            self.wsd(nc.vector, zl)
            gv = self.vi(v.tensor_tensor(self.a_("yg4", 0, T), self.a_("y4", 0, T), self.a_("sz4", 0, T),
                                         op=ALU.mult))
            self.wv(nc.scalar, gv)
            sqv = self.ai(s.activation(self.a_("sq4", 0, T), self.a_("yg4", 0, T), ACT.Square))
            self.wa(nc.tensor, sqv)
            for j, (to, tn) in enumerate(self.ttiles):
                self.pi(t.matmul(self.ps[2 + j][0:1, 0:tn], self.ones[0:128, 0:1],
                                 self.a_("sq4", to, tn), start=(kx == 0), stop=(kx == XC - 1)))
        self.wp(nc.vector)
        row = self.a_("row4", 0, T, pn=1)
        for j, (to, tn) in enumerate(self.ttiles):
            self.vi(v.tensor_copy(self.a_("row4", to, tn, pn=1), self.ps[2 + j][0:1, 0:tn]))
        self.wv(nc.vector)
        self.vi(v.tensor_scalar(row, row, 1.0 / self.DSSM, self.EPS_A, op0=ALU.mult, op1=ALU.add))
        self.wv(nc.scalar)
        sa = self.ai(s.activation(row, row, ACT.Sqrt))
        self.wa(nc.vector, sa)
        rv = self.vi(v.reciprocal(row, row))
        self.wv(nc.tensor, rv)
        for j, (to, tn) in enumerate(self.ttiles):
            pc = self.pi(t.matmul(self.ps[0][0:128, 0:tn], self.ones[0:1, :], self.a_("row4", to, tn, pn=1),
                                  start=True, stop=True))
            self.wp(nc.vector, pc)
            ev = self.vi(v.tensor_copy(self.a_("rrep", to, tn), self.ps[0][0:128, 0:tn]))
            self.wv(nc.tensor, ev)
        for kx in range(XC):
            self.wv(nc.sync)
            yl = self.dsi(y.dma_start(out=self.a_("y4", 0, T), in_=self.d_yfm[kx * 128:(kx + 1) * 128, :]))
            self.wv(nc.scalar); self.wp(nc.scalar)
            zl = self.dsd(nc.scalar.dma_start(out=self.a_("sz4", 0, T), in_=self.d_sz[kx * 128:(kx + 1) * 128, :]))
            self.wsi(nc.vector, yl)
            self.wsd(nc.vector, zl)
            gv = self.vi(v.tensor_tensor(self.a_("yg4", 0, T), self.a_("y4", 0, T), self.a_("sz4", 0, T),
                                         op=ALU.mult))
            self.wv(nc.vector, gv)
            nw = self.ccols[:, self.o_nrmw + kx:self.o_nrmw + kx + 1]
            self.vi(v.scalar_tensor_tensor(self.xa[:, kx * TW + 3:kx * TW + 3 + T], self.a_("yg4", 0, T),
                                           nw, self.a_("rrep", 0, T), op0=ALU.mult, op1=ALU.mult))
        if "dbg_yn" in self.dbg:
            self.wv(nc.gpsimd)
            self.dso(g_.dma_start(out=self.dbg["dbg_yn"].ap(), in_=self.xa[:, 0:XC * TW]))

    def phase5(self):
        nc = self.nc
        v, s, y, g_, t = nc.vector, nc.scalar, nc.sync, nc.gpsimd, nc.tensor
        XC, T, D, TC = self.XC, self.T, self.D, self.TC
        nwid = min(512, D)
        NB = (D + nwid - 1) // nwid
        assert TC <= 8
        self.wv(nc.sync); self.wa(nc.sync); self.wp(nc.sync)
        self.wso(nc.sync); self.wso(nc.scalar)
        last = [0, 0]
        for nb in range(NB):
            for k in range(XC):
                par = k % 2
                self.wp(nc.sync, last[par])
                r0 = (nb * XC + k) * 128
                self.dwi(par, y.dma_start(out=self.a_("w5", par * 512, nwid),
                                           in_=self.i_w2b[r0:r0 + 128, 0:nwid]))
                self.wwi(nc.tensor, par)
                if k == 0 and nb == 0:
                    self.wv(nc.tensor)
                mm = None
                for ct in range(TC):
                    if k == 0 and nb > 0:
                        self.wv(nc.tensor, self.p5_ov[ct])
                    mm = t.matmul(self.ps[ct][0:128, 0:nwid], self.xa[0:128, self.tw_off(k, 3 + ct * 128):
                                                                      self.tw_off(k, 3 + ct * 128) + 128],
                                  self.a_("w5", par * 512, nwid), start=(k == 0), stop=(k == XC - 1))
                last[par] = self.pi(mm)
            if not hasattr(self, "p5_ov"):
                self.p5_ov = [0] * TC; self.p5_od = [0, 0]
            for ct in range(TC):
                self.wv(nc.scalar); self.wp(nc.scalar)
                fl = self.dsd(nc.scalar.dma_start(out=self.a_("ffnx5", (ct % 2) * 512, nwid),
                                                  in_=self.d_ffnx[ct * 128:(ct + 1) * 128,
                                                                  nb * nwid:nb * nwid + nwid]))
                self.wsd(nc.vector, fl)
                self.wp(nc.vector, last[(XC - 1) % 2])
                self.wso(nc.vector, self.p5_od[ct % 2])
                ov = self.vi(v.tensor_tensor(self.a_("ost5", (ct % 2) * 512, nwid), self.ps[ct][0:128, 0:nwid],
                                             self.a_("ffnx5", (ct % 2) * 512, nwid), op=ALU.add))
                self.p5_ov[ct] = ov
                self.wv(nc.gpsimd, ov)
                self.p5_od[ct % 2] = self.dso(g_.dma_start(
                    out=self.o_out[ct * 128:(ct + 1) * 128, nb * nwid:nb * nwid + nwid],
                    in_=self.a_("ost5", (ct % 2) * 512, nwid)))


def kernel(**inputs):
    cfg = real_cfg()
    b = Build(cfg)
    maps = b.host_inputs(inputs)
    nc = b.build(dbg=False)
    res = run_bass_kernel_spmd(nc, maps, core_ids=list(range(cfg["NCORES"])))
    outs = [r["out"] for r in res.results]
    full = np.concatenate(outs, 0).reshape(1, cfg["NCORES"] * cfg["T"], cfg["D"])
    return np.ascontiguousarray(full.astype(np.float32))


# version 20 (final)
# speedup vs baseline: 1.3063x; 1.0206x over previous
import numpy as np
import concourse.bass as bass
import concourse.mybir as mybir
from concourse.bass_utils import run_bass_kernel_spmd

F32 = mybir.dt.float32
BF16 = mybir.dt.bfloat16
ALU = mybir.AluOpType
ACT = mybir.ActivationFunctionType
AX = mybir.AxisListType

NEG_BIG = -1.0e30


def real_cfg():
    return dict(
        NCORES=8, D=4096, DSSM=4096, S=128, NH=64, HD=64, K=4, T=1024, Q=128,
        NSTR=4, CDIM=1024, SINK_IT=50, TAU=0.1,
        EPS_LN=1e-5, EPS_A=1e-5, EPS_F=1.1920929e-07,
        MM="fp32",
    )


class Build:
    def __init__(self, cfg):
        self.cfg = cfg
        for k_, v_ in cfg.items():
            setattr(self, k_, v_)
        assert self.S == 128 and self.HD == 64 and self.K == 4 and self.Q == 128
        self.KC = self.D // 128
        self.XC = self.DSSM // 128
        self.NCH = self.T // self.Q
        self.TC = self.T // 128
        self.CONV = self.DSSM + 2 * self.S
        self.PROJ = self.DSSM + self.CONV + self.NH
        self.TW = self.T + 4
        self.NL = self.NSTR * self.NSTR + 2 * self.NSTR
        self.CC = (self.CDIM + 127) // 128
        self.HP = 128 // self.HD
        self.NC_B = 2 * self.XC
        self.NC_C = 2 * self.XC + 1
        self.NC_DT = 2 * self.XC + 2
        self.NCHK = 2 * self.XC + 3
        self.ttiles = []
        o = 0
        while o < self.T:
            n = min(512, self.T - o)
            self.ttiles.append((o, n))
            o += n
        self.NTT = len(self.ttiles)
        assert self.NTT <= 2
        A = {}
        A["p0_x"] = 0; A["p0_xa"] = 4096
        A["stage"] = 0
        A["xsil"] = 1056
        A["acc"] = 2080
        A["btok"] = 3104
        A["piece"] = 7424; A["xw"] = 7680; A["sstage"] = 7936
        A["stageB"] = 12288
        A["wlgall"] = 3264
        A["dt_ax"] = 4608; A["dt_mx"] = 5632
        A["wraw"] = 6656
        A["hloc"] = 8192
        A["z_st"] = 0; A["sz_st"] = 1088; A["lg_st"] = 2176; A["wlg"] = 3264
        A["logT"] = 8192; A["P"] = 8448; A["rs"] = 8640; A["hp"] = 8704; A["ho"] = 8768
        A["fx"] = 0; A["fxn"] = 4096
        A["xp"] = 8832; A["xpt"] = 9856; A["wbr"] = 10880; A["ybr"] = 11456
        A["hg"] = 0; A["sread"] = 4096; A["yst3"] = 8192
        A["gst"] = 12288; A["nlt"] = 12416; A["lrow"] = 12480
        A["est"] = 12736; A["mst"] = 12992; A["ytr"] = 13248
        A["ldiag"] = 12608
        A["y4"] = 0; A["sz4"] = 4096; A["yg4"] = 8192; A["sq4"] = 9216
        A["row4"] = 10304; A["rrep"] = 11392
        A["w5"] = 0; A["ffnx5"] = 2048; A["ost5"] = 3072
        A["LT"] = 13440; A["dtT"] = 13952; A["acumT"] = 14464
        A["wfacT"] = 14976; A["dARep"] = 15488
        A["statc"] = 16000; A["lendrep"] = 16128
        self.A = A

    def host_inputs(self, inputs):
        x = np.ascontiguousarray(np.asarray(inputs["L0_attn_hidden_states"], np.float32).reshape(-1, self.D))
        L = x.shape[0]
        assert L == self.NCORES * self.T, (L, self.NCORES, self.T)
        wcat = np.ascontiguousarray(
            np.concatenate([np.asarray(inputs["L0_ffn_W_res"], np.float32),
                            np.asarray(inputs["L0_ffn_W_pre"], np.float32),
                            np.asarray(inputs["L0_ffn_W_post"], np.float32)], 0).T)
        wcat = np.ascontiguousarray(
            wcat.reshape(self.KC, 128, self.NL).transpose(1, 0, 2).reshape(128, self.KC * self.NL))
        basec = np.ascontiguousarray(
            np.concatenate([np.asarray(inputs["L0_ffn_H_res_base"], np.float32).reshape(-1),
                            np.asarray(inputs["L0_ffn_H_pre_base"], np.float32),
                            np.asarray(inputs["L0_ffn_H_post_base"], np.float32)])[None, :])
        sidx = np.arange(128)[:, None]; tidx = np.arange(128)[None, :]
        KC = self.KC
        w1 = np.asarray(inputs["L0_attn_in_proj_w"], np.float32)
        w2 = np.asarray(inputs["L0_attn_out_proj_w"], np.float32)
        wbr0 = np.asarray(inputs["L0_ffn_W_branch"], np.float32)

        def blocktiles(groups, padw):
            out = np.zeros((len(groups) * KC * 128, padw), np.float32)
            for gi, g in enumerate(groups):
                oc = 0
                for ncid in g:
                    c0, w = self.nc_cols(ncid)
                    blk = w1[:, c0:c0 + w].reshape(KC, 128, w)
                    out.reshape(len(groups), KC, 128, padw)[gi, :, :, oc:oc + w] = blk
                    oc += w
            return np.ascontiguousarray(out)

        w1p1 = blocktiles(self.wslab_groups(1), 256)
        w1p2 = blocktiles(self.wslab_groups(2), 384)
        nwid = min(512, self.D)
        NB = (self.D + nwid - 1) // nwid
        w2b = np.ascontiguousarray(
            w2.reshape(self.XC, 128, NB, nwid).transpose(2, 0, 1, 3).reshape(NB * self.XC * 128, nwid))
        nby = (self.CDIM + 511) // 512
        wbrb = np.ascontiguousarray(
            wbr0.reshape(self.CC, 128, nby, 512).transpose(2, 0, 1, 3).reshape(nby * self.CC * 128, 512))
        common = dict(
            w1p1=w1p1, w1p2=w1p2, w2b=w2b, wbrb=wbrb,
            wcat=wcat, basec=basec,
            eyer=np.eye(self.NSTR, dtype=np.float32).reshape(1, -1),
            ident=np.eye(128, dtype=np.float32),
            allones=np.ones((128, 128), np.float32),
            trineg=np.where(sidx <= tidx, 0.0, NEG_BIG).astype(np.float32),
            tril1=(sidx <= tidx).astype(np.float32),
            convw=np.ascontiguousarray(np.asarray(inputs["L0_attn_conv_w"], np.float32)),
            alog=np.asarray(inputs["L0_attn_A_log"], np.float32).reshape(1, -1),
            dtb=np.asarray(inputs["L0_attn_dt_bias"], np.float32).reshape(-1, 1),
            dh=np.asarray(inputs["L0_attn_D"], np.float32).reshape(1, -1),
            nrmw=np.asarray(inputs["L0_attn_norm_w"], np.float32).reshape(-1),
            lnw=np.asarray(inputs["L0_attn_ln_w"], np.float32).reshape(-1),
            mup=np.asarray(inputs["L0_attn_mup_vector"], np.float32).reshape(-1),
            gate=np.asarray(inputs["L0_ffn_gate"], np.float32).reshape(1, 1),
        )
        maps = []
        for i in range(self.NCORES):
            lo = i * self.T
            halo = x[lo - 3:lo] if i > 0 else np.zeros((3, self.D), np.float32)
            rm = np.zeros((8, 8), np.float32)
            zm = np.zeros((8, 1), np.float32)
            for j in range(8):
                if j < i:
                    zm[j, 0] = 1.0
                    for k in range(j + 1, i):
                        rm[k, j] = 1.0
            m = dict(common)
            m.update(xsh=np.ascontiguousarray(np.concatenate([halo, x[lo:lo + self.T]], 0)),
                     rankm=rm, zmask=zm)
            maps.append(m)
        return maps

    def build(self, dbg=False):
        nc = bass.Bass("TRN2", target_bir_lowering=False, debug=False)
        self.nc = nc
        D, DSSM, NH, T, NL, NSTR, CDIM = self.D, self.DSSM, self.NH, self.T, self.NL, self.NSTR, self.CDIM
        din = lambda n, s: nc.dram_tensor(n, s, F32, kind="ExternalInput")
        self.i_x = din("xsh", [T + 3, D])
        self.NG1 = len(self.wslab_groups(1))
        self.NG2 = len(self.wslab_groups(2))
        self.NB5 = (D + min(512, D) - 1) // min(512, D)
        self.NBY = (CDIM + 511) // 512
        self.i_w1p1 = din("w1p1", [self.NG1 * self.KC * 128, 256])
        self.i_w1p2 = din("w1p2", [self.NG2 * self.KC * 128, 384])
        self.i_w2b = din("w2b", [self.NB5 * self.XC * 128, min(512, D)])
        self.i_wbrb = din("wbrb", [self.NBY * self.CC * 128, 512])
        self.i_wcat = din("wcat", [128, self.KC * NL])
        self.i_conv = din("convw", [self.CONV, self.K])
        self.i_alog = din("alog", [1, NH])
        self.i_dtb = din("dtb", [NH, 1])
        self.i_dh = din("dh", [1, NH])
        self.i_nrmw = din("nrmw", [DSSM])
        self.i_lnw = din("lnw", [D])
        self.i_mup = din("mup", [self.PROJ])
        self.i_gate = din("gate", [1, 1])
        self.i_basec = din("basec", [1, NL])
        self.i_eyer = din("eyer", [1, NSTR * NSTR])
        self.i_ident = din("ident", [128, 128])
        self.i_ones = din("allones", [128, 128])
        self.i_trineg = din("trineg", [128, 128])
        self.i_tril1 = din("tril1", [128, 128])
        self.i_rankm = din("rankm", [8, 8])
        self.i_zmask = din("zmask", [8, 1])
        self.o_out = nc.dram_tensor("out", [T, D], F32, kind="ExternalOutput")
        self.d_xstok = nc.dram_tensor("xstok_d", [T, DSSM], F32)
        self.d_s = nc.dram_tensor("s_d", [self.NCH * 128, DSSM], F32)
        self.d_sz = nc.dram_tensor("sz_d", [DSSM, T], F32)
        self.d_yfm = nc.dram_tensor("yfm_d", [DSSM, T], F32)
        self.d_ffnx = nc.dram_tensor("ffnx_d", [T, D], F32)
        self.d_ccin = nc.dram_tensor("ccin", [129, DSSM], F32)
        self.d_ccout = nc.dram_tensor("ccout", [self.NCORES * 129, DSSM], F32)
        self.dbg = {}
        if dbg:
            for nm, sh in [("dbg_xa", [128, self.KC * self.TW]), ("dbg_lt", [128, self.NCH * NH]),
                           ("dbg_wf", [128, self.NCH * NH]),
                           ("dbg_bfm", [128, T]), ("dbg_hloc", [128, DSSM]),
                           ("dbg_stB", [128, 1028]), ("dbg_accB", [128, T]),
                           ("dbg_logt", [128, self.TC * NL]), ("dbg_p", [128, self.TC * NSTR * NSTR]),
                           ("dbg_y", [128, DSSM]), ("dbg_yn", [128, self.XC * self.TW]),
                           ("dbg_hg", [128, DSSM])]:
                self.dbg[nm] = nc.dram_tensor(nm, sh, F32, kind="ExternalOutput")

        import contextlib
        st = contextlib.ExitStack()
        sb = lambda n, s, d=F32: st.enter_context(nc.sbuf_tensor(n, s, d))
        if self.MM == "fp32":
            self.xa = sb("xa", [128, self.KC * self.TW])
        else:
            raise NotImplementedError
        self.AR = sb("AR", [128, 16192])
        self.bfm = sb("bfm", [128, T])
        self.cfm = sb("cfm", [128, T])
        self.ident = sb("ident_sb", [128, 128])
        self.ones = sb("ones_sb", [128, 128])
        self.trineg = sb("trineg_sb", [128, 128])
        self.tril1 = sb("tril1_sb", [128, 128])
        ncc = self.KC + self.XC + self.NCHK + (self.CONV // 128) * self.K + 8
        self.ccols = sb("ccols", [128, ncc])
        self.o_lnw = 0
        self.o_nrmw = self.KC
        self.o_mup = self.KC + self.XC
        self.o_convc = self.KC + self.XC + self.NCHK
        self.smalls = sb("smalls", [128, 136])
        self.rank_sb = sb("rank_sb", [8, 8])
        self.zmask_sb = sb("zmask_sb", [8, 1])
        self.rows = sb("rows_sb", [1, 192])
        self.dtb_sb = sb("dtb_sb", [NH, 1])
        self.base_rep = sb("base_rep", [128, NL])
        self.hbl = sb("hbl", [128, NSTR * NSTR])
        self.gcol = sb("gcol", [128, 1])
        self.drep = sb("drep", [128, NH])
        self.negarep = sb("negarep", [128, NH])
        self.rlnw = sb("rlnw", [128, self.KC])
        self.sink_sb = sb("sink_sb", [128, 448])
        self.SK = {"logT": 0, "P": 192, "rs": 320, "hp": 352, "ho": 384}
        self.ps = [st.enter_context(nc.psum_tensor(f"ps{i}", [128, 512], F32)) for i in range(8)]
        sem = lambda n: st.enter_context(nc.semaphore(n))
        self.s_si = sem("si"); self.c_si = 0
        self.s_so = sem("so"); self.c_so = 0
        self.s_sd = sem("sd"); self.c_sd = 0
        self.s_sv = sem("sv"); self.c_sv = 0
        self.s_sa = sem("sa"); self.c_sa = 0
        self.s_sp = sem("sp"); self.c_sp = 0
        self.s_cc = sem("scc"); self.c_cc = 0
        self.s_w = [sem("sw0"), sem("sw1")]; self.c_w = [0, 0]
        blk = st.enter_context(nc.Block())

        @blk.gpsimd
        def _(g):
            self.emit_all()
        st.close()
        return nc

    def vi(self, instr):
        instr.then_inc(self.s_sv, 1); self.c_sv += 1; return self.c_sv

    def ai(self, instr):
        instr.then_inc(self.s_sa, 1); self.c_sa += 1; return self.c_sa

    def pi(self, instr):
        instr.then_inc(self.s_sp, 1); self.c_sp += 1; return self.c_sp

    def dsi(self, instr):
        instr.then_inc(self.s_si, 16); self.c_si += 16; return self.c_si

    def dso(self, instr):
        instr.then_inc(self.s_so, 16); self.c_so += 16; return self.c_so

    def dsd(self, instr):
        instr.then_inc(self.s_sd, 16); self.c_sd += 16; return self.c_sd

    def dwi(self, par, instr):
        instr.then_inc(self.s_w[par], 16); self.c_w[par] += 16; return self.c_w[par]

    def wwi(self, eng, par, n=None):
        n = self.c_w[par] if n is None else n
        if n > 0: eng.wait_ge(self.s_w[par], n)

    def wv(self, eng, n=None):
        n = self.c_sv if n is None else n
        if n > 0: eng.wait_ge(self.s_sv, n)

    def wa(self, eng, n=None):
        n = self.c_sa if n is None else n
        if n > 0: eng.wait_ge(self.s_sa, n)

    def wp(self, eng, n=None):
        n = self.c_sp if n is None else n
        if n > 0: eng.wait_ge(self.s_sp, n)

    def wsi(self, eng, n=None):
        n = self.c_si if n is None else n
        if n > 0: eng.wait_ge(self.s_si, n)

    def wso(self, eng, n=None):
        n = self.c_so if n is None else n
        if n > 0: eng.wait_ge(self.s_so, n)

    def wsd(self, eng, n=None):
        n = self.c_sd if n is None else n
        if n > 0: eng.wait_ge(self.s_sd, n)

    def s_(self, name, c0=0, cn=None, p0=0, pn=128):
        o = self.SK[name] + c0
        return self.sink_sb[p0:p0 + pn, o:o + cn]

    def a_(self, name, c0=0, cn=None, p0=0, pn=128):
        base = self.A[name]
        cn = 1 if cn is None else cn
        return self.AR[p0:p0 + pn, base + c0:base + c0 + cn]

    def bcastv(self, ap2d, ngrp, rep):
        return bass.AP(ap2d.tensor, ap2d.offset,
                       [list(ap2d.ap[0]), [ap2d.ap[1][0], ngrp], [0, rep]])

    def tw_off(self, k, col):
        return k * self.TW + col

    def emit_all(self):
        self.load_consts()
        self.phase0()
        self.phase1()
        self.collective()
        self.phase2()
        self.phase3()
        self.ffn_mix()
        self.phase4()
        self.phase5()
        nc = self.nc
        nc.sync.wait_ge(self.s_si, self.c_si)
        nc.scalar.wait_ge(self.s_sd, self.c_sd)
        nc.gpsimd.wait_ge(self.s_so, self.c_so)

    def load_consts(self):
        nc = self.nc
        v, s, y, t = nc.vector, nc.scalar, nc.sync, nc.tensor
        NH, KC, XC, NL, NSTR = self.NH, self.KC, self.XC, self.NL, self.NSTR
        self.dsi(y.dma_start(out=self.ident[:], in_=self.i_ident.ap()))
        self.dsi(y.dma_start(out=self.ones[:], in_=self.i_ones.ap()))
        self.dsi(y.dma_start(out=self.trineg[:], in_=self.i_trineg.ap()))
        self.dsi(y.dma_start(out=self.tril1[:], in_=self.i_tril1.ap()))
        self.dsi(y.dma_start(out=self.rank_sb[:], in_=self.i_rankm.ap()))
        self.dsi(y.dma_start(out=self.zmask_sb[:], in_=self.i_zmask.ap()))
        self.dsi(y.dma_start(out=self.dtb_sb[:], in_=self.i_dtb.ap()))
        self.o_rbase, self.o_reye, self.o_rgate, self.o_ralog, self.o_rdh = 0, NL, NL + 16, NL + 24, NL + 24 + NH
        self.dsi(y.dma_start(out=self.rows[0:1, 0:NL], in_=self.i_basec.ap()))
        self.dsi(y.dma_start(out=self.rows[0:1, self.o_reye:self.o_reye + NSTR * NSTR], in_=self.i_eyer.ap()))
        self.dsi(y.dma_start(out=self.rows[0:1, self.o_rgate:self.o_rgate + 1], in_=self.i_gate.ap()))
        self.dsi(y.dma_start(out=self.rows[0:1, self.o_ralog:self.o_ralog + NH], in_=self.i_alog.ap()))
        self.dsi(y.dma_start(out=self.rows[0:1, self.o_rdh:self.o_rdh + NH], in_=self.i_dh.ap()))
        self.dsi(y.dma_start(out=self.ccols[:, self.o_lnw:self.o_lnw + KC],
                             in_=self.i_lnw.ap().rearrange("(c p) -> p c", p=128), allow_slow_non_contiguous=True))
        self.dsi(y.dma_start(out=self.ccols[:, self.o_nrmw:self.o_nrmw + XC],
                             in_=self.i_nrmw.ap().rearrange("(c p) -> p c", p=128), allow_slow_non_contiguous=True))
        nfull = self.PROJ // 128
        self.dsi(y.dma_start(out=self.ccols[:, self.o_mup:self.o_mup + nfull],
                             in_=self.i_mup[0:nfull * 128].rearrange("(c p) -> p c", p=128), allow_slow_non_contiguous=True))
        if self.PROJ > nfull * 128:
            self.dsi(y.dma_start(out=self.ccols[0:NH, self.o_mup + nfull:self.o_mup + nfull + 1],
                                 in_=self.i_mup[nfull * 128:nfull * 128 + NH].rearrange("(p o) -> p o", o=1), allow_slow_non_contiguous=True))
        ncv = self.CONV // 128
        self.dsi(y.dma_start(out=self.ccols[:, self.o_convc:self.o_convc + ncv * self.K].rearrange(
                                 "p (c k) -> p c k", k=self.K),
                             in_=self.i_conv.ap().rearrange("(c p) k -> p c k", p=128),
                             allow_slow_non_contiguous=True))
        self.wsi(nc.vector); self.wsi(nc.scalar); self.wsi(nc.tensor)
        self.vi(v.reciprocal(self.rlnw[:, 0:KC], self.ccols[:, self.o_lnw:self.o_lnw + KC]))
        ps = self.ps[0]
        t.matmul(ps[:, 0:NL], self.ones[0:1, :], self.rows[0:1, 0:NL], start=True, stop=True)
        t.matmul(ps[:, 32:32 + 16], self.ones[0:1, :], self.rows[0:1, self.o_reye:self.o_reye + 16],
                 start=True, stop=True)
        t.matmul(ps[:, 64:65], self.ones[0:1, :], self.rows[0:1, self.o_rgate:self.o_rgate + 1],
                 start=True, stop=True)
        pc = self.pi(t.matmul(ps[:, 96:96 + NH], self.ones[0:1, :], self.rows[0:1, self.o_rdh:self.o_rdh + NH],
                              start=True, stop=True))
        self.wp(nc.vector, pc)
        self.vi(v.tensor_copy(self.base_rep[:], ps[:, 0:NL]))
        self.vi(v.tensor_copy(self.drep[:], ps[:, 96:96 + NH]))
        self.wp(nc.scalar, pc)
        ga = self.ai(s.activation(self.gcol[:], ps[:, 64:65], ACT.Sigmoid))
        self.wa(nc.vector, ga)
        gm = self.vi(v.tensor_scalar(self.smalls[:, 130:131], self.gcol[:], -1.0, 1.0, op0=ALU.mult, op1=ALU.add))
        self.vi(v.tensor_copy(self.hbl[:], ps[:, 32:48]))
        self.wv(nc.vector)
        self.vi(v.tensor_scalar(self.hbl[:], self.hbl[:], self.smalls[:, 130:131], None, op0=ALU.mult))
        ea = self.ai(s.activation(self.rows[0:1, self.o_ralog:self.o_ralog + NH],
                                  self.rows[0:1, self.o_ralog:self.o_ralog + NH], ACT.Exp))
        self.wa(nc.vector, ea)
        na = self.vi(v.tensor_scalar(self.rows[0:1, self.o_ralog:self.o_ralog + NH],
                                     self.rows[0:1, self.o_ralog:self.o_ralog + NH], -1.0, None, op0=ALU.mult))
        self.wv(nc.tensor, na)
        pc2 = self.pi(t.matmul(ps[:, 128:128 + NH], self.ones[0:1, :],
                               self.rows[0:1, self.o_ralog:self.o_ralog + NH], start=True, stop=True))
        self.wp(nc.vector, pc2)
        self.vi(v.tensor_copy(self.negarep[:], ps[:, 128:128 + NH]))

    def phase0(self):
        nc = self.nc
        v, s, y, t = nc.vector, nc.scalar, nc.sync, nc.tensor
        T, TC, KC, D = self.T, self.TC, self.KC, self.D
        EPSL, EPSA, EPSF = self.EPS_LN, self.EPS_A, self.EPS_F
        sm = self.smalls
        for ct in range(TC + 1):
            halo = (ct == TC)
            pn = 3 if halo else 128
            r0 = 0 if halo else 3 + ct * 128
            self.wv(nc.sync)
            self.wa(nc.sync)
            self.wp(nc.sync)
            ld = self.dsi(y.dma_start(out=self.a_("p0_x", 0, D, pn=pn), in_=self.i_x[r0:r0 + pn, :]))
            self.wsi(nc.vector, ld)
            self.wsi(nc.scalar, ld)
            xt = self.a_("p0_x", 0, D, pn=pn)
            c0 = ct * 4
            stc = lambda j: self.a_("statc", c0 + j, 1, pn=pn)
            self.vi(v.tensor_reduce(sm[0:pn, 0:1], xt, axis=AX.X, op=ALU.add))
            sq = self.a_("p0_xa", 0, D, pn=pn)
            sa1 = self.ai(s.activation(sq, xt, ACT.Square))
            self.wa(nc.vector, sa1)
            self.vi(v.tensor_reduce(sm[0:pn, 1:2], sq, axis=AX.X, op=ALU.add))
            self.wv(nc.vector)
            self.vi(v.tensor_scalar(stc(0), sm[0:pn, 0:1], 1.0 / D, None, op0=ALU.mult))
            self.vi(v.tensor_scalar(sm[0:pn, 1:2], sm[0:pn, 1:2], 1.0 / D, None, op0=ALU.mult))
            self.wv(nc.vector)
            self.vi(v.tensor_tensor(sm[0:pn, 2:3], stc(0), stc(0), op=ALU.mult))
            self.wv(nc.vector)
            self.vi(v.tensor_tensor(sm[0:pn, 3:4], sm[0:pn, 1:2], sm[0:pn, 2:3], op=ALU.subtract))
            self.wv(nc.vector)
            self.vi(v.tensor_scalar(sm[0:pn, 4:5], sm[0:pn, 3:4], EPSL, None, op0=ALU.add))
            self.wv(nc.scalar)
            sa2 = self.ai(s.activation(sm[0:pn, 5:6], sm[0:pn, 4:5], ACT.Sqrt))
            self.wa(nc.vector, sa2)
            self.vi(v.reciprocal(stc(1), sm[0:pn, 5:6]))
            self.wv(nc.vector)
            self.vi(v.tensor_tensor(sm[0:pn, 6:7], sm[0:pn, 3:4], stc(1), op=ALU.mult))
            self.wv(nc.vector)
            self.vi(v.tensor_tensor(sm[0:pn, 6:7], sm[0:pn, 6:7], stc(1), op=ALU.mult))
            self.wv(nc.vector)
            self.vi(v.tensor_scalar(sm[0:pn, 7:8], sm[0:pn, 6:7], EPSA, None, op0=ALU.add))
            self.vi(v.tensor_scalar(sm[0:pn, 8:9], sm[0:pn, 6:7], EPSF, None, op0=ALU.add))
            self.wv(nc.scalar)
            sa3 = self.ai(s.activation(sm[0:pn, 9:10], sm[0:pn, 7:8], ACT.Sqrt))
            self.wa(nc.vector, sa3)
            self.vi(v.reciprocal(sm[0:pn, 10:11], sm[0:pn, 9:10]))
            self.wv(nc.vector)
            self.vi(v.tensor_tensor(stc(2), stc(1), sm[0:pn, 10:11], op=ALU.mult))
            self.vi(v.reciprocal(sm[0:pn, 11:12], sm[0:pn, 8:9]))
            self.wv(nc.vector)
            self.vi(v.tensor_tensor(sm[0:pn, 12:13], sm[0:pn, 7:8], sm[0:pn, 11:12], op=ALU.mult))
            self.wv(nc.scalar)
            sa4 = self.ai(s.activation(stc(3), sm[0:pn, 12:13], ACT.Sqrt))
            self.wa(nc.vector, sa4)
            self.wv(nc.vector)
            self.vi(v.tensor_scalar(self.a_("p0_xa", 0, D, pn=pn), xt, stc(0), stc(2),
                                    op0=ALU.subtract, op1=ALU.mult))
            self.wv(nc.tensor)
            if not hasattr(self, "p0_ev"):
                self.p0_ev = [0, 0]
            for k in range(KC):
                par = k % 2
                pb = self.ps[6 + par]
                self.wv(nc.tensor, self.p0_ev[par])
                pc = self.pi(t.transpose(pb[0:128, 0:pn], self.a_("p0_xa", k * 128, 128, pn=pn),
                                         self.ident[0:pn, 0:pn]))
                self.wp(nc.vector, pc)
                dst = self.tw_off(k, 0 if halo else 3 + ct * 128)
                ev = self.vi(v.tensor_scalar(self.xa[:, dst:dst + pn], pb[0:128, 0:pn],
                                             self.ccols[:, self.o_lnw + k:self.o_lnw + k + 1], None,
                                             op0=ALU.mult))
                self.p0_ev[par] = ev

    def wslab_groups(self, phase):
        XC = self.XC
        if phase == 1:
            groups = [[self.NC_B, self.NC_C], [self.NC_DT, XC]]
            xs = list(range(XC + 1, 2 * XC))
            for i in range(0, len(xs), 2):
                groups.append(xs[i:i + 2])
        else:
            zs = list(range(XC))
            groups = []
            for i in range(0, len(zs), 3):
                groups.append(zs[i:i + 3])
        return groups

    def nc_cols(self, ncid):
        w = self.NH if ncid == self.NC_DT else 128
        return ncid * 128, w

    def emit_group(self, g, gsrc=None, with_logits=False, halo=True, bank3=True):
        nc = self.nc
        v, s, y, t = nc.vector, nc.scalar, nc.sync, nc.tensor
        KC = self.KC
        c0 = self.nc_cols(g[0])[0]
        wsum = sum(self.nc_cols(n)[1] for n in g)
        if not hasattr(self, "w_last_mm"):
            self.w_last_mm = [0, 0]
        last_mm = self.w_last_mm
        for k in range(KC):
            par = k % 2
            self.wp(nc.sync, last_mm[par])
            gten, gidx, padw = gsrc
            r0 = (gidx * KC + k) * 128
            self.dwi(par, y.dma_start(out=self.a_("wraw", par * 384, padw),
                                      in_=gten[r0:r0 + 128, 0:padw]))
            self.wwi(nc.tensor, par)
            if k == 0:
                self.wv(nc.tensor, self.bank_free)
                if with_logits:
                    self.wv(nc.tensor, self.wlg_ready)
            mm = None
            for i, ncid in enumerate(g):
                oc = sum(self.nc_cols(n)[1] for n in g[:i])
                w = self.nc_cols(ncid)[1]
                lhs = self.a_("wraw", par * 384 + oc, w)
                bb = 3 * i if bank3 else 2 * i
                for j, (to, tn) in enumerate(self.ttiles):
                    mm = t.matmul(self.ps[bb + j][0:w, 0:tn], lhs,
                                  self.xa[:, self.tw_off(k, 3 + to):self.tw_off(k, 3 + to) + tn],
                                  start=(k == 0), stop=(k == KC - 1))
                if halo and ncid != self.NC_DT:
                    mm = t.matmul(self.ps[3 * i + 2][0:w, 0:3], lhs,
                                  self.xa[:, self.tw_off(k, 0):self.tw_off(k, 0) + 3],
                                  start=(k == 0), stop=(k == KC - 1))
            if with_logits:
                lhs = self.a_("wlgall", k * self.NL, self.NL)
                for j, (to, tn) in enumerate(self.ttiles):
                    mm = t.matmul(self.ps[6 + j][0:self.NL, 0:tn], lhs,
                                  self.xa[:, self.tw_off(k, 3 + to):self.tw_off(k, 3 + to) + tn],
                                  start=(k == 0), stop=(k == KC - 1))
            last_mm[par] = self.pi(mm)
        return self.c_sp

    def phase1(self):
        nc = self.nc
        v, g_ = nc.vector, nc.gpsimd
        T = self.T
        self.wv(nc.sync); self.wa(nc.sync); self.wp(nc.sync)
        self.wv(nc.gpsimd); self.wa(nc.gpsimd); self.wp(nc.gpsimd)
        self.bank_free = self.c_sv
        z0 = self.vi(v.memset(self.a_("hloc", 0, self.DSSM), 0.0))
        self.wv(nc.vector, z0)
        for gi1, g in enumerate(self.wslab_groups(1)):
            done = self.emit_group(g, gsrc=(self.i_w1p1, gi1, 256), halo=True, bank3=True)
            self.wp(nc.vector, done)
            stgs = []
            for i, ncid in enumerate(g):
                stg = "stage" if i == 0 else "stageB"
                stgs.append(stg)
                w = self.nc_cols(ncid)[1]
                mupc = self.ccols[0:w, self.o_mup + ncid:self.o_mup + ncid + 1]
                if ncid != self.NC_DT:
                    self.vi(v.tensor_scalar(self.a_(stg, 0, 3, pn=w), self.ps[3 * i + 2][0:w, 0:3],
                                            mupc, None, op0=ALU.mult))
                for j, (to, tn) in enumerate(self.ttiles):
                    self.vi(v.tensor_scalar(self.a_(stg, 3 + to, tn, pn=w), self.ps[3 * i + j][0:w, 0:tn],
                                            mupc, None, op0=ALU.mult))
            self.bank_free = self.c_sv
            for i, ncid in enumerate(g):
                stg = stgs[i]
                self.wv(nc.vector)
                if ncid == self.NC_DT:
                    self.proc_dt(stg)
                elif ncid == self.NC_B:
                    if "dbg_stB" in self.dbg:
                        self.wv(nc.gpsimd)
                        self.dso(nc.gpsimd.dma_start(out=self.dbg["dbg_stB"].ap(), in_=self.a_(stg, 0, 1028)))
                    self.proc_conv(ncid, self.bfm[:], stg)
                    if "dbg_accB" in self.dbg:
                        self.wv(nc.gpsimd); self.wa(nc.gpsimd)
                        self.dso(nc.gpsimd.dma_start(out=self.dbg["dbg_accB"].ap(), in_=self.a_("acc", 0, self.T)))
                    self.trans_chunks(self.bfm[:], "btok")
                elif ncid == self.NC_C:
                    self.proc_conv(ncid, self.cfm[:], stg)
                else:
                    self.proc_xs(ncid - self.XC, stg)
        if "dbg_hloc" in self.dbg:
            self.wv(nc.gpsimd)
            self.dso(g_.dma_start(out=self.dbg["dbg_hloc"].ap(), in_=self.a_("hloc", 0, self.DSSM)))
        if "dbg_lt" in self.dbg:
            self.wv(nc.gpsimd)
            self.dso(g_.dma_start(out=self.dbg["dbg_lt"].ap(), in_=self.a_("LT", 0, self.NCH * self.NH)))
        if "dbg_wf" in self.dbg:
            self.wv(nc.gpsimd)
            self.dso(g_.dma_start(out=self.dbg["dbg_wf"].ap(), in_=self.a_("wfacT", 0, self.NCH * self.NH)))
        if "dbg_bfm" in self.dbg:
            self.wv(nc.gpsimd)
            self.dso(g_.dma_start(out=self.dbg["dbg_bfm"].ap(), in_=self.bfm[:]))
        if "dbg_xa" in self.dbg:
            self.wv(nc.gpsimd)
            self.dso(g_.dma_start(out=self.dbg["dbg_xa"].ap(), in_=self.xa[:]))

    def trans_chunks(self, src, dstname):
        nc = self.nc
        v, t = nc.vector, nc.tensor
        for cc in range(self.NCH):
            self.wv(nc.tensor)
            pc = self.pi(t.transpose(self.ps[7][0:128, 0:128], src[:, cc * 128:(cc + 1) * 128], self.ident[:]))
            self.wp(nc.vector, pc)
            self.vi(v.tensor_copy(self.a_(dstname, cc * 128, 128), self.ps[7][0:128, 0:128]))

    def proc_conv(self, ncid, dst, stg):
        nc = self.nc
        v, s = nc.vector, nc.scalar
        T = self.T
        cchunk = ncid - self.XC
        base = self.o_convc + cchunk * self.K
        acc = self.a_("acc", 0, T)
        self.vi(v.tensor_scalar(acc, self.a_(stg, 0, T), self.ccols[:, base:base + 1], None, op0=ALU.mult))
        for k in range(1, self.K):
            self.wv(nc.vector)
            self.vi(v.scalar_tensor_tensor(acc, self.a_(stg, k, T), self.ccols[:, base + k:base + k + 1],
                                           acc, op0=ALU.mult, op1=ALU.add))
        self.wv(nc.scalar)
        self.ai(s.activation(dst, acc, ACT.Silu))
        self.wa(nc.vector)
        self.wa(nc.tensor)

    def proc_dt(self, stg):
        nc = self.nc
        v, s, t = nc.vector, nc.scalar, nc.tensor
        NH, NCH, T = self.NH, self.NCH, self.T
        x = self.a_(stg, 3, T, pn=NH)
        self.vi(v.tensor_scalar(x, x, self.dtb_sb[0:NH, :], None, op0=ALU.add))
        ax = self.a_("dt_ax", 0, T, pn=NH)
        mx = self.a_("dt_mx", 0, T, pn=NH)
        self.wv(nc.scalar)
        a1 = self.ai(s.activation(ax, x, ACT.Abs))
        self.wa(nc.scalar, a1)
        a2 = self.ai(s.activation(ax, ax, ACT.Exp, scale=-1.0))
        self.wa(nc.scalar, a2)
        a3 = self.ai(s.activation(ax, ax, ACT.Ln, bias=1.0))
        self.wv(nc.vector)
        self.vi(v.tensor_scalar(mx, x, 0.0, None, op0=ALU.max))
        self.wa(nc.vector, a3)
        self.wv(nc.vector)
        spv = self.vi(v.tensor_tensor(x, ax, mx, op=ALU.add))
        self.wv(nc.tensor, spv)
        for cc in range(NCH):
            pc = self.pi(t.transpose(self.ps[7][0:128, 0:NH], x[0:NH, cc * 128:(cc + 1) * 128],
                                     self.ident[0:NH, 0:NH]))
            self.wp(nc.vector, pc)
            ev = self.vi(v.tensor_copy(self.a_("dtT", cc * NH, NH), self.ps[7][0:128, 0:NH]))
            self.wv(nc.vector)
            av = self.vi(v.tensor_tensor(self.a_("acumT", cc * NH, NH), self.a_("dtT", cc * NH, NH),
                                         self.negarep[:, 0:NH], op=ALU.mult))
            self.wv(nc.tensor, av)
            pc = self.pi(t.matmul(self.ps[7][0:128, 128:128 + NH], self.tril1[:],
                                  self.a_("acumT", cc * NH, NH), start=True, stop=True))
            self.wp(nc.vector, pc)
            lt = self.vi(v.tensor_copy(self.a_("LT", cc * NH, NH), self.ps[7][0:128, 128:128 + NH]))
            self.wv(nc.tensor, lt)
            pc = self.pi(t.transpose(self.ps[7][0:NH, 256:384], self.a_("LT", cc * NH, NH), self.ident[:]))
            self.wp(nc.vector, pc)
            lc = self.vi(v.tensor_copy(self.smalls[0:NH, 16:17], self.ps[7][0:NH, 256 + 127:256 + 128]))
            self.wv(nc.vector)
            if cc == 0:
                self.vi(v.tensor_copy(self.smalls[0:NH, 17:18], self.smalls[0:NH, 16:17]))
            else:
                self.vi(v.tensor_tensor(self.smalls[0:NH, 17:18], self.smalls[0:NH, 17:18],
                                        self.smalls[0:NH, 16:17], op=ALU.add))
            self.wa(nc.scalar, 0); self.wv(nc.scalar)
            ec = self.ai(s.activation(self.smalls[0:NH, 18:19], self.smalls[0:NH, 16:17], ACT.Exp))
            self.wa(nc.vector, ec)
            d1 = self.vi(v.tensor_scalar(self.smalls[0:NH, 64:64 + NH], self.ident[0:NH, 0:NH],
                                         self.smalls[0:NH, 18:19], None, op0=ALU.mult))
            self.wv(nc.tensor, d1)
            pc = self.pi(t.matmul(self.ps[7][0:128, 384:384 + NH], self.ones[0:NH, :],
                                  self.smalls[0:NH, 64:64 + NH], start=True, stop=True))
            self.wp(nc.vector, pc)
            self.vi(v.tensor_copy(self.a_("dARep", cc * NH, NH), self.ps[7][0:128, 384:384 + NH]))
            self.wv(nc.vector)
            d2 = self.vi(v.tensor_scalar(self.smalls[0:NH, 64:64 + NH], self.ident[0:NH, 0:NH],
                                         self.smalls[0:NH, 16:17], None, op0=ALU.mult))
            self.wv(nc.tensor, d2)
            pc = self.pi(t.matmul(self.ps[7][0:128, 384:384 + NH], self.ones[0:NH, :],
                                  self.smalls[0:NH, 64:64 + NH], start=True, stop=True))
            self.wp(nc.vector, pc)
            lr = self.vi(v.tensor_copy(self.a_("lendrep", 0, NH), self.ps[7][0:128, 384:384 + NH]))
            self.wv(nc.vector, lr)
            sv = self.vi(v.tensor_tensor(self.a_("wfacT", cc * NH, NH), self.a_("lendrep", 0, NH),
                                         self.a_("LT", cc * NH, NH), op=ALU.subtract))
            self.wv(nc.scalar, sv)
            ae = self.ai(s.activation(self.a_("wfacT", cc * NH, NH), self.a_("wfacT", cc * NH, NH), ACT.Exp))
            self.wa(nc.vector, ae)
            self.vi(v.tensor_tensor(self.a_("wfacT", cc * NH, NH), self.a_("wfacT", cc * NH, NH),
                                    self.a_("dtT", cc * NH, NH), op=ALU.mult))
            self.wv(nc.tensor)
        self.wv(nc.scalar)
        af = self.ai(s.activation(self.a_("acumT", 0, NCH * NH), self.a_("LT", 0, NCH * NH), ACT.Exp))
        self.wa(nc.vector, af)
        self.wa(nc.tensor, af)

    def proc_xs(self, kx, stg):
        nc = self.nc
        v, s, y, g_, t = nc.vector, nc.scalar, nc.sync, nc.gpsimd, nc.tensor
        NH, NCH, T, HP, HD = self.NH, self.NCH, self.T, self.HP, self.HD
        xsil = self.a_("xsil", 0, T)
        base = self.o_convc + kx * self.K
        acc = self.a_("acc", 0, T)
        self.vi(v.tensor_scalar(acc, self.a_(stg, 0, T), self.ccols[:, base:base + 1], None, op0=ALU.mult))
        for k in range(1, self.K):
            self.wv(nc.vector)
            self.vi(v.scalar_tensor_tensor(acc, self.a_(stg, k, T), self.ccols[:, base + k:base + k + 1],
                                           acc, op0=ALU.mult, op1=ALU.add))
        self.wv(nc.scalar)
        sil = self.ai(s.activation(xsil, acc, ACT.Silu))
        self.wa(nc.tensor, sil)
        self.wa(nc.vector, sil)
        if not hasattr(self, "xs_pv"):
            self.xs_pv = [0, 0]; self.xs_sv = [0, 0]; self.xs_dx = [0, 0]; self.xs_ds = [0, 0]
            self.xs_mm = [0, 0]
        for cc in range(NCH):
            par = cc % 2
            pb = self.ps[6 + par]
            piece = self.a_("piece", par * 128, 128)
            xw = self.a_("xw", par * 128, 128)
            sst = self.a_("sstage", par * 128, 128)
            self.wv(nc.tensor, self.xs_pv[par])
            pc = self.pi(t.transpose(pb[0:128, 0:128], xsil[:, cc * 128:(cc + 1) * 128], self.ident[:]))
            self.wp(nc.vector, pc)
            self.wso(nc.vector, self.xs_dx[par])
            pv = self.vi(v.tensor_copy(piece, pb[0:128, 0:128]))
            self.wv(nc.gpsimd, pv)
            self.xs_dx[par] = self.dso(g_.dma_start(
                out=self.d_xstok[cc * 128:(cc + 1) * 128, kx * 128:(kx + 1) * 128], in_=piece))
            wf = self.a_("wfacT", cc * NH + kx * HP, HP)
            wfv = self.bcastv(wf, HP, HD)
            self.wp(nc.vector, self.xs_mm[par])
            xv = self.vi(v.tensor_tensor(xw.rearrange("p (h d) -> p h d", h=HP),
                                         piece.rearrange("p (h d) -> p h d", h=HP), wfv, op=ALU.mult))
            self.xs_pv[par] = xv
            self.wv(nc.tensor, self.xs_sv[par])
            self.wv(nc.tensor, xv)
            pc2 = self.pi(t.matmul(pb[0:128, 128:256], self.a_("btok", cc * 128, 128), xw,
                                   start=True, stop=True))
            self.xs_mm[par] = pc2
            self.wp(nc.vector, pc2)
            for h in range(HP):
                col = kx * 128 + h * HD
                self.vi(v.scalar_tensor_tensor(self.a_("hloc", col, HD), self.a_("hloc", col, HD),
                                               self.a_("dARep", cc * NH + kx * HP + h, 1),
                                               pb[0:128, 128 + h * HD:128 + (h + 1) * HD],
                                               op0=ALU.mult, op1=ALU.add))
            self.wso(nc.vector, self.xs_ds[par])
            sv = self.vi(v.tensor_copy(sst, pb[0:128, 128:256]))
            self.xs_sv[par] = sv
            self.wv(nc.gpsimd, sv)
            self.xs_ds[par] = self.dso(g_.dma_start(
                out=self.d_s[cc * 128:(cc + 1) * 128, kx * 128:(kx + 1) * 128], in_=sst))

    def collective(self):
        nc = self.nc
        g_ = nc.gpsimd
        v = nc.vector
        self.wv(nc.vector)
        self.wso(nc.vector)
        zr = self.vi(v.memset(self.AR[0:1, 4096:4096 + self.DSSM], 0.0))
        self.wv(nc.gpsimd)
        self.dso(g_.dma_start(out=self.d_ccin[0:128, :], in_=self.a_("hloc", 0, self.DSSM)))
        self.dso(g_.dma_start(out=self.d_ccin[128:129, :], in_=self.AR[0:1, 4096:4096 + self.DSSM]))
        self.wso(nc.gpsimd)
        self.dso(g_.dma_start(out=self.d_ccin[128:129, 0:self.NH], in_=self.smalls[0:self.NH, 17:18]))
        self.wso(nc.gpsimd)
        g_.collective_compute(
            "AllGather", ALU.bypass,
            replica_groups=[list(range(self.NCORES))],
            ins=[self.d_ccin.ap()],
            outs=[self.d_ccout.ap()],
        ).then_inc(self.s_cc, 1)
        self.c_cc += 1

    def phase2(self):
        nc = self.nc
        v, s, y, g_, t = nc.vector, nc.scalar, nc.sync, nc.gpsimd, nc.tensor
        T, TC, NL = self.T, self.TC, self.NL
        self.wv(nc.sync); self.wa(nc.sync); self.wp(nc.sync)
        ld = self.dsi(y.dma_start(out=self.a_("wlgall", 0, self.KC * NL), in_=self.i_wcat.ap()))
        self.wsi(nc.vector, ld)
        for k in range(self.KC):
            self.vi(v.tensor_scalar(self.a_("wlgall", k * NL, NL), self.a_("wlgall", k * NL, NL),
                                    self.rlnw[:, k:k + 1], None, op0=ALU.mult))
        self.wlg_ready = self.c_sv
        groups = self.wslab_groups(2)
        for gi, g in enumerate(groups):
            done = self.emit_group(g, gsrc=(self.i_w1p2, gi, 384), with_logits=(gi == 0),
                                   halo=False, bank3=False)
            for i, ncid in enumerate(g):
                self.wp(nc.vector, done)
                self.wv(nc.vector)
                self.wa(nc.vector)
                mupc = self.ccols[:, self.o_mup + ncid:self.o_mup + ncid + 1]
                for j, (to, tn) in enumerate(self.ttiles):
                    self.vi(v.tensor_scalar(self.a_("z_st", to, tn), self.ps[2 * i + j][0:128, 0:tn],
                                            mupc, None, op0=ALU.mult))
                self.wv(nc.scalar)
                sz = self.ai(s.activation(self.a_("sz_st", 0, T), self.a_("z_st", 0, T), ACT.Silu))
                self.wa(nc.gpsimd, sz)
                self.dso(g_.dma_start(out=self.d_sz[ncid * 128:(ncid + 1) * 128, :], in_=self.a_("sz_st", 0, T)))
                self.wso(nc.scalar)
            if gi == 0:
                self.wp(nc.vector, done)
                for j, (to, tn) in enumerate(self.ttiles):
                    self.vi(v.tensor_copy(self.a_("lg_st", to, tn, pn=NL), self.ps[6 + j][0:NL, 0:tn]))
                self.logits_done = self.c_sv
            self.bank_free = self.c_sv
        self.wso(nc.vector)
        for ct in range(TC):
            self.wv(nc.tensor)
            pc = self.pi(t.transpose(self.ps[0][0:128, 0:NL], self.a_("lg_st", ct * 128, 128, pn=NL),
                                     self.ident[0:NL, 0:NL]))
            self.wp(nc.vector, pc)
            sf = self.a_("statc", ct * 4 + 3, 1)
            sv = self.vi(v.scalar_tensor_tensor(self.s_("logT", ct * NL, NL), self.ps[0][0:128, 0:NL], sf,
                                                self.base_rep[:], op0=ALU.mult, op1=ALU.add))
            self.wv(nc.tensor, sv)
        if "dbg_logt" in self.dbg:
            self.wv(nc.gpsimd)
            self.dso(g_.dma_start(out=self.dbg["dbg_logt"].ap(), in_=self.s_("logT", 0, TC * NL)))
        self.sink_gen = self.sinkhorn_steps()

    def sinkhorn_steps(self):
        nc = self.nc
        v, s = nc.vector, nc.scalar
        TC, NL, NSTR = self.TC, self.NL, self.NSTR
        NN = NSTR * NSTR
        P = lambda: self.s_("P", 0, TC * NN)
        pv = lambda: P().rearrange("p (c i j) -> p c i j", i=NSTR, j=NSTR)
        pvt = lambda: P().rearrange("p (c i j) -> p c j i", i=NSTR, j=NSTR)
        rs = lambda: self.s_("rs", 0, TC * NSTR)
        rsb = lambda: rs().rearrange("p (c n) -> p c n", n=NSTR)
        _r = rsb()
        rsv = bass.AP(_r.tensor, _r.offset, [list(_r.ap[0]), list(_r.ap[1]), list(_r.ap[2]), [0, NSTR]])
        csv = bass.AP(_r.tensor, _r.offset, [list(_r.ap[0]), list(_r.ap[1]), [0, NSTR], list(_r.ap[2])])
        tau_i = 1.0 / self.TAU
        for ct in range(TC):
            self.wv(nc.vector)
            self.vi(v.tensor_scalar(self.s_("P", ct * NN, NN), self.s_("logT", ct * NL, NN), tau_i, None,
                                    op0=ALU.mult))
        self.wv(nc.vector)
        self.vi(v.tensor_reduce(rsb(), pv(), axis=AX.X, op=ALU.max))
        self.wv(nc.vector)
        sb1 = self.vi(v.tensor_tensor(pv(), pv(), rsv, op=ALU.subtract))
        self.wv(nc.scalar, sb1)
        ex = self.ai(s.activation(P(), P(), ACT.Exp))
        self.wa(nc.vector, ex)
        yield
        for it in range(self.SINK_IT):
            self.wv(nc.vector)
            self.vi(v.tensor_reduce(rsb(), pv(), axis=AX.X, op=ALU.add))
            self.wv(nc.vector)
            self.vi(v.reciprocal(rs(), rs()))
            self.wv(nc.vector)
            self.vi(v.tensor_tensor(pv(), pv(), rsv, op=ALU.mult))
            self.wv(nc.vector)
            self.vi(v.tensor_reduce(rsb(), pvt(), axis=AX.X, op=ALU.add))
            self.wv(nc.vector)
            self.vi(v.reciprocal(rs(), rs()))
            self.wv(nc.vector)
            self.vi(v.tensor_tensor(pv(), pv(), csv, op=ALU.mult))
            if it % 8 == 7:
                yield
        for ct in range(TC):
            self.wv(nc.vector)
            self.vi(v.scalar_tensor_tensor(self.s_("P", ct * NN, NN), self.s_("P", ct * NN, NN), self.gcol[:],
                                           self.hbl[:], op0=ALU.mult, op1=ALU.add))
        if "dbg_p" in self.dbg:
            self.wv(nc.gpsimd)
            self.dso(self.nc.gpsimd.dma_start(out=self.dbg["dbg_p"].ap(), in_=self.s_("P", 0, TC * NN)))
        r1 = lambda: self.s_("rs", 0, TC)
        r1b = lambda: r1().rearrange("p (c o) -> p c o", o=1)
        _q = r1b()
        r1v = bass.AP(_q.tensor, _q.offset, [list(_q.ap[0]), list(_q.ap[1]), [0, NSTR]])
        yield
        for nm, off in (("hp", NN), ("ho", NN + NSTR)):
            for ct in range(TC):
                self.wv(nc.vector)
                self.vi(v.tensor_copy(self.s_(nm, ct * NSTR, NSTR), self.s_("logT", ct * NL + off, NSTR)))
            dv = self.s_(nm, 0, TC * NSTR).rearrange("p (c n) -> p c n", n=NSTR)
            self.wv(nc.vector)
            self.vi(v.tensor_reduce(r1b(), dv, axis=AX.X, op=ALU.max))
            self.wv(nc.vector)
            self.vi(v.tensor_tensor(dv, dv, r1v, op=ALU.subtract))
            self.wv(nc.scalar)
            e2 = self.ai(s.activation(self.s_(nm, 0, TC * NSTR), self.s_(nm, 0, TC * NSTR), ACT.Exp))
            self.wa(nc.vector, e2)
            self.vi(v.tensor_reduce(r1b(), dv, axis=AX.X, op=ALU.add))
            self.wv(nc.vector)
            self.vi(v.reciprocal(r1(), r1()))
            self.wv(nc.vector)
            self.vi(v.tensor_tensor(dv, dv, r1v, op=ALU.mult))

    def ffn_mix(self):
        nc = self.nc
        v, s, y, g_, t = nc.vector, nc.scalar, nc.sync, nc.gpsimd, nc.tensor
        T, TC, D, NSTR, CDIM, CC = self.T, self.TC, self.D, self.NSTR, self.CDIM, self.CC
        NN = NSTR * NSTR
        for ct in range(TC):
            self.wv(nc.sync); self.wp(nc.sync)
            ld = self.dsi(y.dma_start(out=self.a_("fx", 0, D), in_=self.i_x[3 + ct * 128:3 + (ct + 1) * 128, :]))
            self.wsi(nc.vector, ld)
            xt = self.a_("fx", 0, D)
            xn = self.a_("fxn", 0, D)
            m = self.a_("statc", ct * 4 + 0, 1)
            r = self.a_("statc", ct * 4 + 1, 1)
            xv = self.vi(v.tensor_scalar(xn, xt, m, r, op0=ALU.subtract, op1=ALU.mult))
            xp = self.a_("xp", 0, CDIM)
            self.wv(nc.vector, xv)
            self.vi(v.tensor_scalar(xp, xn[:, 0:CDIM], self.s_("hp", ct * NSTR, 1), None, op0=ALU.mult))
            for n in range(1, NSTR):
                self.wv(nc.vector)
                self.vi(v.scalar_tensor_tensor(xp, xn[:, n * CDIM:(n + 1) * CDIM],
                                               self.s_("hp", ct * NSTR + n, 1), xp, op0=ALU.mult, op1=ALU.add))
            self.wv(nc.tensor)
            for cc_ in range(CC):
                w = min(128, CDIM - cc_ * 128)
                pc = self.pi(t.transpose(self.ps[1][0:w, 0:128], xp[:, cc_ * 128:cc_ * 128 + w], self.ident[:]))
                self.wp(nc.vector, pc)
                ev = self.vi(v.tensor_copy(self.a_("xpt", cc_ * 128, 128, pn=w), self.ps[1][0:w, 0:128]))
                self.wv(nc.tensor, ev)
            nby = (CDIM + 511) // 512
            if not hasattr(self, "f_wlast"):
                self.f_wlast = [0, 0]
            for nb in range(nby):
                nn = min(512, CDIM - nb * 512)
                mm = None
                for cc_ in range(CC):
                    w = min(128, CDIM - cc_ * 128)
                    par = cc_ % 2
                    wreg = self.a_("wbr", 0, nn, pn=w) if par == 0 else self.a_("stageB", 0, nn, pn=w)
                    self.wv(nc.sync); self.wp(nc.sync, self.f_wlast[par])
                    wl = self.dsi(y.dma_start(out=wreg,
                                              in_=self.i_wbrb[(nb * CC + cc_) * 128:(nb * CC + cc_) * 128 + w,
                                                              0:nn]))
                    self.wsi(nc.tensor, wl)
                    if cc_ == 0:
                        self.wv(nc.tensor)
                    mm = self.pi(t.matmul(self.ps[2][0:128, 0:nn], self.a_("xpt", cc_ * 128, 128, pn=w),
                                  wreg, start=(cc_ == 0), stop=(cc_ == CC - 1)))
                    self.f_wlast[par] = mm
                pc = mm
                self.wp(nc.vector, pc)
                self.vi(v.tensor_copy(self.a_("ybr", nb * 512, nn), self.ps[2][0:128, 0:nn]))
            ybr = self.a_("ybr", 0, CDIM)
            for i in range(NSTR):
                acc = self.a_("xp", 0, CDIM)
                self.wv(nc.vector)
                self.vi(v.scalar_tensor_tensor(acc, xn[:, 0:CDIM], self.s_("P", ct * NN + i * NSTR, 1),
                                               xt[:, i * CDIM:(i + 1) * CDIM], op0=ALU.mult, op1=ALU.add))
                for j in range(1, NSTR):
                    self.wv(nc.vector)
                    self.vi(v.scalar_tensor_tensor(acc, xn[:, j * CDIM:(j + 1) * CDIM],
                                                   self.s_("P", ct * NN + i * NSTR + j, 1), acc,
                                                   op0=ALU.mult, op1=ALU.add))
                self.wv(nc.vector)
                fv = self.vi(v.scalar_tensor_tensor(acc, ybr, self.s_("ho", ct * NSTR + i, 1), acc,
                                                    op0=ALU.mult, op1=ALU.add))
                self.wv(nc.gpsimd, fv)
                self.dso(g_.dma_start(out=self.d_ffnx[ct * 128:(ct + 1) * 128, i * CDIM:(i + 1) * CDIM], in_=acc))
                self.wso(nc.vector)

    def phase3(self):
        nc = self.nc
        v, s, y, g_, t = nc.vector, nc.scalar, nc.sync, nc.gpsimd, nc.tensor
        NH, NCH, XC, HD, HP = self.NH, self.NCH, self.XC, self.HD, self.HP
        for e in (nc.tensor, nc.vector, nc.scalar, nc.sync, nc.gpsimd):
            e.wait_ge(self.s_cc, self.c_cc)
        self.wv(nc.sync); self.wp(nc.sync); self.wa(nc.sync)
        self.wso(nc.sync); self.wso(nc.scalar)
        NC_ = self.NCORES
        lda = self.smalls[0:NC_, 0:NH]
        src = bass.AP(self.d_ccout.ap().tensor, 128 * self.DSSM, [[129 * self.DSSM, NC_], [1, NH]])
        ld = self.dsi(y.dma_start(out=lda, in_=src))
        self.wsi(nc.tensor, ld)
        pc = self.pi(t.matmul(self.ps[0][0:NC_, 0:NH], self.rank_sb[0:NC_, 0:NC_], lda, start=True, stop=True))
        self.wp(nc.scalar, pc)
        fx = self.ai(s.activation(self.smalls[0:NC_, 64:64 + NH], self.ps[0][0:NC_, 0:NH], ACT.Exp))
        self.wa(nc.vector, fx)
        fz = self.vi(v.tensor_scalar(self.smalls[0:NC_, 64:64 + NH], self.smalls[0:NC_, 64:64 + NH],
                                     self.zmask_sb[0:NC_, :], None, op0=ALU.mult))
        hg = self.a_("hg", 0, self.DSSM)
        self.vi(v.memset(hg, 0.0))
        self.wv(nc.tensor, fz)
        pft = self.pi(t.transpose(self.ps[1][0:NH, 0:NC_], self.smalls[0:NC_, 64:64 + NH],
                                  self.ident[0:NC_, 0:NC_]))
        self.wp(nc.vector, pft)
        ftv = self.vi(v.tensor_copy(self.smalls[0:NH, 20:20 + NC_], self.ps[1][0:NH, 0:NC_]))
        for j in range(self.NCORES):
            self.wv(nc.vector)
            dgv = self.vi(v.tensor_scalar(self.smalls[0:NH, 64:64 + NH], self.ident[0:NH, 0:NH],
                                          self.smalls[0:NH, 20 + j:21 + j], None, op0=ALU.mult))
            self.wv(nc.tensor)
            pc = self.pi(t.matmul(self.ps[0][0:128, 0:NH], self.ones[0:NH, :],
                                  self.smalls[0:NH, 64:64 + NH], start=True, stop=True))
            self.wv(nc.scalar); self.wp(nc.scalar)
            ldS = self.dsd(nc.scalar.dma_start(out=self.a_("sread", 0, self.DSSM),
                                               in_=self.d_ccout[j * 129:j * 129 + 128, :]))
            self.wsd(nc.vector, ldS)
            self.wp(nc.vector, pc)
            frv = self.bcastv(bass.AP(self.ps[0].ap().tensor, self.ps[0].ap().offset,
                                      [[512, 128], [1, NH]]), NH, HD)
            self.wv(nc.vector)
            self.vi(v.tensor_tensor(self.a_("sread", 0, self.DSSM).rearrange("p (h d) -> p h d", h=NH),
                                    self.a_("sread", 0, self.DSSM).rearrange("p (h d) -> p h d", h=NH),
                                    frv, op=ALU.mult))
            self.wv(nc.vector)
            self.vi(v.tensor_tensor(hg, hg, self.a_("sread", 0, self.DSSM), op=ALU.add))
        if "dbg_hg" in self.dbg:
            self.wv(nc.gpsimd)
            self.dso(g_.dma_start(out=self.dbg["dbg_hg"].ap(), in_=hg))
        for cc in range(NCH):
            next(self.sink_gen, None)
            if cc > 0:
                dv = self.a_("dARep", (cc - 1) * NH, NH)
                dvv = self.bcastv(dv, NH, HD)
                self.wv(nc.vector)
                self.vi(v.tensor_tensor(hg.rearrange("p (h d) -> p h d", h=NH),
                                        hg.rearrange("p (h d) -> p h d", h=NH), dvv, op=ALU.mult))
                self.wv(nc.scalar); self.wp(nc.scalar)
                ldS = self.dsd(nc.scalar.dma_start(out=self.a_("sread", 0, self.DSSM),
                                                   in_=self.d_s[(cc - 1) * 128:cc * 128, :]))
                self.wsd(nc.vector, ldS)
                self.wv(nc.vector)
                self.vi(v.tensor_tensor(hg, hg, self.a_("sread", 0, self.DSSM), op=ALU.add))
            self.wv(nc.sync); self.wp(nc.sync)
            ldx = self.dsi(y.dma_start(out=self.a_("sread", 0, self.DSSM),
                                       in_=self.d_xstok[cc * 128:(cc + 1) * 128, :]))
            xstok = self.a_("sread", 0, self.DSSM)
            pcg = self.pi(t.matmul(self.ps[1][0:128, 0:128], self.bfm[:, cc * 128:(cc + 1) * 128],
                                   self.cfm[:, cc * 128:(cc + 1) * 128], start=True, stop=True))
            self.wp(nc.vector, pcg)
            gv = self.vi(v.tensor_copy(self.a_("gst", 0, 128), self.ps[1][0:128, 0:128]))
            self.wv(nc.vector)
            nv = self.vi(v.tensor_scalar(self.a_("nlt", 0, NH), self.a_("LT", cc * NH, NH), -1.0, None,
                                         op0=ALU.mult))
            self.wsi(nc.vector, ldx)
            self.wv(nc.tensor, nv)
            if not hasattr(self, "p3_ae"):
                self.p3_ae = [0, 0]; self.p3_mv = [0, 0]; self.p3_mm = [0, 0]
                self.p3_evac = 0; self.p3_ytr = [0, 0]; self.p3_ydma = [0, 0]
            mvh = [0] * NH

            def emit_T1EM(h):
                par = h % 2
                pT = self.ps[0] if par == 0 else self.ps[3]
                _c = self.a_("LT", cc * NH + h, 1)
                ltb = bass.AP(_c.tensor, _c.offset, [list(_c.ap[0]), [0, 128]])
                self.wa(nc.tensor, self.p3_ae[par])
                t.matmul(pT[0:128, 0:128], ltb, self.ident[:], start=True, stop=False)
                pc = self.pi(t.matmul(pT[0:128, 0:128], self.ident[:], self.trineg[:],
                                      start=False, stop=True))
                est = self.a_("est", par * 128, 128)
                mst = self.a_("mst", par * 128, 128)
                self.wp(nc.scalar, pc)
                self.wv(nc.scalar, max(self.p3_mv[par], nv))
                ae = self.ai(s.activation(est, pT[0:128, 0:128], ACT.Exp,
                                          bias=self.a_("nlt", h, 1)))
                self.p3_ae[par] = ae
                self.wa(nc.vector, ae)
                self.wp(nc.vector, self.p3_mm[par])
                mv = self.vi(v.scalar_tensor_tensor(mst, est, self.a_("dtT", cc * NH + h, 1),
                                                    self.a_("gst", 0, 128), op0=ALU.mult, op1=ALU.mult))
                self.p3_mv[par] = mv
                mvh[h] = mv

            emit_T1EM(0)
            for h in range(NH):
                if h + 1 < NH:
                    emit_T1EM(h + 1)
                hh = h % 8
                if hh == 0:
                    self.wv(nc.tensor, self.p3_evac)
                self.wv(nc.tensor, mvh[h])
                mst = self.a_("mst", (h % 2) * 128, 128)
                t.matmul(self.ps[4][0:128, hh * 64:hh * 64 + 64], mst, xstok[:, h * 64:h * 64 + 64],
                         start=True, stop=True)
                self.p3_mm[h % 2] = self.pi(t.matmul(self.ps[5][0:128, hh * 64:hh * 64 + 64],
                                 self.cfm[:, cc * 128:(cc + 1) * 128], hg[:, h * 64:h * 64 + 64],
                                 start=True, stop=True))
                if hh == 7:
                    hgp = h // 8
                    self.wp(nc.vector)
                    for hh2 in range(8):
                        h2 = hgp * 8 + hh2
                        self.vi(v.tensor_scalar(self.a_("yst3", h2 * 64, 64),
                                                self.ps[5][0:128, hh2 * 64:hh2 * 64 + 64],
                                                self.a_("acumT", cc * NH + h2, 1), None, op0=ALU.mult))
                        self.vi(v.tensor_tensor(self.a_("yst3", h2 * 64, 64), self.a_("yst3", h2 * 64, 64),
                                                self.ps[4][0:128, hh2 * 64:hh2 * 64 + 64], op=ALU.add))
                        self.vi(v.scalar_tensor_tensor(self.a_("yst3", h2 * 64, 64),
                                                       xstok[:, h2 * 64:h2 * 64 + 64],
                                                       self.drep[:, h2:h2 + 1], self.a_("yst3", h2 * 64, 64),
                                                       op0=ALU.mult, op1=ALU.add))
                    self.p3_evac = self.c_sv
            if "dbg_y" in self.dbg and cc == 0:
                self.wv(nc.gpsimd)
                self.dso(g_.dma_start(out=self.dbg["dbg_y"].ap(), in_=self.a_("yst3", 0, self.DSSM)))
            for kx in range(XC):
                self.wv(nc.tensor)
                pc = self.pi(t.transpose(self.ps[2][0:128, 0:128], self.a_("yst3", kx * 128, 128), self.ident[:]))
                self.wp(nc.vector, pc)
                yv = self.vi(v.tensor_copy(self.a_("ytr", 0, 128), self.ps[2][0:128, 0:128]))
                self.wv(nc.gpsimd, yv)
                self.dso(g_.dma_start(out=self.d_yfm[kx * 128:(kx + 1) * 128, cc * 128:(cc + 1) * 128],
                                      in_=self.a_("ytr", 0, 128)))
                self.wso(nc.vector)
        for _ in self.sink_gen:
            pass

    def phase4(self):
        nc = self.nc
        v, s, y, g_, t = nc.vector, nc.scalar, nc.sync, nc.gpsimd, nc.tensor
        XC, T, TW = self.XC, self.T, self.TW
        self.wso(nc.sync); self.wso(nc.scalar)
        p4_gv = [0, 0]
        p4_sq = 0

        def p4_load(kx, par):
            yl = self.dsi(y.dma_start(out=self.a_("y4", par * 1024, T),
                                      in_=self.d_yfm[kx * 128:(kx + 1) * 128, :]))
            zl = self.dsd(nc.scalar.dma_start(out=self.a_("sz4", par * 1024, T),
                                              in_=self.d_sz[kx * 128:(kx + 1) * 128, :]))
            return yl, zl

        self.wv(nc.sync); self.wa(nc.sync); self.wv(nc.scalar); self.wp(nc.scalar)
        pend = p4_load(0, 0)
        for kx in range(XC):
            par = kx % 2
            yl, zl = pend
            if kx + 1 < XC:
                self.wv(nc.sync, p4_gv[par ^ 1])
                self.wv(nc.scalar, p4_gv[par ^ 1])
                pend = p4_load(kx + 1, par ^ 1)
            self.wsi(nc.vector, yl)
            self.wsd(nc.vector, zl)
            self.wa(nc.vector, p4_sq)
            gv = self.vi(v.tensor_tensor(self.a_("yg4", 0, T), self.a_("y4", par * 1024, T),
                                         self.a_("sz4", par * 1024, T), op=ALU.mult))
            p4_gv[par] = gv
            self.wv(nc.scalar, gv)
            self.wp(nc.scalar)
            sqv = self.ai(s.activation(self.a_("sq4", 0, T), self.a_("yg4", 0, T), ACT.Square))
            p4_sq = sqv
            self.wa(nc.tensor, sqv)
            for j, (to, tn) in enumerate(self.ttiles):
                self.pi(t.matmul(self.ps[2 + j][0:1, 0:tn], self.ones[0:128, 0:1],
                                 self.a_("sq4", to, tn), start=(kx == 0), stop=(kx == XC - 1)))
        self.wp(nc.vector)
        row = self.a_("row4", 0, T, pn=1)
        for j, (to, tn) in enumerate(self.ttiles):
            self.vi(v.tensor_copy(self.a_("row4", to, tn, pn=1), self.ps[2 + j][0:1, 0:tn]))
        self.wv(nc.vector)
        self.vi(v.tensor_scalar(row, row, 1.0 / self.DSSM, self.EPS_A, op0=ALU.mult, op1=ALU.add))
        self.wv(nc.scalar)
        sa = self.ai(s.activation(row, row, ACT.Sqrt))
        self.wa(nc.vector, sa)
        rv = self.vi(v.reciprocal(row, row))
        self.wv(nc.tensor, rv)
        for j, (to, tn) in enumerate(self.ttiles):
            pc = self.pi(t.matmul(self.ps[0][0:128, 0:tn], self.ones[0:1, :], self.a_("row4", to, tn, pn=1),
                                  start=True, stop=True))
            self.wp(nc.vector, pc)
            ev = self.vi(v.tensor_copy(self.a_("rrep", to, tn), self.ps[0][0:128, 0:tn]))
            self.wv(nc.tensor, ev)
        p4_gv = [0, 0]
        self.wv(nc.sync); self.wv(nc.scalar); self.wp(nc.scalar)
        pend = p4_load(0, 0)
        for kx in range(XC):
            par = kx % 2
            yl, zl = pend
            if kx + 1 < XC:
                self.wv(nc.sync, p4_gv[par ^ 1])
                self.wv(nc.scalar, p4_gv[par ^ 1])
                pend = p4_load(kx + 1, par ^ 1)
            self.wsi(nc.vector, yl)
            self.wsd(nc.vector, zl)
            gv = self.vi(v.tensor_tensor(self.a_("yg4", 0, T), self.a_("y4", par * 1024, T),
                                         self.a_("sz4", par * 1024, T), op=ALU.mult))
            p4_gv[par] = gv
            nw = self.ccols[:, self.o_nrmw + kx:self.o_nrmw + kx + 1]
            self.vi(v.scalar_tensor_tensor(self.xa[:, kx * TW + 3:kx * TW + 3 + T], self.a_("yg4", 0, T),
                                           nw, self.a_("rrep", 0, T), op0=ALU.mult, op1=ALU.mult))
        if "dbg_yn" in self.dbg:
            self.wv(nc.gpsimd)
            self.dso(g_.dma_start(out=self.dbg["dbg_yn"].ap(), in_=self.xa[:, 0:XC * TW]))

    def phase5(self):
        nc = self.nc
        v, s, y, g_, t = nc.vector, nc.scalar, nc.sync, nc.gpsimd, nc.tensor
        XC, T, D, TC = self.XC, self.T, self.D, self.TC
        nwid = min(512, D)
        NB = (D + nwid - 1) // nwid
        assert TC <= 8
        self.wv(nc.sync); self.wa(nc.sync); self.wp(nc.sync)
        self.wso(nc.sync); self.wso(nc.scalar)
        last = [0, 0]
        for nb in range(NB):
            for k in range(XC):
                par = k % 2
                self.wp(nc.sync, last[par])
                r0 = (nb * XC + k) * 128
                self.dwi(par, y.dma_start(out=self.a_("w5", par * 512, nwid),
                                           in_=self.i_w2b[r0:r0 + 128, 0:nwid]))
                self.wwi(nc.tensor, par)
                if k == 0:
                    self.wv(nc.tensor)
                mm = None
                for ct in range(TC):
                    mm = t.matmul(self.ps[ct][0:128, 0:nwid], self.xa[0:128, self.tw_off(k, 3 + ct * 128):
                                                                      self.tw_off(k, 3 + ct * 128) + 128],
                                  self.a_("w5", par * 512, nwid), start=(k == 0), stop=(k == XC - 1))
                last[par] = self.pi(mm)
            if not hasattr(self, "p5_ov"):
                self.p5_ov = [0] * TC; self.p5_od = [0, 0]
            for ct in range(TC):
                self.wv(nc.scalar); self.wp(nc.scalar)
                fl = self.dsd(nc.scalar.dma_start(out=self.a_("ffnx5", (ct % 2) * 512, nwid),
                                                  in_=self.d_ffnx[ct * 128:(ct + 1) * 128,
                                                                  nb * nwid:nb * nwid + nwid]))
                self.wsd(nc.vector, fl)
                self.wp(nc.vector, last[(XC - 1) % 2])
                self.wso(nc.vector, self.p5_od[ct % 2])
                ov = self.vi(v.tensor_tensor(self.a_("ost5", (ct % 2) * 512, nwid), self.ps[ct][0:128, 0:nwid],
                                             self.a_("ffnx5", (ct % 2) * 512, nwid), op=ALU.add))
                self.p5_ov[ct] = ov
                self.wv(nc.gpsimd, ov)
                self.p5_od[ct % 2] = self.dso(g_.dma_start(
                    out=self.o_out[ct * 128:(ct + 1) * 128, nb * nwid:nb * nwid + nwid],
                    in_=self.a_("ost5", (ct % 2) * 512, nwid)))


def kernel(**inputs):
    cfg = real_cfg()
    b = Build(cfg)
    maps = b.host_inputs(inputs)
    nc = b.build(dbg=False)
    res = run_bass_kernel_spmd(nc, maps, core_ids=list(range(cfg["NCORES"])))
    outs = [r["out"] for r in res.results]
    full = np.concatenate(outs, 0).reshape(1, cfg["NCORES"] * cfg["T"], cfg["D"])
    return np.ascontiguousarray(full.astype(np.float32))
